# version 08
# baseline (speedup 1.0000x reference)
import numpy as np
import ml_dtypes
from contextlib import ExitStack
import concourse.bass as bass
import concourse.mybir as mybir
from concourse.bass_utils import run_bass_kernel_spmd

F32 = mybir.dt.float32
BF16 = mybir.dt.bfloat16
U8 = mybir.dt.uint8
I32 = mybir.dt.int32
U32 = mybir.dt.uint32
AF = mybir.ActivationFunctionType
ALU = mybir.AluOpType
AX = mybir.AxisListType.X
BF = ml_dtypes.bfloat16

B, S, H = 4, 2048, 2048
NCORES = 8
TT_ = 8192
RMS_EPS_L1 = float(np.finfo(np.float32).eps)


def _l0_costab():
    inv = (1.0 / 10000.0 ** (np.arange(0, H, 2, dtype=np.float32) / H)).astype(np.float32)
    fr = np.arange(S, dtype=np.float32)[:, None] * inv[None, :]
    return np.cos(fr).astype(np.float32), np.sin(fr).astype(np.float32)


def _l1_costab():
    inv = (1.0 / 10000.0 ** (np.arange(0, 128, 2, dtype=np.float32) / 128)).astype(np.float32)
    fr = np.arange(S, dtype=np.float32)[:, None] * inv[None, :]
    return np.cos(fr).astype(np.float32), np.sin(fr).astype(np.float32)


def _mla_costab():
    inv = (1.0 / 10000.0 ** (np.arange(0, 64, 2, dtype=np.float32) / 64)).astype(np.float32)
    fr = np.arange(S, dtype=np.float32)[:, None] * inv[None, :]
    return np.cos(fr).astype(np.float32), np.sin(fr).astype(np.float32)


IDENT = np.eye(128, dtype=np.float32).astype(BF)
CAUSAL_MASK = np.triu(np.full((128, 128), -30000.0, dtype=np.float32), 1)


def _l0_colmap(p):
    cols = []
    for lh in range(8):
        if lh < 4:
            base = 512 * p + lh * 128
        else:
            base = 1024 + 512 * p + (lh - 4) * 128
        cols.extend(range(base, base + 128))
    return np.array(cols, dtype=np.int64)


def _pi_n():
    p = []
    for a in range(64):
        p.extend(range(256 * a, 256 * a + 256, 2))
        p.extend(range(256 * a + 1, 256 * a + 256, 2))
    return np.array(p, dtype=np.int64)


def _tiles128(a):
    t = a.shape[0] // 128
    return np.ascontiguousarray(a.reshape(t, 128, -1).transpose(1, 0, 2).reshape(128, -1))



class Seq:
    ENGS = ("sync", "gpsimd", "vector", "scalar", "tensor")

    def __init__(self, es, nc):
        self.es = es
        self.nc = nc
        self.sems = {n: es.enter_context(nc.semaphore("q_" + n)) for n in self.ENGS}
        self.cnt = {n: 0 for n in self.ENGS}
        self.prog = {n: [] for n in self.ENGS}
        self.stage_deps = []
        self.ncc = 0

    def new_stage(self):
        self.stage_deps = [(e, self.cnt[e]) for e in self.ENGS if self.cnt[e] > 0]

    def on(self, eng, fn, after=(), ndma=0):
        deps = []
        for tok in list(after) + self.stage_deps:
            if tok is None:
                continue
            e2, c2 = tok
            if e2 != eng or ndma:
                deps.append((e2, c2))
        inc = 16 * ndma if ndma else 1
        self.cnt[eng] += inc
        target = self.cnt[eng]
        self.prog[eng].append((deps, fn, 0 if ndma else 1))
        return (eng, target)

    def cc(self, fn, after=()):
        name = f"cc{self.ncc}"
        self.ncc += 1
        sem = self.es.enter_context(self.nc.semaphore("q_" + name))
        self.sems[name] = sem
        deps = []
        for tok in list(after) + self.stage_deps:
            if tok is None:
                continue
            deps.append(tok)

        def step(e, fn=fn, sem=sem):
            ins = fn(e)
            ins.then_inc(sem)
        self.prog["gpsimd"].append((deps, step, 2))
        return (name, 1)

    def emit(self, block):
        for eng in self.ENGS:
            steps = self.prog[eng]
            if not steps:
                continue

            def body(e, steps=steps, eng=eng):
                sem = self.sems[eng]
                waited = {}
                for deps, fn, kind in steps:
                    for e2, c2 in deps:
                        if waited.get(e2, 0) < c2:
                            e.wait_ge(self.sems[e2], c2)
                            waited[e2] = c2
                    if kind == 1:
                        last = fn(e)
                        last.then_inc(sem, 1)
                    elif kind == 2:
                        fn(e)
                    else:
                        fn(e, sem)
            getattr(block, eng)(body)

    def finish(self, block, waits):
        def body(e):
            for e2, c2 in waits:
                e.wait_ge(self.sems[e2], c2)
        block.sync(body)


class Pool:

    def __init__(self, nc, es, nbytes):
        self.nb = nbytes
        self.t = es.enter_context(nc.sbuf_tensor("POOL", [128, nbytes], U8))
        self.off = 0
        self.peak = 0

    def reset(self):
        self.off = 0

    def take(self, name, shape, dt):
        assert shape[0] == 128
        cols = 1
        for d in shape[1:]:
            cols *= d
        nbytes = cols * mybir.dt.size(dt)
        off = (self.off + 63) & ~63
        self.off = off + nbytes
        self.peak = max(self.peak, self.off)
        assert self.off <= self.nb, f"pool overflow at {name}: {self.off} > {self.nb}"
        return self.t[:, off:off + nbytes].bitcast(dt)



def ln_zigzag(q, t_xo, xo, xc, scrap, cols, fscr, c0, N, eps, apply_fn, extra_after=()):
    inv_n = 1.0 / float(N)

    def s_mean(e):
        e.mul(cols[:, c0 + 1:c0 + 2], cols[:, c0:c0 + 1], inv_n)
        return e.copy(fscr[:, 0:1], cols[:, c0 + 1:c0 + 2])
    t1 = q.on("scalar", s_mean, after=[t_xo])

    def v_center(e):
        e.tensor_scalar(xc[:], xo[:], cols[:, c0 + 1:c0 + 2], None, ALU.subtract)
        return e.tensor_copy(fscr[:, 1:2], xc[:, xc.shape[1] - 1:xc.shape[1]])
    t2 = q.on("vector", v_center, after=[t1])

    def s_sq(e):
        e.activation(scrap[:], xc[:], AF.Square, accum_out=cols[:, c0 + 2:c0 + 3])
        return e.copy(fscr[:, 0:1], cols[:, c0 + 2:c0 + 3])
    t3 = q.on("scalar", s_sq, after=[t2])

    def v_w(e):
        e.tensor_scalar(cols[:, c0 + 3:c0 + 4], cols[:, c0 + 2:c0 + 3], inv_n, eps, ALU.mult, ALU.add)
        return e.tensor_copy(fscr[:, 1:2], cols[:, c0 + 3:c0 + 4])
    t4 = q.on("vector", v_w, after=[t3])

    def s_sqrt(e):
        e.activation(cols[:, c0 + 4:c0 + 5], cols[:, c0 + 3:c0 + 4], AF.Sqrt)
        return e.copy(fscr[:, 0:1], cols[:, c0 + 4:c0 + 5])
    t5 = q.on("scalar", s_sqrt, after=[t4])

    def v_rstd(e):
        e.reciprocal(cols[:, c0 + 5:c0 + 6], cols[:, c0 + 4:c0 + 5])
        return e.tensor_copy(fscr[:, 1:2], cols[:, c0 + 5:c0 + 6])
    t6 = q.on("vector", v_rstd, after=[t5])

    return q.on("scalar", apply_fn, after=[t6] + list(extra_after))



def emit_comb_out(q, comb, idt, psTr, ctr, fscr, agin_d, after):
    stores = []
    tEv = None
    tSt = None
    for ci in range(8):
        for h2 in range(2):
            def trp(e, ci=ci, h2=h2):
                for jj in range(8):
                    tj = h2 * 8 + jj
                    ins = e.transpose(psTr[:, jj * 128:(jj + 1) * 128],
                                      comb[:, tj * 1024 + ci * 128:tj * 1024 + (ci + 1) * 128],
                                      idt)
                return ins
            tt = q.on("tensor", trp, after=[after, tEv])
            eng = "scalar" if (2 * ci + h2) % 2 == 0 else "vector"

            def evp(e, h2=h2, ci=ci, eng=eng):
                cp = e.copy if eng == "scalar" else e.tensor_copy
                for jj in range(8):
                    cp(ctr[:, h2 * 1024 + jj * 128:h2 * 1024 + (jj + 1) * 128],
                       psTr[:, jj * 128:(jj + 1) * 128])
                return cp(fscr[:, 4:5], ctr[:, h2 * 1024 + 1023:h2 * 1024 + 1024])
            tEv = q.on(eng, evp, after=[tt, tSt, tEv])

        def st(e, sem, ci=ci):
            e.dma_start(out=agin_d[ci * 128:(ci + 1) * 128, :],
                        in_=ctr[:, 0:1024]).then_inc(sem, 16)
            e.dma_start(out=agin_d[1024 + ci * 128:1024 + (ci + 1) * 128, :],
                        in_=ctr[:, 1024:2048]).then_inc(sem, 16)
        tSt = q.on("sync", st, after=[tEv], ndma=2)
        stores.append(tSt)
    return stores


class XTrOut:

    def __init__(self, q, idt, psTr, xtT, fscr, agin_d):
        self.q, self.idt, self.psTr, self.xtT = q, idt, psTr, xtT
        self.fscr, self.agin_d = fscr, agin_d
        self.tEv = None
        self.done = [None] * 8

    def add_tile(self, jt, src_tile, after):
        for h2 in range(2):
            def trp(e, h2=h2, src_tile=src_tile):
                for jj in range(8):
                    hi = h2 * 8 + jj
                    ins = e.transpose(self.psTr[:, jj * 128:(jj + 1) * 128],
                                      src_tile[:, hi * 128:(hi + 1) * 128], self.idt)
                return ins
            tt = self.q.on("tensor", trp, after=[after, self.tEv])
            eng = "scalar" if h2 == 0 else "vector"

            def evp(e, h2=h2, jt=jt, eng=eng):
                cp = e.copy if eng == "scalar" else e.tensor_copy
                for jj in range(8):
                    hi = h2 * 8 + jj
                    cp(self.xtT[:, hi * 1024 + jt * 128:hi * 1024 + (jt + 1) * 128],
                       self.psTr[:, jj * 128:(jj + 1) * 128])
                return cp(self.fscr[:, 4:5],
                          self.xtT[:, (h2 * 8 + 7) * 1024 + (jt + 1) * 128 - 1:(h2 * 8 + 7) * 1024 + (jt + 1) * 128])
            self.tEv = self.q.on(eng, evp, after=[tt, self.tEv])
        self.done[jt] = self.tEv

    def store(self):
        def st(e, sem):
            for hi in range(16):
                e.dma_start(out=self.agin_d[hi * 128:(hi + 1) * 128, :],
                            in_=self.xtT[:, hi * 1024:(hi + 1) * 1024]).then_inc(sem, 16)
        return self.q.on("sync", st, after=[self.done[7]], ndma=16)



def emit_l0_qkv(nc, q, pool, dr, G):
    pool.reset()
    sb = pool.take
    xT = sb("xT", [128, 16 * S], BF16)
    wb = sb("wb", [128, 16 * 1024], BF16)
    cosT = sb("cosT", [128, 4 * S], F32)
    sinT = sb("sinT", [128, 4 * S], F32)
    scr = sb("scr", [128, S], F32)
    scr2 = sb("scr2", [128, S], F32)
    tmp1 = sb("tmp1", [128, S], F32)
    tmp2 = sb("tmp2", [128, S], F32)
    ro = sb("ro", [128, 2 * S], BF16)
    vo = sb("vo", [128, 2 * 1024], BF16)
    fscr = sb("fscr", [128, 8], F32)
    psQ = G["psBIG"]
    psV = G["ps2"]

    def ldtab(e, sem):
        e.dma_start(out=cosT[:], in_=dr["l0cos"]).then_inc(sem, 16)
        e.dma_start(out=sinT[:], in_=dr["l0sin"]).then_inc(sem, 16)
    tTab = q.on("sync", ldtab, ndma=2)

    def ldx(e, sem):
        for i in range(16):
            e.dma_start(out=xT[:, i * S:(i + 1) * S],
                        in_=dr["a1xT"][128 * i:128 * (i + 1), :]).then_inc(sem, 16)
    tX = q.on("gpsimd", ldx, ndma=16)

    tW = [None] * 3
    tLastUse = [None] * 3
    prevEvac = None
    ro_store = []
    vo_store = []
    for w in range(3):
        wd = [dr["a1wq"], dr["a1wk"], dr["a1wv"]][w]

        def ldw(e, sem, wd=wd):
            for i in range(16):
                e.dma_start(out=wb[:, i * 1024:(i + 1) * 1024],
                            in_=wd[128 * i:128 * (i + 1), :]).then_inc(sem, 16)
        tW[w] = q.on("gpsimd", ldw, after=[tLastUse[w - 1] if w else None], ndma=16)

        if w < 2:
            for a in range(4):
                def mmA(e, a=a):
                    for n in range(4):
                        for i in range(16):
                            ins = e.matmul(psQ[:, n * 512:(n + 1) * 512],
                                           wb[:, i * 1024 + a * 128:i * 1024 + (a + 1) * 128],
                                           xT[:, i * S + n * 512:i * S + (n + 1) * 512],
                                           start=(i == 0), stop=(i == 15))
                    return ins
                tA_ = q.on("tensor", mmA, after=[tX, tW[w], prevEvac])

                def evA(e):
                    e.tensor_copy(scr[:], psQ[:, 0:S])
                    return e.tensor_copy(fscr[:, 1:2], scr[:, S - 1:S])
                prevEvac = q.on("vector", evA, after=[tA_])

                def mmB(e, a=a):
                    for n in range(4):
                        for i in range(16):
                            ins = e.matmul(psQ[:, n * 512:(n + 1) * 512],
                                           wb[:, i * 1024 + (a + 4) * 128:i * 1024 + (a + 5) * 128],
                                           xT[:, i * S + n * 512:i * S + (n + 1) * 512],
                                           start=(i == 0), stop=(i == 15))
                    return ins
                tB_ = q.on("tensor", mmB, after=[prevEvac])
                tLastUse[w] = tB_

                def evB(e):
                    e.tensor_copy(scr2[:], psQ[:, 0:S])
                    return e.tensor_copy(fscr[:, 3:4], scr2[:, S - 1:S])
                prevEvac = q.on("vector", evB, after=[tB_])

                def rope(e, a=a):
                    e.tensor_tensor(tmp1[:], scr[:], cosT[:, a * S:(a + 1) * S], ALU.mult)
                    e.tensor_tensor(tmp2[:], scr2[:], sinT[:, a * S:(a + 1) * S], ALU.mult)
                    e.tensor_tensor(ro[:, 0:S], tmp1[:], tmp2[:], ALU.subtract)
                    e.tensor_tensor(tmp1[:], scr2[:], cosT[:, a * S:(a + 1) * S], ALU.mult)
                    e.tensor_tensor(tmp2[:], scr[:], sinT[:, a * S:(a + 1) * S], ALU.mult)
                    e.tensor_tensor(ro[:, S:2 * S], tmp1[:], tmp2[:], ALU.add)
                    return e.tensor_copy(fscr[:, 2:3], ro[:, 2 * S - 1:2 * S])
                tRope = q.on("vector", rope, after=[prevEvac, tTab,
                                                    ro_store[-1] if ro_store else None])

                od = [dr["qT0"], dr["kT0"]][w]

                def st2(e, sem, a=a, od=od):
                    e.dma_start(out=od[a * 128:(a + 1) * 128, :], in_=ro[:, 0:S]).then_inc(sem, 16)
                    e.dma_start(out=od[(a + 4) * 128:(a + 5) * 128, :], in_=ro[:, S:2 * S]).then_inc(sem, 16)
                ro_store.append(q.on("sync", st2, after=[tRope], ndma=2))
        else:
            prevV = None
            for j in range(16):
                def mmV(e, j=j):
                    for n in range(2):
                        for i in range(16):
                            ins = e.matmul(psV[:, n * 512:(n + 1) * 512],
                                           xT[:, i * S + j * 128:i * S + j * 128 + 128],
                                           wb[:, i * 1024 + n * 512:i * 1024 + (n + 1) * 512],
                                           start=(i == 0), stop=(i == 15))
                    return ins
                tV = q.on("tensor", mmV, after=[tW[2], prevV])
                tLastUse[2] = tV

                def evV(e, j=j):
                    e.tensor_copy(vo[:, (j % 2) * 1024:(j % 2 + 1) * 1024], psV[:])
                    return e.tensor_copy(fscr[:, 3:4], vo[:, (j % 2) * 1024 + 1023:(j % 2) * 1024 + 1024])
                prevV = q.on("vector", evV, after=[tV, vo_store[-2] if len(vo_store) >= 2 else None])

                def stV(e, sem, j=j):
                    e.dma_start(out=dr["vt0"][128 * j:128 * (j + 1), :],
                                in_=vo[:, (j % 2) * 1024:(j % 2 + 1) * 1024]).then_inc(sem, 16)
                vo_store.append(q.on("sync", stV, after=[prevV], ndma=1))



def emit_l0_attn(nc, q, pool, dr, G):
    pool.reset()
    sb = pool.take
    SCALE = float(1.0 / np.sqrt(128.0))
    qT = sb("qT_s", [128, 8 * S], BF16)
    kT = sb("kT_s", [128, 8 * S], BF16)
    vtok = sb("vt_s", [128, 16 * 1024], BF16)
    comb = sb("comb", [128, 16 * 1024], BF16)
    qs = sb("qs", [128, S], BF16)
    qf = sb("qf", [128, S], BF16)
    kf = sb("kf", [128, S], BF16)
    kft = sb("kft", [128, 16 * 128], BF16)
    tA = sb("tA", [128, S], F32)
    tB = sb("tB", [128, S], F32)
    P = sb("P", [128, S], BF16)
    PT = sb("PT", [128, 16 * 512], BF16)
    mem = sb("mem", [128, 132], BF16)
    msc = sb("msc", [128, 512], F32)
    mtmp = sb("mtmp", [128, 128], F32)
    onec = sb("onec", [128, 4], BF16)
    gbc = sb("gbc", [128, 8], F32)
    ombc = sb("ombc", [128, 8], F32)
    cols = sb("cols", [128, 8], F32)
    wcols = sb("wcols", [128, 4], F32)
    lcols = sb("lcols", [128, 4], F32)
    ctr = sb("ctr", [128, 2048], BF16)
    fscr = sb("fscr", [128, 8], F32)
    idt = G["idt"]
    psS = G["psBIG"]
    psT = G["psTr"]
    psM = G["ps3"]
    psD = G["ps2"][:, 0:512]
    psA = G["ps2"][:, 512:1024]

    def ld(e, sem):
        for lh in range(8):
            e.dma_start(out=qT[:, lh * S:(lh + 1) * S],
                        in_=dr["qT0"][128 * lh:128 * (lh + 1), :]).then_inc(sem, 16)
            e.dma_start(out=kT[:, lh * S:(lh + 1) * S],
                        in_=dr["kT0"][128 * lh:128 * (lh + 1), :]).then_inc(sem, 16)
        for j in range(16):
            e.dma_start(out=vtok[:, j * 1024:(j + 1) * 1024],
                        in_=dr["vt0"][128 * j:128 * (j + 1), :]).then_inc(sem, 16)
        e.dma_start(out=gbc[:], in_=dr["gate"].broadcast_to([128, 8])).then_inc(sem, 16)
    tLD = q.on("sync", ld, ndma=33)

    def prep_g(e):
        e.activation(gbc[:], gbc[:], AF.Sigmoid)
        return e.copy(fscr[:, 0:1], gbc[:, 7:8])
    tG = q.on("scalar", prep_g, after=[tLD])

    def prep_omg(e):
        e.tensor_scalar(ombc[:], gbc[:], -1.0, 1.0, ALU.mult, ALU.add)
        e.memset(onec[:], 1.0)
        return e.tensor_copy(fscr[:, 1:2], ombc[:, 7:8])
    tOMG = q.on("vector", prep_omg, after=[tG])

    tPTev = None
    tMemEv = None
    tAttEv = None
    tDev = None
    for lh in range(8):
        hOFF = lh * S

        def qsv(e, hOFF=hOFF):
            e.tensor_scalar_mul(qs[:], qT[:, hOFF:hOFF + S], SCALE)
            return e.tensor_copy(fscr[:, 2:3], qs[:, S - 1:S])
        tQS = q.on("vector", qsv, after=[tLD])

        def kf1(e, hOFF=hOFF):
            e.tensor_single_scalar(tA[:], kT[:, hOFF:hOFF + S], 0.0, ALU.min)
            return e.tensor_copy(fscr[:, 2:3], tA[:, S - 1:S])
        t1 = q.on("vector", kf1, after=[tLD])

        def kf2(e):
            e.activation(tB[:], tA[:], AF.Exp)
            return e.copy(fscr[:, 3:4], tB[:, S - 1:S])
        t2 = q.on("scalar", kf2, after=[t1])

        def kf3(e, hOFF=hOFF):
            e.tensor_single_scalar(tA[:], kT[:, hOFF:hOFF + S], 0.0, ALU.max)
            e.tensor_tensor(kf[:], tB[:], tA[:], ALU.add)
            return e.tensor_copy(fscr[:, 2:3], kf[:, S - 1:S])
        t3 = q.on("vector", kf3, after=[t2])

        for bt in range(2):
            def trk(e, bt=bt):
                for i in range(8):
                    ins = e.transpose(psT[:, i * 128:(i + 1) * 128],
                                      kf[:, (bt * 8 + i) * 128:(bt * 8 + i + 1) * 128], idt)
                return ins
            tt = q.on("tensor", trk, after=[t3, tPTev])
            eng = "scalar" if bt == 0 else "vector"

            def evk(e, bt=bt, eng=eng):
                cp = e.copy if eng == "scalar" else e.tensor_copy
                for i in range(8):
                    cp(kft[:, (bt * 8 + i) * 128:(bt * 8 + i + 1) * 128],
                       psT[:, i * 128:(i + 1) * 128])
                return cp(fscr[:, 4:5], kft[:, (bt * 8 + 7) * 128 + 127:(bt * 8 + 7) * 128 + 128])
            tPTev = q.on(eng, evk, after=[tt, tPTev])
        tTrKdone = tPTev

        def qf1(e, hOFF=hOFF):
            e.tensor_single_scalar(tA[:], qT[:, hOFF:hOFF + S], 0.0, ALU.min)
            return e.tensor_copy(fscr[:, 2:3], tA[:, S - 1:S])
        t4 = q.on("vector", qf1, after=[t3])

        def qf2(e):
            e.activation(tB[:], tA[:], AF.Exp)
            return e.copy(fscr[:, 3:4], tB[:, S - 1:S])
        t5 = q.on("scalar", qf2, after=[t4])

        def qf3(e, hOFF=hOFF):
            e.tensor_single_scalar(tA[:], qT[:, hOFF:hOFF + S], 0.0, ALU.max)
            e.tensor_tensor(qf[:], tB[:], tA[:], ALU.add)
            return e.tensor_copy(fscr[:, 2:3], qf[:, S - 1:S])
        t6 = q.on("vector", qf3, after=[t5])

        def memmm(e, lh=lh):
            for j in range(16):
                e.matmul(psM[:, 0:128], kft[:, j * 128:(j + 1) * 128],
                         vtok[:, j * 1024 + lh * 128:j * 1024 + (lh + 1) * 128],
                         start=(j == 0), stop=(j == 15))
            for j in range(16):
                ins = e.matmul(psM[:, 128:129], kft[:, j * 128:(j + 1) * 128],
                               onec[:, 0:1], start=(j == 0), stop=(j == 15))
            return ins
        tMM = q.on("tensor", memmm, after=[tTrKdone, tOMG, tMemEv])

        def memev(e):
            e.copy(mem[:, 0:129], psM[:, 0:129])
            return e.copy(fscr[:, 5:6], mem[:, 128:129])
        tMemEv = q.on("scalar", memev, after=[tMM])

        for qg in range(4):
            tPh = []
            tEx = [None] * 4
            tWS = [None] * 4
            tLS = [None] * 4
            for qc in range(4):
                jq = qg * 4 + qc

                def smm(e, hOFF=hOFF, jq=jq):
                    for n in range(4):
                        ins = e.matmul(psS[:, n * 512:(n + 1) * 512],
                                       qs[:, jq * 128:(jq + 1) * 128],
                                       kT[:, hOFF + n * 512:hOFF + (n + 1) * 512],
                                       start=True, stop=True)
                    return ins
                tS = q.on("tensor", smm, after=[tQS, tEx[qc - 1] if qc else tPTev, None if qc else tAttEv])

                def smax(e):
                    e.reduce_max(cols[:, 0:1], psS[:], axis=AX, negate=True)
                    return e.tensor_copy(cols[:, 1:2], cols[:, 0:1])
                tMx = q.on("vector", smax, after=[tS])

                def sexp(e):
                    e.activation(P[:], psS[:], AF.Exp, bias=cols[:, 0:1], scale=1.0,
                                 accum_out=cols[:, 3:4])
                    e.copy(fscr[:, 6:7], P[:, S - 1:S])
                    return e.copy(fscr[:, 6:7], cols[:, 3:4])
                tEx[qc] = q.on("scalar", sexp, after=[tMx])

                def vrec(e):
                    e.reciprocal(cols[:, 4:5], cols[:, 3:4])
                    return e.tensor_copy(cols[:, 5:6], cols[:, 4:5])
                tR = q.on("vector", vrec, after=[tEx[qc]])

                def swc(e, lh=lh, qc=qc):
                    e.mul(wcols[:, qc:qc + 1], cols[:, 4:5], ombc[:, lh:lh + 1])
                    return e.copy(fscr[:, 7:8], wcols[:, qc:qc + 1])
                tWS[qc] = q.on("scalar", swc, after=[tR])

                for bt in range(2):
                    def trp(e, bt=bt):
                        for i in range(8):
                            ins = e.transpose(psT[:, i * 128:(i + 1) * 128],
                                              P[:, (bt * 8 + i) * 128:(bt * 8 + i + 1) * 128], idt)
                        return ins
                    tt = q.on("tensor", trp, after=[tEx[qc], tPTev])
                    eng = "scalar" if bt == 0 else "vector"

                    def evp(e, bt=bt, qc=qc, eng=eng):
                        cp = e.copy if eng == "scalar" else e.tensor_copy
                        for i in range(8):
                            j = bt * 8 + i
                            cp(PT[:, j * 512 + qc * 128:j * 512 + (qc + 1) * 128],
                               psT[:, i * 128:(i + 1) * 128])
                        return cp(fscr[:, 4:5], PT[:, (bt * 8 + 7) * 512 + qc * 128 + 127:(bt * 8 + 7) * 512 + qc * 128 + 128])
                    tPTev = q.on(eng, evp, after=[tt, tPTev])
                    tPh.append(tPTev)

                def dmm(e, jq=jq):
                    return e.matmul(psD[:, 0:129], qf[:, jq * 128:(jq + 1) * 128],
                                    mem[:, 0:129], start=True, stop=True)
                tD = q.on("tensor", dmm, after=[tMemEv, t6, tDev, None if qc else tAttEv])

                def dev(e, qc=qc):
                    e.tensor_copy(msc[:, qc * 128:(qc + 1) * 128], psD[:, 0:128])
                    e.reciprocal(cols[:, 6:7], psD[:, 128:129])
                    return e.tensor_copy(cols[:, 7:8], cols[:, 6:7])
                tDev = q.on("vector", dev, after=[tD])

                def slc(e, lh=lh, qc=qc):
                    e.mul(lcols[:, qc:qc + 1], cols[:, 6:7], gbc[:, lh:lh + 1])
                    return e.copy(fscr[:, 7:8], lcols[:, qc:qc + 1])
                tLS[qc] = q.on("scalar", slc, after=[tDev])

            def amm(e, lh=lh):
                for qc in range(4):
                    for j in range(16):
                        ins = e.matmul(psA[:, qc * 128:(qc + 1) * 128],
                                       PT[:, j * 512 + qc * 128:j * 512 + (qc + 1) * 128],
                                       vtok[:, j * 1024 + lh * 128:j * 1024 + (lh + 1) * 128],
                                       start=(j == 0), stop=(j == 15))
                return ins
            tAm = q.on("tensor", amm, after=[tPh[-1], tAttEv])

            def combv(e, lh=lh, qg=qg):
                for qc in range(4):
                    jq = qg * 4 + qc
                    e.tensor_scalar(mtmp[:], msc[:, qc * 128:(qc + 1) * 128],
                                    lcols[:, qc:qc + 1], None, ALU.mult)
                    e.scalar_tensor_tensor(
                        comb[:, jq * 1024 + lh * 128:jq * 1024 + (lh + 1) * 128],
                        psA[:, qc * 128:(qc + 1) * 128], wcols[:, qc:qc + 1], mtmp[:],
                        ALU.mult, ALU.add)
                return e.tensor_copy(fscr[:, 2:3], comb[:, (qg * 4 + 3) * 1024 + lh * 128 + 127:(qg * 4 + 3) * 1024 + lh * 128 + 128])
            tAttEv = q.on("vector", combv, after=[tAm, tWS[3], tLS[3], tDev])

    stores = emit_comb_out(q, comb, idt, psT, ctr, fscr, dr["agin_comb"], tAttEv)
    tAG = q.cc(lambda e: e.collective_compute(
        "AllGather", ALU.bypass, replica_groups=G["rg8"],
        ins=[dr["agin_comb"].opt()], outs=[dr["agout_comb"].opt()]),
        after=stores)
    return tAG



def emit_oproj_ln(nc, q, pool, dr, G):
    pool.reset()
    sb = pool.take
    cbT = sb("cbT_s", [128, 16 * 1024], BF16)
    woT = sb("woT_s", [128, 16 * H], BF16)
    xres = sb("xres", [128, 2 * H], F32)
    xo = sb("xo", [128, H], F32)
    xc = sb("xc", [128, H], F32)
    scrap = sb("scrap", [128, H], F32)
    x1f = sb("x1f_s", [128, 2 * H], F32)
    x1b = sb("x1b_s", [128, 2 * H], BF16)
    xtT = sb("xtT", [128, 16 * 1024], BF16)
    cols = sb("cols", [128, 8], F32)
    fscr = sb("fscr", [128, 8], F32)
    psO = G["psBIG"]
    tr = XTrOut(q, G["idt"], G["psTr"], xtT, fscr, dr["agin_x1"])

    def ldc(e, sem):
        for k in range(16):
            e.indirect_dma_start(
                out=cbT[:, k * 1024:(k + 1) * 1024], out_offset=None,
                in_=dr["agout_comb"],
                in_offset=bass.IndirectOffsetOnAxis(G["idxB"][:, k:k + 1], axis=0),
            ).then_inc(sem, 16)
    tLD = q.on("gpsimd", ldc, after=[dr["tAGcomb"]], ndma=16)

    def ldw(e, sem):
        for i in range(16):
            e.dma_start(out=woT[:, i * H:(i + 1) * H],
                        in_=dr["bwoT"][128 * i:128 * (i + 1), :]).then_inc(sem, 16)
    tW = q.on("gpsimd", ldw, ndma=16)

    tXO = [None] * 8
    tAPP = [None] * 8
    tStore = [None] * 8
    for jt in range(8):
        slot = jt % 2

        def ldx(e, sem, jt=jt, slot=slot):
            e.dma_start(out=xres[:, slot * H:(slot + 1) * H],
                        in_=dr["bx"][128 * jt:128 * (jt + 1), :]).then_inc(sem, 16)
        tX = q.on("sync", ldx, after=[tXO[jt - 2] if jt >= 2 else None], ndma=1)

        def mm(e, jt=jt):
            for n in range(4):
                for i in range(16):
                    ins = e.matmul(psO[:, n * 512:(n + 1) * 512],
                                   cbT[:, i * 1024 + jt * 128:i * 1024 + (jt + 1) * 128],
                                   woT[:, i * H + n * 512:i * H + (n + 1) * 512],
                                   start=(i == 0), stop=(i == 15))
            return ins
        tM = q.on("tensor", mm, after=[tLD, tW, tXO[jt - 1] if jt >= 1 else None])

        def v_xo(e, slot=slot):
            e.scalar_tensor_tensor(xo[:], psO[:], 1.0, xres[:, slot * H:(slot + 1) * H],
                                   ALU.mult, ALU.add, accum_out=cols[:, 0:1])
            return e.tensor_copy(fscr[:, 1:2], cols[:, 0:1])
        tXO[jt] = q.on("vector", v_xo, after=[tM, tX])

        def apply(e, jt=jt, slot=slot):
            e.activation(x1f[:, slot * H:(slot + 1) * H], xc[:], AF.Copy,
                         scale=cols[:, 5:6])
            e.copy(x1b[:, slot * H:(slot + 1) * H],
                   x1f[:, slot * H:(slot + 1) * H])
            e.copy(fscr[:, 0:1], x1f[:, slot * H + H - 1:slot * H + H])
            return e.copy(fscr[:, 0:1], x1b[:, slot * H + H - 1:slot * H + H])
        tAPP[jt] = ln_zigzag(q, tXO[jt], xo, xc, scrap, cols, fscr, 0, H, 1e-5, apply,
                             extra_after=[tStore[jt - 2] if jt >= 2 else None,
                                          tr.done[jt - 2] if jt >= 2 else None])

        def st(e, sem, jt=jt, slot=slot):
            e.dma_start(out=dr["x1f"][128 * jt:128 * (jt + 1), :],
                        in_=x1f[:, slot * H:(slot + 1) * H]).then_inc(sem, 16)
        tStore[jt] = q.on("sync", st, after=[tAPP[jt]], ndma=1)

        tr.add_tile(jt, x1b[:, slot * H:(slot + 1) * H], tAPP[jt])

    tSt = tr.store()
    tAG = q.cc(lambda e: e.collective_compute(
        "AllGather", ALU.bypass, replica_groups=G["rg8"],
        ins=[dr["agin_x1"].opt()], outs=[dr["agout_x1"].opt()]),
        after=[tSt, tStore[6], tStore[7]])
    return tAG



def _emit_decode(q, tiles, src_u8, scl, dst, dst_off, after, m75):
    tA, tB, tBi, nb1, nb2, d1, d2, sgt = tiles
    NC = 2048

    def v_bytes(e):
        e.tensor_copy(tA[:], src_u8[:])
        e.tensor_scalar(tB[:], tA[:], 0.0625, -0.46875, ALU.mult, ALU.add)
        e.tensor_copy(tBi[:], tB[:])
        e.tensor_copy(tB[:], tBi[:])
        e.scalar_tensor_tensor(nb1[:], tB[:], -16.0, tA[:], ALU.mult, ALU.add)
        e.tensor_copy(nb2[:], tB[:])
        return e.tensor_copy(tA[:, 0:1], nb2[:, NC - 1:NC])
    tV = q.on("vector", v_bytes, after=after)

    outs = [None]
    for pi, nb in ((0, nb1), (1, nb2)):
        def s_sign(e, nb=nb):
            e.activation(sgt[:], nb[:], AF.Sign, bias=m75[:, 0:1])
            return e.copy(tA[:, 1:2], sgt[:, NC - 1:NC])
        tS = q.on("scalar", s_sign, after=[tV, outs[-1]])

        def v_dec(e, nb=nb, pi=pi):
            e.scalar_tensor_tensor(d1[:], sgt[:], -4.0, nb[:], ALU.mult, ALU.add)
            e.tensor_scalar(d2[:], d1[:], -8.0, 0.0, ALU.add, ALU.max)
            e.tensor_scalar(nb[:], d1[:], -10.0, 0.0, ALU.add, ALU.max)
            e.scalar_tensor_tensor(d2[:], nb[:], 2.0, d2[:], ALU.mult, ALU.add)
            e.tensor_tensor(d2[:], d2[:], d1[:], ALU.add)
            e.tensor_scalar(d2[:], d2[:], -4.0, None, ALU.add)
            e.tensor_tensor(d2[:], d2[:], sgt[:], ALU.mult)
            e.scalar_tensor_tensor(dst[:, dst_off + pi * NC:dst_off + (pi + 1) * NC],
                                   d2[:], -0.5, scl[:], ALU.mult, ALU.mult)
            return e.tensor_copy(tA[:, 2:3],
                                 dst[:, dst_off + (pi + 1) * NC - 1:dst_off + (pi + 1) * NC])
        outs.append(q.on("vector", v_dec, after=[tS]))
    return outs[-1]


def emit_l0_ffn(nc, q, pool, dr, G):
    pool.reset()
    sb = pool.take
    wg = sb("wg", [128, 16 * 2048], BF16)
    wu = sb("wu", [128, 16 * 2048], BF16)
    u8t = sb("u8t", [128, 2048], U8)
    sclt = sb("sclt", [128, 2048], F32)
    tA = sb("tA", [128, 2048], F32)
    tB = sb("tB", [128, 2048], F32)
    tBi = sb("tBi", [128, 2048], I32)
    obuf = tA.bitcast(BF16)
    nb1 = sb("nb1", [128, 2048], BF16)
    nb2 = sb("nb2", [128, 2048], BF16)
    d1 = sb("d1", [128, 2048], BF16)
    d2 = sb("d2", [128, 2048], BF16)
    sgt = sb("sgt", [128, 2048], BF16)
    sq = sb("sq", [128, 512], F32)
    cc_ = sb("cc_", [128, 512], F32)
    arg = sb("arg", [128, 512], F32)
    th = sq
    dd = cc_
    e1 = arg
    ibuf = sb("ibuf", [128, 2 * 512], BF16)
    fscr = sb("fscr", [128, 8], F32)
    m75c = sb("m75c", [128, 1], F32)
    xcb = sb("xcb", [128, 16 * 512], BF16)
    psG = G["ps2"]
    psU = G["psBIG"][:, 0:1024]
    psBIG = G["psBIG"]
    dec_tiles = (tA, tB, tBi, nb1, nb2, d1, d2, sgt)
    xg = dr["agout_x1"]

    def mkc(e):
        e.memset(m75c[:], -7.5)
        return e.tensor_copy(fscr[:, 3:4], m75c[:])
    tM75 = q.on("vector", mkc)

    tDecDone = None
    tLdPrev = [None, None]
    tDecQ = []
    for a in range(8):
        for mi, (wdst, t_d, s_d) in enumerate([(wg, dr["tg"], dr["gsT"]), (wu, dr["tu"], dr["usT"])]):
            slot = (2 * a + mi) % 2

            def ldt(e, sem, a=a, t_d=t_d, s_d=s_d):
                e.dma_start(out=u8t[:],
                            in_=t_d[128 * a:128 * (a + 1), :]).then_inc(sem, 16)
                e.dma_start(out=sclt[:],
                            in_=s_d[8 * a:8 * (a + 1), :].unsqueeze(1).broadcast_to([8, 16, 2048])).then_inc(sem, 16)
            tL = q.on("sync", ldt, after=[tLdPrev[slot], tDecDone], ndma=2)
            tDecDone = _emit_decode(q, dec_tiles, u8t[:], sclt[:],
                                    wdst, (2 * a) * 2048, after=[tL, tDecDone, tM75], m75=m75c)
            tLdPrev[slot] = tDecDone
        if a % 2 == 1:
            tDecQ.append(tDecDone)

    tGU = None
    tIst = [None] * 4
    prev_mm = [None, None]
    for tc in range(16):
        r, u = tc // 2, tc % 2

        def ldxc(e, sem, r=r, u=u):
            for i in range(16):
                base = 2048 * r + 256 * (i // 2) + (i % 2)
                e.dma_start(out=xcb[:, i * 512:(i + 1) * 512],
                            in_=xg[base:base + 255:2,
                                   512 * u:512 * (u + 1)]).then_inc(sem, 16)
        tXc = q.on("sync", ldxc, after=[tGU, dr["tAGx1"] if tc == 0 else None], ndma=16)

        for nt in range(16):
            slot = nt % 2

            if tc == 0:
                for qk in range(4):
                    def gummq(e, nt=nt, slot=slot, qk=qk):
                        for i in range(4 * qk, 4 * qk + 4):
                            e.matmul(psG[:, slot * 512:(slot + 1) * 512],
                                     wg[:, i * 2048 + nt * 128:i * 2048 + (nt + 1) * 128],
                                     xcb[:, i * 512:(i + 1) * 512],
                                     start=(i == 0), stop=(i == 15))
                        for i in range(4 * qk, 4 * qk + 4):
                            ins = e.matmul(psU[:, slot * 512:(slot + 1) * 512],
                                           wu[:, i * 2048 + nt * 128:i * 2048 + (nt + 1) * 128],
                                           xcb[:, i * 512:(i + 1) * 512],
                                           start=(i == 0), stop=(i == 15))
                        return ins
                    tGU = q.on("tensor", gummq,
                               after=[tXc, tDecQ[qk],
                                      prev_mm[slot] if qk == 0 else None])
            else:
                def gumm(e, nt=nt, slot=slot):
                    for i in range(16):
                        e.matmul(psG[:, slot * 512:(slot + 1) * 512],
                                 wg[:, i * 2048 + nt * 128:i * 2048 + (nt + 1) * 128],
                                 xcb[:, i * 512:(i + 1) * 512],
                                 start=(i == 0), stop=(i == 15))
                    for i in range(16):
                        ins = e.matmul(psU[:, slot * 512:(slot + 1) * 512],
                                       wu[:, i * 2048 + nt * 128:i * 2048 + (nt + 1) * 128],
                                       xcb[:, i * 512:(i + 1) * 512],
                                       start=(i == 0), stop=(i == 15))
                    return ins
                tGU = q.on("tensor", gumm, after=[tXc, tDecDone, prev_mm[slot]])

            def gelu1(e, slot=slot):
                e.activation(sq[:], psG[:, slot * 512:(slot + 1) * 512], AF.Square)
                return e.copy(fscr[:, 0:1], sq[:, 511:512])
            tq1 = q.on("scalar", gelu1, after=[tGU])

            def gelu2(e, slot=slot):
                e.tensor_scalar(cc_[:], sq[:], 0.044715 * 0.7978845608028654,
                                0.7978845608028654, ALU.mult, ALU.add)
                e.tensor_tensor(arg[:], cc_[:], psG[:, slot * 512:(slot + 1) * 512], ALU.mult)
                return e.tensor_copy(fscr[:, 1:2], arg[:, 511:512])
            tq2 = q.on("vector", gelu2, after=[tq1])

            def gelu3(e):
                e.activation(th[:], arg[:], AF.Tanh)
                return e.copy(fscr[:, 0:1], th[:, 511:512])
            tq3 = q.on("scalar", gelu3, after=[tq2])

            def gelu4(e, slot=slot):
                e.tensor_scalar(dd[:], th[:], 1.0, 0.5, ALU.add, ALU.mult)
                e.tensor_tensor(e1[:], dd[:], psG[:, slot * 512:(slot + 1) * 512], ALU.mult)
                e.tensor_tensor(ibuf[:, slot * 512:(slot + 1) * 512], e1[:],
                                psU[:, slot * 512:(slot + 1) * 512], ALU.mult)
                return e.tensor_copy(fscr[:, 1:2], ibuf[:, slot * 512 + 511:slot * 512 + 512])
            tq4 = q.on("vector", gelu4, after=[tq3, tIst[slot] if nt >= 2 else None])
            prev_mm[slot] = tq4

            def ist(e, sem, nt=nt, tc=tc, slot=slot):
                e.dma_start(out=dr["inter"][128 * nt:128 * (nt + 1), 512 * tc:512 * (tc + 1)],
                            in_=ibuf[:, slot * 512:(slot + 1) * 512]).then_inc(sem, 16)
            tIst[slot] = q.on("sync", ist, after=[tq4], ndma=1)
    tPh1St = [tIst[0], tIst[1]]

    wd = wg
    inT = xcb
    tDec2 = tGU
    tLdPrev2 = [tPh1St[0], tPh1St[1]]
    for a in range(8):
        slot = a % 2

        def ldt2(e, sem, a=a):
            e.dma_start(out=u8t[:],
                        in_=dr["td"][128 * a:128 * (a + 1), :]).then_inc(sem, 16)
            e.dma_start(out=sclt[:],
                        in_=dr["dsT"][8 * a:8 * (a + 1), :].unsqueeze(1).broadcast_to([8, 16, 2048])).then_inc(sem, 16)
        tL2 = q.on("sync", ldt2, after=[tLdPrev2[slot], tPh1St[slot], tDec2], ndma=2)
        tDec2 = _emit_decode(q, dec_tiles, u8t[:], sclt[:],
                             wd, (2 * a) * 2048, after=[tL2, tDec2, tGU], m75=m75c)
        tLdPrev2[slot] = tDec2
        if a == 3:
            tDec2H = tDec2

    tOev = [None, None]
    tOevLast = None
    tOst = [None, None]
    tMM2 = None
    for tc in range(16):
        def ldin(e, sem, tc=tc):
            for i in range(16):
                e.dma_start(out=inT[:, i * 512:(i + 1) * 512],
                            in_=dr["inter"][128 * i:128 * (i + 1), 512 * tc:512 * (tc + 1)]).then_inc(sem, 16)
        tI = q.on("gpsimd", ldin, after=[tMM2, tPh1St[0], tPh1St[1]], ndma=16)

        for tq_ in range(4):
            slot = tq_ % 2

            if tc == 0:
                def dmm2a(e, tq_=tq_):
                    for hq in range(4):
                        for ntx in range(8):
                            ins = e.matmul(psBIG[:, hq * 512:(hq + 1) * 512],
                                           inT[:, ntx * 512 + tq_ * 128:ntx * 512 + (tq_ + 1) * 128],
                                           wd[:, ntx * 2048 + hq * 512:ntx * 2048 + (hq + 1) * 512],
                                           start=(ntx == 0), stop=False)
                    return ins
                tMa = q.on("tensor", dmm2a, after=[tI, tDec2H, tOevLast])

                def dmm2b(e, tq_=tq_):
                    for hq in range(4):
                        for ntx in range(8, 16):
                            ins = e.matmul(psBIG[:, hq * 512:(hq + 1) * 512],
                                           inT[:, ntx * 512 + tq_ * 128:ntx * 512 + (tq_ + 1) * 128],
                                           wd[:, ntx * 2048 + hq * 512:ntx * 2048 + (hq + 1) * 512],
                                           start=False, stop=(ntx == 15))
                    return ins
                tMM2 = q.on("tensor", dmm2b, after=[tMa, tDec2])
            else:
                def dmm2(e, tq_=tq_):
                    for hq in range(4):
                        for ntx in range(16):
                            ins = e.matmul(psBIG[:, hq * 512:(hq + 1) * 512],
                                           inT[:, ntx * 512 + tq_ * 128:ntx * 512 + (tq_ + 1) * 128],
                                           wd[:, ntx * 2048 + hq * 512:ntx * 2048 + (hq + 1) * 512],
                                           start=(ntx == 0), stop=(ntx == 15))
                    return ins
                tMM2 = q.on("tensor", dmm2, after=[tI, tDec2, tOevLast])

            def oev(e, slot=slot):
                e.tensor_copy(obuf[:, slot * 2048:(slot + 1) * 2048], psBIG[:])
                return e.tensor_copy(fscr[:, 1:2], obuf[:, slot * 2048 + 2047:slot * 2048 + 2048])
            tOev[slot] = q.on("vector", oev, after=[tMM2, tOst[slot]])
            tOevLast = tOev[slot]

            def ost(e, sem, tc=tc, tq_=tq_, slot=slot):
                e.dma_start(out=dr["part1"][512 * tc + 128 * tq_:512 * tc + 128 * (tq_ + 1), :],
                            in_=obuf[:, slot * 2048:(slot + 1) * 2048]).then_inc(sem, 16)
            tOst[slot] = q.on("sync", ost, after=[tOev[slot]], ndma=1)

    tRS = q.cc(lambda e: e.collective_compute(
        "ReduceScatter", ALU.add, replica_groups=G["rg8"],
        ins=[dr["part1"].opt()], outs=[dr["rs1"].opt()]),
        after=[tOst[0], tOst[1]])
    return tRS



def emit_postffn_ln(nc, q, pool, dr, G):
    pool.reset()
    sb = pool.take
    pt = sb("pt", [128, 2 * H], BF16)
    xres = sb("xres", [128, 2 * H], F32)
    xo = sb("xo", [128, H], F32)
    xc = sb("xc", [128, H], F32)
    scrap = sb("scrap", [128, H], F32)
    yf = sb("yf_s", [128, 2 * H], F32)
    yb = sb("yb_s", [128, 2 * H], BF16)
    xtT = sb("xtT", [128, 16 * 1024], BF16)
    cols = sb("cols", [128, 8], F32)
    fscr = sb("fscr", [128, 8], F32)
    tr = XTrOut(q, G["idt"], G["psTr"], xtT, fscr, dr["agin_x2"])

    tSum = [None] * 8
    tStore = [None] * 8
    for jt in range(8):
        slot = jt % 2

        def ld(e, sem, jt=jt, slot=slot):
            e.dma_start(out=pt[:, slot * H:(slot + 1) * H],
                        in_=dr["rs1"][128 * jt:128 * (jt + 1), :]).then_inc(sem, 16)
            e.dma_start(out=xres[:, slot * H:(slot + 1) * H],
                        in_=dr["x1f"][128 * jt:128 * (jt + 1), :]).then_inc(sem, 16)
        tL = q.on("sync", ld, after=[tSum[jt - 2] if jt >= 2 else None,
                                     dr["tRS1"] if jt == 0 else None], ndma=2)

        def vsum(e, slot=slot):
            e.scalar_tensor_tensor(xo[:], pt[:, slot * H:(slot + 1) * H], 1.0,
                                   xres[:, slot * H:(slot + 1) * H],
                                   ALU.mult, ALU.add, accum_out=cols[:, 0:1])
            return e.tensor_copy(fscr[:, 1:2], cols[:, 0:1])
        tSum[jt] = q.on("vector", vsum, after=[tL])

        def apply(e, jt=jt, slot=slot):
            e.activation(yf[:, slot * H:(slot + 1) * H], xc[:], AF.Copy, scale=cols[:, 5:6])
            e.copy(yb[:, slot * H:(slot + 1) * H], yf[:, slot * H:(slot + 1) * H])
            e.copy(fscr[:, 0:1], yf[:, slot * H + H - 1:slot * H + H])
            return e.copy(fscr[:, 0:1], yb[:, slot * H + H - 1:slot * H + H])
        tAPP = ln_zigzag(q, tSum[jt], xo, xc, scrap, cols, fscr, 0, H, 1e-5, apply,
                         extra_after=[tStore[jt - 2] if jt >= 2 else None,
                                      tr.done[jt - 2] if jt >= 2 else None])

        def st(e, sem, jt=jt, slot=slot):
            e.dma_start(out=dr["x2f"][128 * jt:128 * (jt + 1), :],
                        in_=yf[:, slot * H:(slot + 1) * H]).then_inc(sem, 16)
        tStore[jt] = q.on("sync", st, after=[tAPP], ndma=1)

        tr.add_tile(jt, yb[:, slot * H:(slot + 1) * H], tAPP)

    tSt = tr.store()
    tAG = q.cc(lambda e: e.collective_compute(
        "AllGather", ALU.bypass, replica_groups=G["rg8"],
        ins=[dr["agin_x2"].opt()], outs=[dr["agout_x2"].opt()]),
        after=[tSt, tStore[6], tStore[7]])
    return tAG



def emit_l1_qkv(nc, q, pool, dr, G):
    pool.reset()
    sb = pool.take
    xT = sb("xT_s", [128, 16 * S], BF16)
    wb = sb("wb", [128, 16 * 1024], BF16)
    cosq = sb("cosq_s", [128, 16 * 64], F32)
    sinq = sb("sinq_s", [128, 16 * 64], F32)
    qgc = sb("qgc", [128, 8], F32)
    qn = sb("qn", [128, 2 * 1024], F32)
    qro = sb("qro", [128, 16 * 1024], BF16)
    kro = sb("kro", [128, 16 * 256], BF16)
    vo = sb("vo", [128, 2 * 256], BF16)
    orow = sb("orow", [128, S], BF16)
    sqc = sb("sqc", [128, 64], F32)
    scrap = sb("scrap", [128, 128], F32)
    vtmp = sb("vtmp", [128, 128], F32)
    fscr = sb("fscr", [128, 8], F32)
    idt = G["idt"]
    psQ = G["ps2"]
    psQb = G["psBIG"][:, 0:1024]
    psT = G["psTr"]

    def ld(e, sem):
        e.dma_start(out=cosq[:], in_=dr["l1cos"]).then_inc(sem, 16)
        e.dma_start(out=sinq[:], in_=dr["l1sin"]).then_inc(sem, 16)
        e.dma_start(out=qgc[:], in_=dr["qg"].broadcast_to([128, 8])).then_inc(sem, 16)
    tLD = q.on("sync", ld, ndma=3)

    def ldx(e, sem):
        for i in range(16):
            for blk in range(2):
                e.indirect_dma_start(
                    out=xT[:, i * S + blk * 1024:i * S + (blk + 1) * 1024], out_offset=None,
                    in_=dr["agout_x2"],
                    in_offset=bass.IndirectOffsetOnAxis(G["idxPair"][:, 2 * i + blk:2 * i + blk + 1], axis=0),
                ).then_inc(sem, 16)
    tX = q.on("gpsimd", ldx, after=[dr["tAGx2"]], ndma=32)

    def emit_proj_norm_rope(w_d, ncols, nheads, out_tiles, tPrevUse, eps, prev_t5):
        def ldw(e, sem):
            for i in range(16):
                e.dma_start(out=wb[:, i * ncols:(i + 1) * ncols],
                            in_=w_d[128 * i:128 * (i + 1), :]).then_inc(sem, 16)
        tW = q.on("gpsimd", ldw, after=[tPrevUse], ndma=16)

        toks = [None] * 16
        t5H = [None] * 16
        tLastMM = None
        for j in range(16):
            psq = psQ if j % 2 == 0 else psQb
            sq0 = 32 * (j % 2)
            qo = (j % 2) * 1024

            def pmm(e, j=j, ncols=ncols, psq=psq):
                nch = max(1, ncols // 512)
                for n in range(nch):
                    w512 = min(512, ncols)
                    for i in range(16):
                        ins = e.matmul(psq[:, n * 512:n * 512 + w512],
                                       xT[:, i * S + j * 128:i * S + j * 128 + 128],
                                       wb[:, i * ncols + n * w512:i * ncols + (n + 1) * w512],
                                       start=(i == 0), stop=(i == 15))
                return ins
            tMM = q.on("tensor", pmm,
                       after=[tX, tW, t5H[j - 2] if j >= 2 else prev_t5[j % 2]])
            tLastMM = tMM

            def ssq(e, nheads=nheads, psq=psq, sq0=sq0):
                for lh in range(nheads):
                    ins = e.activation(scrap[:], psq[:, lh * 128:(lh + 1) * 128], AF.Square,
                                       accum_out=sqc[:, sq0 + lh:sq0 + lh + 1])
                return e.copy(fscr[:, 0:1], sqc[:, sq0 + nheads - 1:sq0 + nheads])
            t1 = q.on("scalar", ssq, after=[tMM])

            def vw(e, nheads=nheads, eps=eps, sq0=sq0):
                e.tensor_scalar(sqc[:, sq0 + 8:sq0 + 8 + nheads], sqc[:, sq0:sq0 + nheads],
                                1.0 / 128.0, eps, ALU.mult, ALU.add)
                return e.tensor_copy(fscr[:, 1:2], sqc[:, sq0 + 8 + nheads - 1:sq0 + 8 + nheads])
            t2 = q.on("vector", vw, after=[t1])

            def s3(e, nheads=nheads, sq0=sq0):
                e.activation(sqc[:, sq0 + 16:sq0 + 16 + nheads], sqc[:, sq0 + 8:sq0 + 8 + nheads], AF.Sqrt)
                return e.copy(fscr[:, 0:1], sqc[:, sq0 + 16 + nheads - 1:sq0 + 16 + nheads])
            t3 = q.on("scalar", s3, after=[t2])

            def v4(e, nheads=nheads, sq0=sq0):
                e.reciprocal(sqc[:, sq0 + 24:sq0 + 24 + nheads], sqc[:, sq0 + 16:sq0 + 16 + nheads])
                return e.tensor_copy(fscr[:, 1:2], sqc[:, sq0 + 24 + nheads - 1:sq0 + 24 + nheads])
            t4 = q.on("vector", v4, after=[t3])

            def s5(e, nheads=nheads, psq=psq, sq0=sq0, qo=qo):
                for lh in range(nheads):
                    ins = e.activation(qn[:, qo + lh * 128:qo + (lh + 1) * 128],
                                       psq[:, lh * 128:(lh + 1) * 128], AF.Copy,
                                       scale=sqc[:, sq0 + 24 + lh:sq0 + 25 + lh])
                return e.copy(fscr[:, 0:1], qn[:, qo + nheads * 128 - 1:qo + nheads * 128])
            t5 = q.on("scalar", s5, after=[t4, toks[j - 2] if j >= 2 else None])
            t5H[j] = t5

            def vrope(e, j=j, nheads=nheads, ncols=ncols, qo=qo):
                c = cosq[:, j * 64:(j + 1) * 64]
                s = sinq[:, j * 64:(j + 1) * 64]
                for lh in range(nheads):
                    a = qn[:, qo + lh * 128:qo + lh * 128 + 64]
                    b2 = qn[:, qo + lh * 128 + 64:qo + lh * 128 + 128]
                    e.tensor_tensor(vtmp[:, 0:64], a, c, ALU.mult)
                    e.tensor_tensor(vtmp[:, 64:128], b2, s, ALU.mult)
                    e.tensor_tensor(out_tiles[:, j * ncols + lh * 128:j * ncols + lh * 128 + 64],
                                    vtmp[:, 0:64], vtmp[:, 64:128], ALU.add)
                    e.tensor_tensor(vtmp[:, 0:64], b2, c, ALU.mult)
                    e.tensor_tensor(vtmp[:, 64:128], a, s, ALU.mult)
                    e.tensor_tensor(out_tiles[:, j * ncols + lh * 128 + 64:j * ncols + (lh + 1) * 128],
                                    vtmp[:, 0:64], vtmp[:, 64:128], ALU.subtract)
                return e.tensor_copy(fscr[:, 1:2],
                                     out_tiles[:, (j + 1) * ncols - 1:(j + 1) * ncols])
            toks[j] = q.on("vector", vrope, after=[t5, tLD])
        return tLastMM, toks, (t5H[14], t5H[15])

    tQuse, tQtok, qt5 = emit_proj_norm_rope(dr["e1wq"], 1024, 8, qro, None, RMS_EPS_L1,
                                            (None, None))
    tKuse, tKtok, kt5 = emit_proj_norm_rope(dr["e1wk"], 256, 2, kro, tQuse, RMS_EPS_L1, qt5)

    tTr = None
    tQst = None
    for a in range(8):
        for half in range(2):
            def trq(e, a=a, half=half):
                for jj in range(8):
                    j = 8 * half + jj
                    ins = e.transpose(psT[:, jj * 128:(jj + 1) * 128],
                                      qro[:, j * 1024 + a * 128:j * 1024 + (a + 1) * 128], idt)
                return ins
            tt = q.on("tensor", trq, after=[tQtok[15], tTr])

            eng = "scalar" if half == 0 else "vector"

            def evq(e, half=half, eng=eng):
                cp = e.copy if eng == "scalar" else e.tensor_copy
                for jj in range(8):
                    cp(orow[:, half * 1024 + jj * 128:half * 1024 + (jj + 1) * 128],
                       psT[:, jj * 128:(jj + 1) * 128])
                return cp(fscr[:, 2:3], orow[:, half * 1024 + 1023:half * 1024 + 1024])
            tTr = q.on(eng, evq, after=[tt, tTr, tQst if half == 0 else None])

        def sgq(e, a=a):
            e.tensor_scalar(orow[:], orow[:], qgc[:, a:a + 1], None, ALU.mult)
            return e.tensor_copy(fscr[:, 3:4], orow[:, S - 1:S])
        tG = q.on("vector", sgq, after=[tTr])

        def stq(e, sem, a=a):
            e.dma_start(out=dr["qT1"][a * 128:(a + 1) * 128, :], in_=orow[:]).then_inc(sem, 16)
        tQst = q.on("sync", stq, after=[tG], ndma=1)

    for a in range(2):
        for half in range(2):
            def trk2(e, a=a, half=half):
                for jj in range(8):
                    j = 8 * half + jj
                    ins = e.transpose(psT[:, jj * 128:(jj + 1) * 128],
                                      kro[:, j * 256 + a * 128:j * 256 + (a + 1) * 128], idt)
                return ins
            tt = q.on("tensor", trk2, after=[tKtok[15], tTr])

            eng = "scalar" if half == 0 else "vector"

            def evk2(e, half=half, eng=eng):
                cp = e.copy if eng == "scalar" else e.tensor_copy
                for jj in range(8):
                    cp(orow[:, half * 1024 + jj * 128:half * 1024 + (jj + 1) * 128],
                       psT[:, jj * 128:(jj + 1) * 128])
                return cp(fscr[:, 2:3], orow[:, half * 1024 + 1023:half * 1024 + 1024])
            tTr = q.on(eng, evk2, after=[tt, tTr, tQst if half == 0 else None])

        def stk(e, sem, a=a):
            e.dma_start(out=dr["kT1"][a * 128:(a + 1) * 128, :], in_=orow[:]).then_inc(sem, 16)
        tQst = q.on("sync", stk, after=[tTr], ndma=1)

    def ldwv(e, sem):
        for i in range(16):
            e.dma_start(out=wb[:, i * 256:(i + 1) * 256],
                        in_=dr["e1wv"][128 * i:128 * (i + 1), :]).then_inc(sem, 16)
    tWv = q.on("gpsimd", ldwv, after=[tKuse], ndma=16)

    tVev = None
    tVst = [None, None]
    for j in range(16):
        def vmm(e, j=j):
            for i in range(16):
                ins = e.matmul(psQ[:, 0:256],
                               xT[:, i * S + j * 128:i * S + j * 128 + 128],
                               wb[:, i * 256:(i + 1) * 256],
                               start=(i == 0), stop=(i == 15))
            return ins
        tV = q.on("tensor", vmm, after=[tWv, tVev, kt5[0] if j == 0 else None])

        def vev(e, j=j):
            e.tensor_copy(vo[:, (j % 2) * 256:(j % 2 + 1) * 256], psQ[:, 0:256])
            return e.tensor_copy(fscr[:, 4:5], vo[:, (j % 2) * 256 + 255:(j % 2) * 256 + 256])
        tVev = q.on("vector", vev, after=[tV, tVst[j % 2]])

        def stv(e, sem, j=j):
            e.dma_start(out=dr["vt1"][128 * j:128 * (j + 1), :],
                        in_=vo[:, (j % 2) * 256:(j % 2 + 1) * 256]).then_inc(sem, 16)
        tVst[j % 2] = q.on("sync", stv, after=[tVev], ndma=1)



def emit_causal_attn(nc, q, pool, dr, G, kvc, kvdiv, qTd, kTd, vtd, agin, agout):
    pool.reset()
    sb = pool.take
    SCALE = float(1.0 / np.sqrt(128.0))
    qT = sb("qT_s", [128, 8 * S], BF16)
    kT = sb("kT_s", [128, (kvc // 128) * S], BF16)
    vtok = sb("vt_s", [128, 16 * kvc], BF16)
    comb = sb("comb", [128, 16 * 1024], BF16)
    qs = sb("qs", [128, S], BF16)
    SS = sb("SS", [128, 2 * S], BF16)
    P = sb("P", [128, 2 * S], BF16)
    PT = sb("PT", [128, 16 * 128], BF16)
    ctr = sb("ctr", [128, 2048], BF16)
    cols = sb("cols", [128, 16], F32)
    fscr = sb("fscr", [128, 8], F32)
    idt = G["idt"]
    maskt = G["maskt"]
    psS = G["psBIG"]
    psT = G["psTr"]
    psA = G["ps3"]

    def ld(e, sem):
        for lh in range(8):
            e.dma_start(out=qT[:, lh * S:(lh + 1) * S],
                        in_=qTd[128 * lh:128 * (lh + 1), :]).then_inc(sem, 16)
        for kk in range(kvc // 128):
            e.dma_start(out=kT[:, kk * S:(kk + 1) * S],
                        in_=kTd[128 * kk:128 * (kk + 1), :]).then_inc(sem, 16)
        for j in range(16):
            e.dma_start(out=vtok[:, j * kvc:(j + 1) * kvc],
                        in_=vtd[128 * j:128 * (j + 1), :]).then_inc(sem, 16)
    tLD = q.on("sync", ld, ndma=24 + kvc // 128)

    tC = None
    tPTev = None
    tV1H = [None] * 16
    tExH = [None] * 16
    tTrpH = [None] * 16
    tRH = [None] * 16
    tWSH = [None] * 16
    tWSPrev = None
    for lh in range(8):
        hOFF = lh * S
        kOFF = (lh // kvdiv) * S if kvc != 1024 else lh * S
        voff = (lh // kvdiv) * 128 if kvc != 1024 else lh * 128

        def qsv(e, hOFF=hOFF):
            e.tensor_scalar_mul(qs[:], qT[:, hOFF:hOFF + S], SCALE)
            return e.tensor_copy(fscr[:, 2:3], qs[:, S - 1:S])
        tQS = q.on("vector", qsv, after=[tLD, tC])

        tV1Head = list(tV1H)

        def flush(j):
            nonlocal tC, tPTev, tWSPrev
            pb = 8 * (j % 2)
            tEx = tExH[j]

            def vrec(e, pb=pb):
                e.reciprocal(cols[:, pb + 4:pb + 5], cols[:, pb + 3:pb + 4])
                return e.tensor_copy(cols[:, pb + 5:pb + 6], cols[:, pb + 4:pb + 5])
            tR = q.on("vector", vrec, after=[tEx])

            def swc(e, pb=pb):
                e.copy(cols[:, pb + 6:pb + 7], cols[:, pb + 4:pb + 5])
                return e.copy(fscr[:, 7:8], cols[:, pb + 6:pb + 7])
            tWS = q.on("scalar", swc, after=[tR, tWSPrev, tC])
            tWSPrev = tWS

            nblk = j + 1
            tPh = None
            for b0 in range(0, nblk, 8):
                bn = min(8, nblk - b0)

                def trp(e, b0=b0, bn=bn, j=j):
                    for i in range(bn):
                        ins = e.transpose(psT[:, i * 128:(i + 1) * 128],
                                          P[:, (j % 2) * S + (b0 + i) * 128:(j % 2) * S + (b0 + i + 1) * 128], idt)
                    return ins
                tt = q.on("tensor", trp, after=[tEx, tPTev])
                tTrpH[j] = tt
                eng = "scalar" if (b0 // 8) % 2 == 0 else "vector"

                def evp(e, b0=b0, bn=bn, eng=eng):
                    cp = e.copy if eng == "scalar" else e.tensor_copy
                    for i in range(bn):
                        cp(PT[:, (b0 + i) * 128:(b0 + i + 1) * 128],
                           psT[:, i * 128:(i + 1) * 128])
                    return cp(fscr[:, 4:5], PT[:, (b0 + bn) * 128 - 1:(b0 + bn) * 128])
                tPTev = q.on(eng, evp, after=[tt, tPTev])
                tPh = tPTev

            def amm(e, j=j, voff=voff, nblk=nblk):
                for jj in range(nblk):
                    ins = e.matmul(psA[:, 0:128],
                                   PT[:, jj * 128:(jj + 1) * 128],
                                   vtok[:, jj * kvc + voff:jj * kvc + voff + 128],
                                   start=(jj == 0), stop=(jj == nblk - 1))
                return ins
            tAm = q.on("tensor", amm, after=[tPh, tC])

            def combv(e, lh=lh, j=j, pb=pb):
                e.tensor_scalar(comb[:, j * 1024 + lh * 128:j * 1024 + (lh + 1) * 128],
                                psA[:, 0:128], cols[:, pb + 6:pb + 7], None, ALU.mult)
                return e.tensor_copy(fscr[:, 2:3],
                                     comb[:, j * 1024 + (lh + 1) * 128 - 1:j * 1024 + (lh + 1) * 128])
            tC = q.on("vector", combv, after=[tAm, tWS])

        for jq in range(16):
            kmax = (jq + 1) * 128
            pb = 8 * (jq % 2)
            if jq < 8:
                buf = psS[:, 1024 * (jq % 2):1024 * (jq % 2) + 1024]
                if jq >= 2:
                    sdep = [tV1H[jq - 2]]
                else:
                    sdep = [tV1Head[15], tV1Head[14]]
            else:
                buf = psS
                sdep = [tV1H[6], tV1H[7]] if jq == 8 else [tV1H[jq - 1]]

            def smm(e, jq=jq, kOFF=kOFF, buf=buf):
                kx = (jq + 1) * 128
                n = 0
                while n * 512 < kx:
                    w = min(512, kx - n * 512)
                    ins = e.matmul(buf[:, n * 512:n * 512 + w],
                                   qs[:, jq * 128:(jq + 1) * 128],
                                   kT[:, kOFF + n * 512:kOFF + n * 512 + w],
                                   start=True, stop=True)
                    n += 1
                return ins
            tS = q.on("tensor", smm, after=[tQS] + sdep)

            def vss(e, jq=jq, kmax=kmax, buf=buf, pb=pb):
                po = (jq % 2) * S
                if jq:
                    e.tensor_copy(SS[:, po:po + jq * 128], buf[:, 0:jq * 128])
                e.scalar_tensor_tensor(SS[:, po + jq * 128:po + kmax], buf[:, jq * 128:kmax],
                                       1.0, maskt, ALU.mult, ALU.add)
                e.reduce_max(cols[:, pb:pb + 1], SS[:, po:po + kmax], axis=AX, negate=True)
                return e.tensor_copy(cols[:, pb + 1:pb + 2], cols[:, pb:pb + 1])
            tV1 = q.on("vector", vss, after=[tS, tExH[jq - 2] if jq >= 2 else None])
            tV1H[jq] = tV1

            def sexp(e, kmax=kmax, jq=jq, pb=pb):
                po = (jq % 2) * S
                e.activation(P[:, po:po + kmax], SS[:, po:po + kmax], AF.Exp,
                             bias=cols[:, pb:pb + 1], scale=1.0,
                             accum_out=cols[:, pb + 3:pb + 4])
                e.copy(fscr[:, 6:7], P[:, po + kmax - 1:po + kmax])
                return e.copy(fscr[:, 6:7], cols[:, pb + 3:pb + 4])
            tEx = q.on("scalar", sexp, after=[tV1, tTrpH[jq - 2] if jq >= 2 else tTrpH[14 + (jq % 2)]])
            tExH[jq] = tEx

            if jq:
                flush(jq - 1)
        flush(15)

    stores = emit_comb_out(q, comb, idt, psT, ctr, fscr, agin, tC)
    tAG = q.cc(lambda e: e.collective_compute(
        "AllGather", ALU.bypass, replica_groups=G["rg8"],
        ins=[agin.opt()], outs=[agout.opt()]),
        after=stores)
    return tAG



def emit_oproj_block(nc, q, pool, dr, G, apply_ln, agout_src, tAGsrc, woT_d, x_d,
                     lnw_d, yf_d, agin, agout):
    pool.reset()
    sb = pool.take
    cbT = sb("cbT_s", [128, 16 * 1024], BF16)
    woT = sb("woT_s", [128, 16 * H], BF16)
    lnw = sb("lnw_s", [128, H], F32)
    xres = sb("xres", [128, 2 * H], F32)
    xo = sb("xo", [128, H], F32)
    xc = sb("xc", [128, H], F32)
    scrap = sb("scrap", [128, H], F32)
    yf = sb("yf_s", [128, 2 * H], F32)
    rn = sb("rn_s", [128, H], BF16)
    xtT = sb("xtT", [128, 16 * 1024], BF16)
    cols = sb("cols", [128, 16], F32)
    fscr = sb("fscr", [128, 8], F32)
    psO = G["psBIG"]
    tr = XTrOut(q, G["idt"], G["psTr"], xtT, fscr, agin)

    def ldc(e, sem):
        for k in range(16):
            e.indirect_dma_start(
                out=cbT[:, k * 1024:(k + 1) * 1024], out_offset=None,
                in_=agout_src,
                in_offset=bass.IndirectOffsetOnAxis(G["idxB"][:, k:k + 1], axis=0),
            ).then_inc(sem, 16)
    tLD = q.on("gpsimd", ldc, after=[tAGsrc], ndma=16)

    def ldw(e, sem):
        for i in range(16):
            e.dma_start(out=woT[:, i * H:(i + 1) * H],
                        in_=woT_d[128 * i:128 * (i + 1), :]).then_inc(sem, 16)
        e.dma_start(out=lnw[:], in_=lnw_d.broadcast_to([128, H])).then_inc(sem, 16)
    tW = q.on("gpsimd", ldw, ndma=17)

    tXO = [None] * 8
    tStore = [None] * 8
    for jt in range(8):
        slot = jt % 2

        def ldx(e, sem, jt=jt, slot=slot):
            e.dma_start(out=xres[:, slot * H:(slot + 1) * H],
                        in_=x_d[128 * jt:128 * (jt + 1), :]).then_inc(sem, 16)
        tX = q.on("sync", ldx, after=[tXO[jt - 2] if jt >= 2 else None], ndma=1)

        def mm(e, jt=jt):
            for n in range(4):
                for i in range(16):
                    ins = e.matmul(psO[:, n * 512:(n + 1) * 512],
                                   cbT[:, i * 1024 + jt * 128:i * 1024 + (jt + 1) * 128],
                                   woT[:, i * H + n * 512:i * H + (n + 1) * 512],
                                   start=(i == 0), stop=(i == 15))
            return ins
        tM = q.on("tensor", mm, after=[tLD, tW, tXO[jt - 1] if jt >= 1 else None])

        def v_xo(e, slot=slot):
            e.scalar_tensor_tensor(xo[:], psO[:], 1.0, xres[:, slot * H:(slot + 1) * H],
                                   ALU.mult, ALU.add, accum_out=cols[:, 0:1])
            return e.tensor_copy(fscr[:, 1:2], cols[:, 0:1])
        tXO[jt] = q.on("vector", v_xo, after=[tM, tX])

        if apply_ln:
            def apply(e, slot=slot):
                e.activation(yf[:, slot * H:(slot + 1) * H], xc[:], AF.Copy,
                             scale=cols[:, 5:6])
                e.activation(scrap[:], yf[:, slot * H:(slot + 1) * H], AF.Square,
                             accum_out=cols[:, 8:9])
                return e.copy(fscr[:, 0:1], cols[:, 8:9])
            tA1 = ln_zigzag(q, tXO[jt], xo, xc, scrap, cols, fscr, 0, H, 1e-5, apply,
                            extra_after=[tStore[jt - 2] if jt >= 2 else None,
                                         tr.done[jt - 2] if jt >= 2 else None])
        else:
            def apply(e, slot=slot):
                e.copy(yf[:, slot * H:(slot + 1) * H], xo[:])
                e.activation(scrap[:], xo[:], AF.Square, accum_out=cols[:, 8:9])
                return e.copy(fscr[:, 0:1], cols[:, 8:9])
            tA1 = q.on("scalar", apply, after=[tXO[jt],
                                              tStore[jt - 2] if jt >= 2 else None,
                                              tr.done[jt - 2] if jt >= 2 else None])

        def v_rw(e):
            e.tensor_scalar(cols[:, 9:10], cols[:, 8:9], 1.0 / H, 1e-5, ALU.mult, ALU.add)
            return e.tensor_copy(fscr[:, 1:2], cols[:, 9:10])
        t6 = q.on("vector", v_rw, after=[tA1])

        def s_rs(e):
            e.activation(cols[:, 10:11], cols[:, 9:10], AF.Sqrt)
            return e.copy(fscr[:, 0:1], cols[:, 10:11])
        t7 = q.on("scalar", s_rs, after=[t6])

        def v_rr(e):
            e.reciprocal(cols[:, 11:12], cols[:, 10:11])
            return e.tensor_copy(fscr[:, 1:2], cols[:, 11:12])
        t8 = q.on("vector", v_rr, after=[t7])

        def s_rn(e, slot=slot):
            e.activation(scrap[:], yf[:, slot * H:(slot + 1) * H], AF.Copy,
                         scale=cols[:, 11:12])
            return e.copy(fscr[:, 0:1], scrap[:, H - 1:H])
        t9 = q.on("scalar", s_rn, after=[t8])

        def v_rn(e):
            e.tensor_tensor(rn[:], scrap[:], lnw[:], ALU.mult)
            return e.tensor_copy(fscr[:, 1:2], rn[:, H - 1:H])
        t10 = q.on("vector", v_rn, after=[t9, tW, tr.done[jt - 1] if jt >= 1 else None])

        def st(e, sem, jt=jt, slot=slot):
            e.dma_start(out=yf_d[128 * jt:128 * (jt + 1), :],
                        in_=yf[:, slot * H:(slot + 1) * H]).then_inc(sem, 16)
        tStore[jt] = q.on("sync", st, after=[t10], ndma=1)

        tr.add_tile(jt, rn[:], t10)

    tSt = tr.store()
    tAG = q.cc(lambda e: e.collective_compute(
        "AllGather", ALU.bypass, replica_groups=G["rg8"],
        ins=[agin.opt()], outs=[agout.opt()]),
        after=[tSt, tStore[6], tStore[7]])
    return tAG



def emit_mla_proj(nc, q, pool, dr, G):
    pool.reset()
    sb = pool.take
    xcol = sb("xcol", [128, 16 * 512], BF16)
    kvdb = sb("kvdb", [128, 16 * 512], BF16)
    qdb = sb("qdb", [128, 16 * 512], BF16)
    kvdT = sb("kvdT", [128, 4 * S], BF16)
    qdT = sb("qdT", [128, 4 * S], BF16)
    kub = sb("kub", [128, 4 * 512], BF16)
    qub = sb("qub", [128, 4 * 512], BF16)
    vub = sb("vub", [128, 4 * 1024], BF16)
    rkb = sb("rkb", [128, 16 * 512], BF16)
    rqb = sb("rqb", [128, 4 * 512], BF16)
    ktot = sb("ktot", [128, 16 * 1024], BF16)
    qtot = sb("qtot", [128, 16 * 1024], BF16)
    vtot = sb("vtot", [128, 2 * 1024], BF16)
    cosm = sb("cosm_s", [128, 16 * 32], F32)
    sinm = sb("sinm_s", [128, 16 * 32], F32)
    rf = sb("rf", [128, 512], F32)
    rs = sb("rs", [128, 64], F32)
    orow = sb("orow", [128, S], BF16)
    fscr = sb("fscr", [128, 8], F32)
    idt = G["idt"]
    psA = G["ps3"]
    psB = G["ps2"][:, 0:512]
    psV = G["psBIG"][:, 0:1024]
    psT = G["psTr"]

    def ld(e, sem):
        e.dma_start(out=cosm[:], in_=dr["mcos"]).then_inc(sem, 16)
        e.dma_start(out=sinm[:], in_=dr["msin"]).then_inc(sem, 16)
    tLD = q.on("sync", ld, ndma=2)

    def ldw(e, sem):
        for i in range(16):
            e.dma_start(out=kvdb[:, i * 512:(i + 1) * 512],
                        in_=dr["kvd"][128 * i:128 * (i + 1), :]).then_inc(sem, 16)
        for i in range(16):
            e.dma_start(out=qdb[:, i * 512:(i + 1) * 512],
                        in_=dr["qd"][128 * i:128 * (i + 1), :]).then_inc(sem, 16)
        for i in range(16):
            e.dma_start(out=rkb[:, i * 512:(i + 1) * 512],
                        in_=dr["rk"][128 * i:128 * (i + 1), :]).then_inc(sem, 16)
        for lt in range(4):
            e.dma_start(out=kub[:, lt * 512:(lt + 1) * 512],
                        in_=dr["ku"][128 * lt:128 * (lt + 1), :]).then_inc(sem, 16)
            e.dma_start(out=qub[:, lt * 512:(lt + 1) * 512],
                        in_=dr["qu"][128 * lt:128 * (lt + 1), :]).then_inc(sem, 16)
            e.dma_start(out=vub[:, lt * 1024:(lt + 1) * 1024],
                        in_=dr["vu"][128 * lt:128 * (lt + 1), :]).then_inc(sem, 16)
            e.dma_start(out=rqb[:, lt * 512:(lt + 1) * 512],
                        in_=dr["rq"][128 * lt:128 * (lt + 1), :]).then_inc(sem, 16)
    tW = q.on("gpsimd", ldw, ndma=64)

    tXc = None
    tEvPrev = None
    tRopeK = None
    tKrLast = None
    for sc in range(4):
        def ldxc(e, sem, sc=sc):
            for i in range(16):
                e.indirect_dma_start(
                    out=xcol[:, i * 512:(i + 1) * 512], out_offset=None,
                    in_=dr["agout_xn1"],
                    in_offset=bass.IndirectOffsetOnAxis(
                        G["idxPair"][:, 2 * i + sc // 2:2 * i + sc // 2 + 1], axis=0),
                    element_offset=512 * (sc % 2),
                ).then_inc(sem, 16)
        tXc = q.on("gpsimd", ldxc, after=[tEvPrev, tKrLast, dr["tAGxn1"] if sc == 0 else None], ndma=16)

        for which, (wsrc, dstT) in enumerate([(kvdb, kvdT), (qdb, qdT)]):
            for lt in range(4):
                def dmm(e, sc=sc, lt=lt, wsrc=wsrc):
                    for i in range(16):
                        ins = e.matmul(psA[:, 0:512],
                                       wsrc[:, i * 512 + lt * 128:i * 512 + (lt + 1) * 128],
                                       xcol[:, i * 512:(i + 1) * 512],
                                       start=(i == 0), stop=(i == 15))
                    return ins
                tM = q.on("tensor", dmm, after=[tXc, tW, tEvPrev])

                def ev(e, sc=sc, lt=lt, dstT=dstT):
                    e.tensor_copy(dstT[:, lt * S + sc * 512:lt * S + (sc + 1) * 512], psA[:, 0:512])
                    return e.tensor_copy(fscr[:, 0:1],
                                         dstT[:, lt * S + (sc + 1) * 512 - 1:lt * S + (sc + 1) * 512])
                tEvPrev = q.on("vector", ev, after=[tM])

        for jj in range(4):
            j = sc * 4 + jj

            def krmm(e, jj=jj):
                for i in range(16):
                    ins = e.matmul(psB[:, 0:512],
                                   xcol[:, i * 512 + jj * 128:i * 512 + (jj + 1) * 128],
                                   rkb[:, i * 512:(i + 1) * 512],
                                   start=(i == 0), stop=(i == 15))
                return ins
            tKr = q.on("tensor", krmm, after=[tXc, tW, tRopeK])
            tKrLast = tKr

            def evkr(e):
                e.tensor_copy(rf[:], psB[:, 0:512])
                return e.tensor_copy(fscr[:, 1:2], rf[:, 511:512])
            tEvK = q.on("vector", evkr, after=[tKr])

            def ropek(e, j=j):
                c = cosm[:, j * 32:(j + 1) * 32]
                s = sinm[:, j * 32:(j + 1) * 32]
                for n in range(8):
                    x1 = rf[:, n * 64:n * 64 + 32]
                    x2 = rf[:, n * 64 + 32:n * 64 + 64]
                    e.tensor_tensor(rs[:, 0:32], x1, c, ALU.mult)
                    e.tensor_tensor(rs[:, 32:64], x2, s, ALU.mult)
                    e.tensor_tensor(ktot[:, j * 1024 + n * 128 + 64:j * 1024 + n * 128 + 96],
                                    rs[:, 0:32], rs[:, 32:64], ALU.subtract)
                    e.tensor_tensor(rs[:, 0:32], x2, c, ALU.mult)
                    e.tensor_tensor(rs[:, 32:64], x1, s, ALU.mult)
                    e.tensor_tensor(ktot[:, j * 1024 + n * 128 + 96:j * 1024 + (n + 1) * 128],
                                    rs[:, 0:32], rs[:, 32:64], ALU.add)
                return e.tensor_copy(fscr[:, 1:2], ktot[:, (j + 1) * 1024 - 1:(j + 1) * 1024])
            tRopeK = q.on("vector", ropek, after=[tEvK, tLD])

    tVst = [None, None]
    tPrevA = tRopeK
    tPrevB = tRopeK
    tPrevV = None
    for j in range(16):
        def kpmm(e, j=j):
            for lt in range(4):
                ins = e.matmul(psA[:, 0:512],
                               kvdT[:, lt * S + j * 128:lt * S + (j + 1) * 128],
                               kub[:, lt * 512:(lt + 1) * 512],
                               start=(lt == 0), stop=(lt == 3))
            return ins
        tKp = q.on("tensor", kpmm, after=[tEvPrev, tPrevA])

        def evkp(e, j=j):
            for n in range(8):
                e.tensor_copy(ktot[:, j * 1024 + n * 128:j * 1024 + n * 128 + 64],
                              psA[:, n * 64:(n + 1) * 64])
            return e.tensor_copy(fscr[:, 2:3], ktot[:, j * 1024 + 7 * 128 + 63:j * 1024 + 7 * 128 + 64])
        tPrevA = q.on("vector", evkp, after=[tKp])

        def qpmm(e, j=j):
            for lt in range(4):
                ins = e.matmul(psB[:, 0:512],
                               qdT[:, lt * S + j * 128:lt * S + (j + 1) * 128],
                               qub[:, lt * 512:(lt + 1) * 512],
                               start=(lt == 0), stop=(lt == 3))
            return ins
        tQp = q.on("tensor", qpmm, after=[tPrevB])

        def evqp(e, j=j):
            for n in range(8):
                e.tensor_copy(qtot[:, j * 1024 + n * 128:j * 1024 + n * 128 + 64],
                              psB[:, n * 64:(n + 1) * 64])
            return e.tensor_copy(fscr[:, 3:4], qtot[:, j * 1024 + 7 * 128 + 63:j * 1024 + 7 * 128 + 64])
        tPrevB = q.on("vector", evqp, after=[tQp])

        def qrmm(e, j=j):
            for lt in range(4):
                ins = e.matmul(psB[:, 0:512],
                               qdT[:, lt * S + j * 128:lt * S + (j + 1) * 128],
                               rqb[:, lt * 512:(lt + 1) * 512],
                               start=(lt == 0), stop=(lt == 3))
            return ins
        tQr = q.on("tensor", qrmm, after=[tPrevB])

        def evqr(e):
            e.tensor_copy(rf[:], psB[:, 0:512])
            return e.tensor_copy(fscr[:, 1:2], rf[:, 511:512])
        tEvQr = q.on("vector", evqr, after=[tQr])

        def ropeq(e, j=j):
            c = cosm[:, j * 32:(j + 1) * 32]
            s = sinm[:, j * 32:(j + 1) * 32]
            for n in range(8):
                x1 = rf[:, n * 64:n * 64 + 32]
                x2 = rf[:, n * 64 + 32:n * 64 + 64]
                e.tensor_tensor(rs[:, 0:32], x1, c, ALU.mult)
                e.tensor_tensor(rs[:, 32:64], x2, s, ALU.mult)
                e.tensor_tensor(qtot[:, j * 1024 + n * 128 + 64:j * 1024 + n * 128 + 96],
                                rs[:, 0:32], rs[:, 32:64], ALU.subtract)
                e.tensor_tensor(rs[:, 0:32], x2, c, ALU.mult)
                e.tensor_tensor(rs[:, 32:64], x1, s, ALU.mult)
                e.tensor_tensor(qtot[:, j * 1024 + n * 128 + 96:j * 1024 + (n + 1) * 128],
                                rs[:, 0:32], rs[:, 32:64], ALU.add)
            return e.tensor_copy(fscr[:, 1:2], qtot[:, (j + 1) * 1024 - 1:(j + 1) * 1024])
        tPrevB = q.on("vector", ropeq, after=[tEvQr, tLD])

        def vmm(e, j=j):
            for n in range(2):
                for lt in range(4):
                    ins = e.matmul(psV[:, n * 512:(n + 1) * 512],
                                   kvdT[:, lt * S + j * 128:lt * S + (j + 1) * 128],
                                   vub[:, lt * 1024 + n * 512:lt * 1024 + (n + 1) * 512],
                                   start=(lt == 0), stop=(lt == 3))
            return ins
        tV = q.on("tensor", vmm, after=[tEvPrev, tPrevV])

        def vev(e, j=j):
            e.tensor_copy(vtot[:, (j % 2) * 1024:(j % 2 + 1) * 1024], psV[:])
            return e.tensor_copy(fscr[:, 4:5], vtot[:, (j % 2) * 1024 + 1023:(j % 2) * 1024 + 1024])
        tPrevV = q.on("vector", vev, after=[tV, tVst[j % 2]])

        def stv(e, sem, j=j):
            e.dma_start(out=dr["vt2"][128 * j:128 * (j + 1), :],
                        in_=vtot[:, (j % 2) * 1024:(j % 2 + 1) * 1024]).then_inc(sem, 16)
        tVst[j % 2] = q.on("sync", stv, after=[tPrevV], ndma=1)

    tTr = None
    tQst = None
    for src_t, dst_d, tok0 in ((qtot, dr["qT2"], tPrevB), (ktot, dr["kT2"], tPrevA)):
        for a in range(8):
            for half in range(2):
                def trq(e, a=a, half=half, src_t=src_t):
                    for jj in range(8):
                        j = 8 * half + jj
                        ins = e.transpose(psT[:, jj * 128:(jj + 1) * 128],
                                          src_t[:, j * 1024 + a * 128:j * 1024 + (a + 1) * 128], idt)
                    return ins
                tt = q.on("tensor", trq, after=[tok0, tRopeK, tTr])

                eng = "scalar" if half == 0 else "vector"

                def evq(e, half=half, eng=eng):
                    cp = e.copy if eng == "scalar" else e.tensor_copy
                    for jj in range(8):
                        cp(orow[:, half * 1024 + jj * 128:half * 1024 + (jj + 1) * 128],
                           psT[:, jj * 128:(jj + 1) * 128])
                    return cp(fscr[:, 2:3], orow[:, half * 1024 + 1023:half * 1024 + 1024])
                tTr = q.on(eng, evq, after=[tt, tTr, tQst if half == 0 else None])

            def stq(e, sem, a=a, dst_d=dst_d):
                e.dma_start(out=dst_d[a * 128:(a + 1) * 128, :], in_=orow[:]).then_inc(sem, 16)
            tQst = q.on("sync", stq, after=[tTr], ndma=1)



def emit_moe(nc, q, pool, dr, G):
    pool.reset()
    sb = pool.take
    xcb = sb("xcb", [128, 16 * 512], BF16)
    wgb = sb("wgb", [128, 16 * 1024], BF16)
    wub = sb("wub", [128, 16 * 1024], BF16)
    wdb = sb("wdb", [128, 8 * H], BF16)
    rwb = sb("rwb", [128, 16 * 7], BF16)
    rbb = sb("rbb", [128, 7], F32)
    wselc = sb("wselc", [128, 2], F32)
    hT = sb("hT", [128, 8 * 512], BF16)
    sgf = sb("sgf", [128, 512], F32)
    t1f = sb("t1f", [128, 512], F32)
    lgAll = sb("lgAll", [128, 448], F32)
    prAll = sb("prAll", [128, 448], F32)
    m1rep = sb("m1rep", [128, 448], F32)
    eqw = sb("eqw", [128, 448], F32)
    pmAll = sb("pmAll", [128, 448], F32)
    m1 = sb("m1", [128, 64], F32)
    m2 = sb("m2", [128, 64], F32)
    dc = sb("dc", [128, 64], F32)
    sgc = sb("sgc", [128, 64], F32)
    u1 = sb("u1", [128, 64], F32)
    nn = sb("nn", [128, 64], F32)
    s12 = sb("s12", [128, 64], F32)
    invc = sb("invc", [128, 64], F32)
    wAll = sb("wAll", [128, 64], F32)
    dscr = sb("dscr", [128, 512], F32)
    obuf = sb("obuf", [128, 2 * H], BF16)
    fscr = sb("fscr", [128, 64], F32)
    psR = G["ps2"][:, 0:512]
    psG = G["ps2"][:, 512:1024]
    psU = G["ps3"]
    psD = G["psBIG"]
    xsrc = dr["agout_xn2"]

    def xld(e, sem, tc):
        r, u = tc // 2, tc % 2
        for i in range(16):
            e.dma_start(out=xcb[:, i * 512:(i + 1) * 512],
                        in_=xsrc[2048 * r + 128 * i:2048 * r + 128 * (i + 1),
                                 512 * u:512 * (u + 1)]).then_inc(sem, 16)

    def ld(e, sem):
        e.dma_start(out=rbb[:], in_=dr["mrb"].broadcast_to([128, 7])).then_inc(sem, 16)
        e.dma_start(out=wselc[:], in_=dr["mwsel"].broadcast_to([128, 2])).then_inc(sem, 16)
    tLD = q.on("sync", ld, ndma=2)

    def ldw(e, sem):
        for i in range(16):
            e.dma_start(out=rwb[:, i * 7:(i + 1) * 7],
                        in_=dr["mrw"][128 * i:128 * (i + 1), :]).then_inc(sem, 16)
        for i in range(16):
            e.dma_start(out=wgb[:, i * 1024:(i + 1) * 1024],
                        in_=dr["mwg"][128 * i:128 * (i + 1), :]).then_inc(sem, 16)
        for i in range(16):
            e.dma_start(out=wub[:, i * 1024:(i + 1) * 1024],
                        in_=dr["mwu"][128 * i:128 * (i + 1), :]).then_inc(sem, 16)
        for i in range(8):
            e.dma_start(out=wdb[:, i * H:(i + 1) * H],
                        in_=dr["mwd"][128 * i:128 * (i + 1), :]).then_inc(sem, 16)
    tW = q.on("gpsimd", ldw, ndma=56)

    tXc = None
    tREv = None
    for tc in range(16):
        tXc = q.on("sync", lambda e, sem, tc=tc: xld(e, sem, tc),
                   after=[tREv, dr["tAGxn2"] if tc == 0 else None], ndma=16)

        for u in range(4):
            tt = tc * 4 + u

            def rmm(e, u=u):
                for i in range(16):
                    ins = e.matmul(psR[:, (u % 2) * 128:(u % 2) * 128 + 7],
                                   xcb[:, i * 512 + u * 128:i * 512 + (u + 1) * 128],
                                   rwb[:, i * 7:(i + 1) * 7],
                                   start=(i == 0), stop=(i == 15))
                return ins
            tRm = q.on("tensor", rmm, after=[tXc, tW, tREv])

            def rev(e, tt=tt, u=u):
                e.scalar_tensor_tensor(
                    lgAll[:].rearrange("p (s g) -> p s g", s=7)[:, :, tt:tt + 1],
                    psR[:, (u % 2) * 128:(u % 2) * 128 + 7], 1.0, rbb[:],
                    ALU.mult, ALU.add)
                return e.tensor_copy(fscr[:, 0:1], lgAll[:, 384 + tt:385 + tt])
            tREv = q.on("vector", rev, after=[tRm])

    def sprob(e):
        e.activation(prAll[:], lgAll[:], AF.Sigmoid)
        return e.copy(fscr[:, 0:32], prAll[:, 416:448])
    tS0 = q.on("scalar", sprob, after=[tREv])

    def vm1(e):
        e.reduce_max(m1[:], prAll[:].rearrange("p (s g) -> p g s", g=64), axis=AX)
        return e.tensor_copy(fscr[:, 0:64], m1[:])
    tM1 = q.on("vector", vm1, after=[tS0])

    def repm1(e, sem):
        e.dma_start(out=m1rep[:].rearrange("p (s g) -> p s g", s=7),
                    in_=m1[:].unsqueeze(1).broadcast_to([128, 7, 64])).then_inc(sem, 16)
    tRep = q.on("gpsimd", repm1, after=[tM1], ndma=1)

    def vm2(e):
        e.tensor_tensor(eqw[:], prAll[:], m1rep[:], ALU.is_equal)
        e.scalar_tensor_tensor(pmAll[:], eqw[:], -1e30, prAll[:], ALU.mult, ALU.add)
        e.reduce_max(m2[:], pmAll[:].rearrange("p (s g) -> p g s", g=64), axis=AX)
        return e.tensor_copy(fscr[:, 0:64], m2[:])
    tM2 = q.on("vector", vm2, after=[tRep])

    def sm2(e):
        e.copy(dscr[:, 0:64], m2[:])
        return e.copy(fscr[:, 0:64], dscr[:, 0:64])
    tM2h = q.on("scalar", sm2, after=[tM2])

    def vdc(e):
        e.scalar_tensor_tensor(dc[:], dscr[:, 0:64], -0.9999999, prAll[:, 0:64],
                               ALU.mult, ALU.add)
        e.tensor_tensor(s12[:], m1[:], dscr[:, 0:64], ALU.add)
        return e.tensor_copy(fscr[:, 0:64], dc[:])
    tDc = q.on("vector", vdc, after=[tM2h])

    def ssg(e):
        e.activation(sgc[:], dc[:], AF.Sign)
        return e.copy(fscr[:, 0:64], sgc[:])
    tSg = q.on("scalar", ssg, after=[tDc])

    def vwf(e):
        e.reciprocal(invc[:], s12[:])
        e.tensor_tensor(u1[:], sgc[:], prAll[:, 0:64], ALU.mult)
        e.tensor_copy(dscr[:], dscr[:])
        e.tensor_tensor(nn[:], u1[:], prAll[:, 0:64], ALU.add)
        e.tensor_copy(dscr[:], dscr[:])
        e.tensor_tensor(nn[:], nn[:], invc[:], ALU.mult)
        e.tensor_copy(dscr[:], dscr[:])
        e.tensor_scalar(wAll[:], nn[:], wselc[:, 1:2], wselc[:, 0:1], ALU.mult, ALU.add)
        return e.tensor_copy(fscr[:, 0:64], wAll[:])
    tWfin = q.on("vector", vwf, after=[tSg])

    tDevPrev = [None] * 2
    tDevLast = None
    tOst = [None] * 2
    tGU = None
    tHt = None
    for tc in range(16):
        tXc = q.on("sync", lambda e, sem, tc=tc: xld(e, sem, tc),
                   after=[tGU, tWfin if tc == 0 else None], ndma=16)

        tHtPar = [None, None]
        for it in range(8):
            pg = psG if it % 2 == 0 else psR

            def gmm(e, it=it, pg=pg):
                for i in range(16):
                    ins = e.matmul(pg[:, 0:512],
                                   wgb[:, i * 1024 + it * 128:i * 1024 + (it + 1) * 128],
                                   xcb[:, i * 512:(i + 1) * 512],
                                   start=(i == 0), stop=(i == 15))
                return ins
            tGm = q.on("tensor", gmm, after=[tXc, tW, tHtPar[it % 2]])

            def umm(e, it=it):
                for i in range(16):
                    ins = e.matmul(psU[:, 0:512],
                                   wub[:, i * 1024 + it * 128:i * 1024 + (it + 1) * 128],
                                   xcb[:, i * 512:(i + 1) * 512],
                                   start=(i == 0), stop=(i == 15))
                return ins
            tUm = q.on("tensor", umm, after=[tHt])
            tGU = tUm

            def ssig(e, pg=pg):
                e.activation(sgf[:], pg[:, 0:512], AF.Sigmoid)
                return e.copy(fscr[:, 0:1], sgf[:, 511:512])
            tSgs = q.on("scalar", ssig, after=[tGm])

            def vht(e, it=it, pg=pg):
                e.tensor_tensor(t1f[:], sgf[:], pg[:, 0:512], ALU.mult)
                e.scalar_tensor_tensor(hT[:, it * 512:(it + 1) * 512], t1f[:], 0.125,
                                       psU[:, 0:512], ALU.mult, ALU.mult)
                return e.tensor_copy(fscr[:, 0:1], hT[:, (it + 1) * 512 - 1:(it + 1) * 512])
            tHt = q.on("vector", vht, after=[tSgs, tUm])
            tHtPar[it % 2] = tHt

        for u in range(4):
            slot = u % 2
            tt = tc * 4 + u

            def dmm(e, u=u):
                for n in range(4):
                    for i in range(8):
                        ins = e.matmul(psD[:, n * 512:(n + 1) * 512],
                                       hT[:, i * 512 + u * 128:i * 512 + (u + 1) * 128],
                                       wdb[:, i * H + n * 512:i * H + (n + 1) * 512],
                                       start=(i == 0), stop=(i == 7))
                return ins
            tDm = q.on("tensor", dmm, after=[tHt, tDevLast])

            def dev(e, tt=tt, slot=slot):
                e.tensor_scalar(obuf[:, slot * H:(slot + 1) * H], psD[:], wAll[:, tt:tt + 1],
                                None, ALU.mult)
                return e.tensor_copy(fscr[:, 0:1], obuf[:, slot * H + H - 1:slot * H + H])
            tDevPrev[slot] = q.on("vector", dev, after=[tDm, tWfin, tOst[slot]])
            tDevLast = tDevPrev[slot]

            def ost(e, sem, tc=tc, u=u, slot=slot):
                e.dma_start(out=dr["part2"][tc * 512 + u * 128:tc * 512 + (u + 1) * 128, :],
                            in_=obuf[:, slot * H:(slot + 1) * H]).then_inc(sem, 16)
            tOst[slot] = q.on("sync", ost, after=[tDevPrev[slot]], ndma=1)

    tRS = q.cc(lambda e: e.collective_compute(
        "ReduceScatter", ALU.add, replica_groups=G["rg8"],
        ins=[dr["part2"].opt()], outs=[dr["rs2"].opt()]),
        after=[tOst[0], tOst[1]])
    return tRS



def emit_final(nc, q, pool, dr, G):
    pool.reset()
    sb = pool.take
    pt = sb("pt", [128, 2 * H], BF16)
    xres = sb("xres", [128, 2 * H], F32)
    xres2 = sb("xres2", [128, 2 * H], F32)
    acc = sb("acc", [128, H], F32)
    xo = sb("xo", [128, H], F32)
    xc = sb("xc", [128, H], F32)
    scrap = sb("scrap", [128, H], F32)
    yf = sb("yf_s", [128, 2 * H], F32)
    cols = sb("cols", [128, 8], F32)
    fscr = sb("fscr", [128, 8], F32)

    tSum = [None] * 8
    tStore = [None] * 8
    for jt in range(8):
        slot = jt % 2

        def ld(e, sem, jt=jt, slot=slot):
            e.dma_start(out=pt[:, slot * H:(slot + 1) * H],
                        in_=dr["rs2"][128 * jt:128 * (jt + 1), :]).then_inc(sem, 16)
            e.dma_start(out=xres[:, slot * H:(slot + 1) * H],
                        in_=dr["hf"][128 * jt:128 * (jt + 1), :]).then_inc(sem, 16)
            e.dma_start(out=xres2[:, slot * H:(slot + 1) * H],
                        in_=dr["x3f"][128 * jt:128 * (jt + 1), :]).then_inc(sem, 16)
        tL = q.on("sync", ld, after=[tSum[jt - 2] if jt >= 2 else None,
                                     dr["tRS2"] if jt == 0 else None], ndma=3)

        def vsum(e, slot=slot):
            e.scalar_tensor_tensor(acc[:], pt[:, slot * H:(slot + 1) * H], 1.0,
                                   xres2[:, slot * H:(slot + 1) * H], ALU.mult, ALU.add)
            e.scalar_tensor_tensor(xo[:], acc[:], 1.0, xres[:, slot * H:(slot + 1) * H],
                                   ALU.mult, ALU.add, accum_out=cols[:, 0:1])
            return e.tensor_copy(fscr[:, 1:2], cols[:, 0:1])
        tSum[jt] = q.on("vector", vsum, after=[tL])

        def apply(e, jt=jt, slot=slot):
            e.activation(yf[:, slot * H:(slot + 1) * H], xc[:], AF.Copy, scale=cols[:, 5:6])
            return e.copy(fscr[:, 0:1], yf[:, slot * H + H - 1:slot * H + H])
        tAPP = ln_zigzag(q, tSum[jt], xo, xc, scrap, cols, fscr, 0, H, 1e-5, apply,
                         extra_after=[tStore[jt - 2] if jt >= 2 else None])

        def st(e, sem, jt=jt, slot=slot):
            e.dma_start(out=dr["out"][128 * jt:128 * (jt + 1), :],
                        in_=yf[:, slot * H:(slot + 1) * H]).then_inc(sem, 16)
        tStore[jt] = q.on("sync", st, after=[tAPP], ndma=1)

    return [tStore[6], tStore[7]]



def build_fused(dbg=()):
    nc = bass.Bass("TRN2", target_bir_lowering=False, debug=False)
    dr = {}

    def din(name, shape, dt):
        dr[name] = nc.dram_tensor(name, shape, dt, kind="ExternalInput").ap()

    def dint(name, shape, dt):
        kind = "ExternalOutput" if name in dbg else "Internal"
        kw = {"addr_space": "Shared"} if (name.startswith("agout") and name not in dbg) else {}
        dr[name] = nc.dram_tensor(name, shape, dt, kind=kind, **kw).ap()

    din("a1xT", [H, S], F32)
    din("a1wq", [H, 1024], F32)
    din("a1wk", [H, 1024], F32)
    din("a1wv", [H, 1024], F32)
    din("l0cos", [128, 4 * S], F32)
    din("l0sin", [128, 4 * S], F32)
    din("gate", [1, 8], F32)
    din("identI", [128, 128], BF16)
    din("maskI", [128, 128], F32)
    din("idxBI", [128, 16], U32)
    din("idxPairI", [128, 32], U32)
    din("bwoT", [H, H], F32)
    din("bx", [1024, H], F32)
    din("tg", [1024, 2048], U8)
    din("tu", [1024, 2048], U8)
    din("td", [1024, 2048], U8)
    din("gsT", [64, 2048], F32)
    din("usT", [64, 2048], F32)
    din("dsT", [64, 2048], F32)
    din("e1wq", [H, 1024], F32)
    din("e1wk", [H, 256], F32)
    din("e1wv", [H, 256], F32)
    din("qg", [1, 8], F32)
    din("l1cos", [128, 16 * 64], F32)
    din("l1sin", [128, 16 * 64], F32)
    din("fwoT", [H, H], F32)
    din("ln1w", [1, H], F32)
    din("kvd", [H, 512], F32)
    din("qd", [H, 512], F32)
    din("ku", [512, 512], F32)
    din("qu", [512, 512], F32)
    din("vu", [512, 1024], F32)
    din("rk", [H, 512], F32)
    din("rq", [512, 512], F32)
    din("mcos", [128, 16 * 32], F32)
    din("msin", [128, 16 * 32], F32)
    din("hwoT", [H, H], F32)
    din("ln2w", [1, H], F32)
    din("mwg", [H, 1024], F32)
    din("mwu", [H, 1024], F32)
    din("mwd", [1024, H], F32)
    din("mrw", [H, 7], F32)
    din("mrb", [1, 7], F32)
    din("mwsel", [1, 2], F32)

    dint("qT0", [1024, S], BF16)
    dint("kT0", [1024, S], BF16)
    dint("vt0", [S, 1024], BF16)
    dint("agin_comb", [2048, 1024], BF16)
    dint("agout_comb", [16384, 1024], BF16)
    dint("x1f", [1024, H], F32)
    dint("agin_x1", [2048, 1024], BF16)
    dint("agout_x1", [16384, 1024], BF16)
    dint("inter", [2048, TT_], BF16)
    dint("part1", [TT_, H], BF16)
    dint("rs1", [1024, H], BF16)
    dint("x2f", [1024, H], F32)
    dint("agin_x2", [2048, 1024], BF16)
    dint("agout_x2", [16384, 1024], BF16)
    dint("qT1", [1024, S], BF16)
    dint("kT1", [256, S], BF16)
    dint("vt1", [S, 256], BF16)
    dint("agin_y1", [2048, 1024], BF16)
    dint("agout_y1", [16384, 1024], BF16)
    dint("x3f", [1024, H], F32)
    dint("agin_n1", [2048, 1024], BF16)
    dint("agout_xn1", [16384, 1024], BF16)
    dint("qT2", [1024, S], BF16)
    dint("kT2", [1024, S], BF16)
    dint("vt2", [S, 1024], BF16)
    dint("agin_y2", [2048, 1024], BF16)
    dint("agout_y2", [16384, 1024], BF16)
    dint("hf", [1024, H], F32)
    dint("agin_n2", [2048, 1024], BF16)
    dint("agout_xn2", [16384, 1024], BF16)
    dint("part2", [TT_, H], BF16)
    dint("rs2", [1024, H], BF16)
    dr["out"] = nc.dram_tensor("out", [1024, H], F32, kind="ExternalOutput").ap()

    with ExitStack() as es:
        idt = es.enter_context(nc.sbuf_tensor("idt_c", [128, 128], BF16))
        maskt = es.enter_context(nc.sbuf_tensor("maskt_c", [128, 128], F32))
        idxB = es.enter_context(nc.sbuf_tensor("idxB_c", [128, 16], U32))
        idxPair = es.enter_context(nc.sbuf_tensor("idxPair_c", [128, 32], U32))
        psBIG = es.enter_context(nc.psum_tensor("psBIG", [128, 2048], F32))
        ps2 = es.enter_context(nc.psum_tensor("ps2", [128, 1024], F32))
        ps3 = es.enter_context(nc.psum_tensor("ps3", [128, 512], F32))
        psTr = es.enter_context(nc.psum_tensor("psTr", [128, 1024], BF16))
        pool = Pool(nc, es, 211456)
        block = es.enter_context(nc.Block())
        q = Seq(es, nc)

        G = {
            "idt": idt[:], "maskt": maskt[:], "idxB": idxB[:], "idxPair": idxPair[:],
            "psBIG": psBIG[:], "ps2": ps2[:], "ps3": ps3[:], "psTr": psTr[:],
            "rg8": [list(range(NCORES))],
        }

        def ldconst(e, sem):
            e.dma_start(out=idt[:], in_=dr["identI"]).then_inc(sem, 16)
            e.dma_start(out=maskt[:], in_=dr["maskI"]).then_inc(sem, 16)
            e.dma_start(out=idxB[:], in_=dr["idxBI"]).then_inc(sem, 16)
            e.dma_start(out=idxPair[:], in_=dr["idxPairI"]).then_inc(sem, 16)
        q.on("sync", ldconst, ndma=4)
        q.new_stage()

        emit_l0_qkv(nc, q, pool, dr, G)
        q.new_stage()
        dr["tAGcomb"] = emit_l0_attn(nc, q, pool, dr, G)
        q.new_stage()
        dr["tAGx1"] = emit_oproj_ln(nc, q, pool, dr, G)
        q.new_stage()
        dr["tRS1"] = emit_l0_ffn(nc, q, pool, dr, G)
        q.new_stage()
        dr["tAGx2"] = emit_postffn_ln(nc, q, pool, dr, G)
        q.new_stage()
        emit_l1_qkv(nc, q, pool, dr, G)
        q.new_stage()
        tAGy1 = emit_causal_attn(nc, q, pool, dr, G, 256, 4,
                                 dr["qT1"], dr["kT1"], dr["vt1"],
                                 dr["agin_y1"], dr["agout_y1"])
        q.new_stage()
        dr["tAGxn1"] = emit_oproj_block(nc, q, pool, dr, G, True,
                                        dr["agout_y1"], tAGy1, dr["fwoT"], dr["x2f"],
                                        dr["ln1w"], dr["x3f"], dr["agin_n1"], dr["agout_xn1"])
        q.new_stage()
        emit_mla_proj(nc, q, pool, dr, G)
        q.new_stage()
        tAGy2 = emit_causal_attn(nc, q, pool, dr, G, 1024, 1,
                                 dr["qT2"], dr["kT2"], dr["vt2"],
                                 dr["agin_y2"], dr["agout_y2"])
        q.new_stage()
        tAGxn2 = emit_oproj_block(nc, q, pool, dr, G, False,
                                  dr["agout_y2"], tAGy2, dr["hwoT"], dr["x3f"],
                                  dr["ln2w"], dr["hf"], dr["agin_n2"], dr["agout_xn2"])
        dr["tAGxn2"] = tAGxn2
        q.new_stage()
        dr["tRS2"] = emit_moe(nc, q, pool, dr, G)
        q.new_stage()
        stores = emit_final(nc, q, pool, dr, G)

        q.emit(block)
        q.finish(block, stores)
    return nc



def _run(nc, in_maps):
    return run_bass_kernel_spmd(nc, in_maps, core_ids=list(range(NCORES)))


def _stage_inputs(inputs):
    x = np.asarray(inputs["L0_attn_hidden_states"], np.float32)
    wqT0 = np.ascontiguousarray(np.asarray(inputs["L0_attn_Wq"], np.float32).T)
    wkT0 = np.ascontiguousarray(np.asarray(inputs["L0_attn_Wk"], np.float32).T)
    wvT0 = np.ascontiguousarray(np.asarray(inputs["L0_attn_Wv"], np.float32).T)
    woT0 = np.ascontiguousarray(np.asarray(inputs["L0_attn_Wo"], np.float32).T)
    gateq = np.asarray(inputs["L0_attn_gate"], np.float32).reshape(1, 16)
    cos0, sin0 = _l0_costab()
    cos0T = np.ascontiguousarray(cos0.T)
    sin0T = np.ascontiguousarray(sin0.T)
    pn = _pi_n()
    gp = np.asarray(inputs["L0_ffn_gate_W_packed"])
    up = np.asarray(inputs["L0_ffn_up_W_packed"])
    dp = np.asarray(inputs["L0_ffn_down_W_packed"])
    gs = np.asarray(inputs["L0_ffn_gate_scales"], np.float32)
    us = np.asarray(inputs["L0_ffn_up_scales"], np.float32)
    ds = np.asarray(inputs["L0_ffn_down_scales"], np.float32)
    bg = np.ascontiguousarray(gp.view(np.uint8).reshape(16384, 1024, 4)[:, :, 0][pn].T)
    bu = np.ascontiguousarray(up.view(np.uint8).reshape(16384, 1024, 4)[:, :, 0][pn].T)
    bd = np.ascontiguousarray(dp.view(np.uint8).reshape(2048, 8192, 4)[:, :, 0].T)
    gsT = np.ascontiguousarray(gs[pn].T)
    usT = np.ascontiguousarray(us[pn].T)
    dsT = np.ascontiguousarray(ds.T)
    wqT1 = np.ascontiguousarray(np.asarray(inputs["L1_attn_Wq"], np.float32).T)
    wkT1 = np.ascontiguousarray(np.asarray(inputs["L1_attn_Wk"], np.float32).T)
    wvT1 = np.ascontiguousarray(np.asarray(inputs["L1_attn_Wv"], np.float32).T)
    woT1 = np.ascontiguousarray(np.asarray(inputs["L1_attn_Wo"], np.float32).T)
    qg = np.asarray(inputs["L1_attn_qg"], np.float32).reshape(16)
    cos1, sin1 = _l1_costab()
    cos1t, sin1t = _tiles128(cos1), _tiles128(sin1)
    kvd = np.asarray(inputs["L1_ffn_kv_d_w"], np.float32)
    qd = np.asarray(inputs["L1_ffn_q_d_w"], np.float32)
    ku = np.asarray(inputs["L1_ffn_k_u_w"], np.float32)
    qu = np.asarray(inputs["L1_ffn_q_u_w"], np.float32)
    vu = np.asarray(inputs["L1_ffn_v_u_w"], np.float32)
    rk = np.asarray(inputs["L1_ffn_rope_k_w"], np.float32)
    rq = np.asarray(inputs["L1_ffn_rope_q_w"], np.float32)
    cosm, sinm = _mla_costab()
    cosmt, sinmt = _tiles128(cosm), _tiles128(sinm)
    ow = np.ascontiguousarray(np.asarray(inputs["L1_ffn_o_w"], np.float32))
    ln1w = np.ascontiguousarray(np.asarray(inputs["L1_ffn_ln1_w"], np.float32).reshape(1, H))
    ln2w = np.ascontiguousarray(np.asarray(inputs["L1_ffn_ln2_w"], np.float32).reshape(1, H))
    shg = np.asarray(inputs["L1_ffn_sh_gate"], np.float32)
    shu = np.asarray(inputs["L1_ffn_sh_up"], np.float32)
    shd = np.asarray(inputs["L1_ffn_sh_down"], np.float32)
    rg_ = np.asarray(inputs["L1_ffn_r_gate"], np.float32)
    ru_ = np.asarray(inputs["L1_ffn_r_up"], np.float32)
    rd_ = np.asarray(inputs["L1_ffn_r_down"], np.float32)
    rw = np.asarray(inputs["L1_ffn_router_w"], np.float32)
    rb = np.asarray(inputs["L1_ffn_router_b"], np.float32)
    bwoT = np.ascontiguousarray(woT0[np.concatenate([_l0_colmap(0), _l0_colmap(1)])])

    in_maps = []
    ar = np.arange(128, dtype=np.uint32)
    for c in range(NCORES):
        b, p = c // 2, c % 2
        cm = _l0_colmap(p)
        gsel = gateq[:, [4 * p + i for i in range(4)] + [8 + 4 * p + i for i in range(4)]]
        idxB = np.zeros((128, 16), np.uint32)
        for pk in range(2):
            for ci in range(8):
                idxB[:, pk * 8 + ci] = 2048 * (2 * b + pk) + 1024 * p + 128 * ci + ar
        idxPair = np.zeros((128, 32), np.uint32)
        for i in range(16):
            for blk in range(2):
                idxPair[:, 2 * i + blk] = 2048 * (2 * b + blk) + 128 * i + ar
        if c == 0:
            mg, mu, md = shg, shu, shd
            perm = list(range(7))
            wsel = np.array([[1.0, 0.0]], np.float32)
        else:
            mg, mu, md = rg_[c - 1], ru_[c - 1], rd_[c - 1]
            perm = [c - 1] + [i for i in range(7) if i != c - 1]
            wsel = np.array([[0.0, 0.5]], np.float32)
        cs = slice(512 * p, 512 * (p + 1))
        in_maps.append({
            "a1xT": np.ascontiguousarray(x[b].T),
            "a1wq": np.ascontiguousarray(wqT0[:, cm]),
            "a1wk": np.ascontiguousarray(wkT0[:, cm]),
            "a1wv": np.ascontiguousarray(wvT0[:, cm]),
            "l0cos": _tiles128(cos0T[512 * p:512 * (p + 1)]),
            "l0sin": _tiles128(sin0T[512 * p:512 * (p + 1)]),
            "gate": np.ascontiguousarray(gsel),
            "identI": IDENT,
            "maskI": CAUSAL_MASK,
            "idxBI": idxB,
            "idxPairI": idxPair,
            "bwoT": bwoT,
            "bx": np.ascontiguousarray(x[b][1024 * p:1024 * (p + 1)]),
            "tg": np.ascontiguousarray(bg[:, 2048 * c:2048 * (c + 1)]),
            "tu": np.ascontiguousarray(bu[:, 2048 * c:2048 * (c + 1)]),
            "td": np.ascontiguousarray(bd[1024 * c:1024 * (c + 1), :]),
            "gsT": np.ascontiguousarray(gsT[:, 2048 * c:2048 * (c + 1)]),
            "usT": np.ascontiguousarray(usT[:, 2048 * c:2048 * (c + 1)]),
            "dsT": np.ascontiguousarray(dsT[64 * c:64 * (c + 1), :]),
            "e1wq": np.ascontiguousarray(wqT1[:, 1024 * p:1024 * (p + 1)]),
            "e1wk": np.ascontiguousarray(wkT1[:, 256 * p:256 * (p + 1)]),
            "e1wv": np.ascontiguousarray(wvT1[:, 256 * p:256 * (p + 1)]),
            "qg": np.ascontiguousarray(qg[8 * p:8 * (p + 1)].reshape(1, 8)),
            "l1cos": cos1t, "l1sin": sin1t,
            "fwoT": woT1,
            "ln1w": ln1w,
            "kvd": kvd, "qd": qd,
            "ku": np.ascontiguousarray(ku[:, cs]),
            "qu": np.ascontiguousarray(qu[:, cs]),
            "vu": np.ascontiguousarray(vu[:, 1024 * p:1024 * (p + 1)]),
            "rk": np.ascontiguousarray(rk[:, cs]),
            "rq": np.ascontiguousarray(rq[:, cs]),
            "mcos": cosmt, "msin": sinmt,
            "hwoT": ow,
            "ln2w": ln2w,
            "mwg": np.ascontiguousarray(mg),
            "mwu": np.ascontiguousarray(mu),
            "mwd": np.ascontiguousarray(md),
            "mrw": np.ascontiguousarray(rw[:, perm]),
            "mrb": np.ascontiguousarray(rb[perm].reshape(1, 7)),
            "mwsel": wsel,
        })
    return in_maps


def kernel(**inputs):
    nc = build_fused()
    in_maps = _stage_inputs(inputs)
    res = _run(nc, in_maps)
    out = np.concatenate([np.asarray(r["out"]) for r in res.results], axis=0)
    return out.reshape(B, S, H).astype(np.float32)


# version 09
# speedup vs baseline: 1.0050x; 1.0050x over previous
import numpy as np
import ml_dtypes
from contextlib import ExitStack
import concourse.bass as bass
import concourse.mybir as mybir
from concourse.bass_utils import run_bass_kernel_spmd

F32 = mybir.dt.float32
BF16 = mybir.dt.bfloat16
U8 = mybir.dt.uint8
I32 = mybir.dt.int32
U32 = mybir.dt.uint32
AF = mybir.ActivationFunctionType
ALU = mybir.AluOpType
AX = mybir.AxisListType.X
BF = ml_dtypes.bfloat16

B, S, H = 4, 2048, 2048
NCORES = 8
TT_ = 8192
RMS_EPS_L1 = float(np.finfo(np.float32).eps)


def _l0_costab():
    inv = (1.0 / 10000.0 ** (np.arange(0, H, 2, dtype=np.float32) / H)).astype(np.float32)
    fr = np.arange(S, dtype=np.float32)[:, None] * inv[None, :]
    return np.cos(fr).astype(np.float32), np.sin(fr).astype(np.float32)


def _l1_costab():
    inv = (1.0 / 10000.0 ** (np.arange(0, 128, 2, dtype=np.float32) / 128)).astype(np.float32)
    fr = np.arange(S, dtype=np.float32)[:, None] * inv[None, :]
    return np.cos(fr).astype(np.float32), np.sin(fr).astype(np.float32)


def _mla_costab():
    inv = (1.0 / 10000.0 ** (np.arange(0, 64, 2, dtype=np.float32) / 64)).astype(np.float32)
    fr = np.arange(S, dtype=np.float32)[:, None] * inv[None, :]
    return np.cos(fr).astype(np.float32), np.sin(fr).astype(np.float32)


IDENT = np.eye(128, dtype=np.float32).astype(BF)
CAUSAL_MASK = np.triu(np.full((128, 128), -30000.0, dtype=np.float32), 1)


def _l0_colmap(p):
    cols = []
    for lh in range(8):
        if lh < 4:
            base = 512 * p + lh * 128
        else:
            base = 1024 + 512 * p + (lh - 4) * 128
        cols.extend(range(base, base + 128))
    return np.array(cols, dtype=np.int64)


def _pi_n():
    p = []
    for a in range(64):
        p.extend(range(256 * a, 256 * a + 256, 2))
        p.extend(range(256 * a + 1, 256 * a + 256, 2))
    return np.array(p, dtype=np.int64)


def _tiles128(a):
    t = a.shape[0] // 128
    return np.ascontiguousarray(a.reshape(t, 128, -1).transpose(1, 0, 2).reshape(128, -1))



class Seq:
    ENGS = ("sync", "gpsimd", "vector", "scalar", "tensor")

    def __init__(self, es, nc):
        self.es = es
        self.nc = nc
        self.sems = {n: es.enter_context(nc.semaphore("q_" + n)) for n in self.ENGS}
        self.cnt = {n: 0 for n in self.ENGS}
        self.prog = {n: [] for n in self.ENGS}
        self.stage_deps = []
        self.ncc = 0

    def new_stage(self):
        self.stage_deps = [(e, self.cnt[e]) for e in self.ENGS if self.cnt[e] > 0]

    def on(self, eng, fn, after=(), ndma=0):
        deps = []
        for tok in list(after) + self.stage_deps:
            if tok is None:
                continue
            e2, c2 = tok
            if e2 != eng or ndma:
                deps.append((e2, c2))
        inc = 16 * ndma if ndma else 1
        self.cnt[eng] += inc
        target = self.cnt[eng]
        self.prog[eng].append((deps, fn, 0 if ndma else 1))
        return (eng, target)

    def cc(self, fn, after=()):
        name = f"cc{self.ncc}"
        self.ncc += 1
        sem = self.es.enter_context(self.nc.semaphore("q_" + name))
        self.sems[name] = sem
        deps = []
        for tok in list(after) + self.stage_deps:
            if tok is None:
                continue
            deps.append(tok)

        def step(e, fn=fn, sem=sem):
            ins = fn(e)
            ins.then_inc(sem)
        self.prog["gpsimd"].append((deps, step, 2))
        return (name, 1)

    def emit(self, block):
        for eng in self.ENGS:
            steps = self.prog[eng]
            if not steps:
                continue

            def body(e, steps=steps, eng=eng):
                sem = self.sems[eng]
                waited = {}
                for deps, fn, kind in steps:
                    for e2, c2 in deps:
                        if waited.get(e2, 0) < c2:
                            e.wait_ge(self.sems[e2], c2)
                            waited[e2] = c2
                    if kind == 1:
                        last = fn(e)
                        last.then_inc(sem, 1)
                    elif kind == 2:
                        fn(e)
                    else:
                        fn(e, sem)
            getattr(block, eng)(body)

    def finish(self, block, waits):
        def body(e):
            for e2, c2 in waits:
                e.wait_ge(self.sems[e2], c2)
        block.sync(body)


class Pool:

    def __init__(self, nc, es, nbytes):
        self.nb = nbytes
        self.t = es.enter_context(nc.sbuf_tensor("POOL", [128, nbytes], U8))
        self.off = 0
        self.peak = 0

    def reset(self):
        self.off = 0

    def take(self, name, shape, dt):
        assert shape[0] == 128
        cols = 1
        for d in shape[1:]:
            cols *= d
        nbytes = cols * mybir.dt.size(dt)
        off = (self.off + 63) & ~63
        self.off = off + nbytes
        self.peak = max(self.peak, self.off)
        assert self.off <= self.nb, f"pool overflow at {name}: {self.off} > {self.nb}"
        return self.t[:, off:off + nbytes].bitcast(dt)



def ln_zigzag(q, t_xo, xo, xc, scrap, cols, fscr, c0, N, eps, apply_fn, extra_after=()):
    inv_n = 1.0 / float(N)

    def s_mean(e):
        e.mul(cols[:, c0 + 1:c0 + 2], cols[:, c0:c0 + 1], inv_n)
        return e.copy(fscr[:, 0:1], cols[:, c0 + 1:c0 + 2])
    t1 = q.on("scalar", s_mean, after=[t_xo])

    def v_center(e):
        e.tensor_scalar(xc[:], xo[:], cols[:, c0 + 1:c0 + 2], None, ALU.subtract)
        return e.tensor_copy(fscr[:, 1:2], xc[:, xc.shape[1] - 1:xc.shape[1]])
    t2 = q.on("vector", v_center, after=[t1])

    def s_sq(e):
        e.activation(scrap[:], xc[:], AF.Square, accum_out=cols[:, c0 + 2:c0 + 3])
        return e.copy(fscr[:, 0:1], cols[:, c0 + 2:c0 + 3])
    t3 = q.on("scalar", s_sq, after=[t2])

    def v_w(e):
        e.tensor_scalar(cols[:, c0 + 3:c0 + 4], cols[:, c0 + 2:c0 + 3], inv_n, eps, ALU.mult, ALU.add)
        return e.tensor_copy(fscr[:, 1:2], cols[:, c0 + 3:c0 + 4])
    t4 = q.on("vector", v_w, after=[t3])

    def s_sqrt(e):
        e.activation(cols[:, c0 + 4:c0 + 5], cols[:, c0 + 3:c0 + 4], AF.Sqrt)
        return e.copy(fscr[:, 0:1], cols[:, c0 + 4:c0 + 5])
    t5 = q.on("scalar", s_sqrt, after=[t4])

    def v_rstd(e):
        e.reciprocal(cols[:, c0 + 5:c0 + 6], cols[:, c0 + 4:c0 + 5])
        return e.tensor_copy(fscr[:, 1:2], cols[:, c0 + 5:c0 + 6])
    t6 = q.on("vector", v_rstd, after=[t5])

    return q.on("scalar", apply_fn, after=[t6] + list(extra_after))



def emit_comb_out(q, comb, idt, psTr, ctr, fscr, agin_d, after):
    stores = []
    tEv = None
    tSt = None
    for ci in range(8):
        for h2 in range(2):
            def trp(e, ci=ci, h2=h2):
                for jj in range(8):
                    tj = h2 * 8 + jj
                    ins = e.transpose(psTr[:, jj * 128:(jj + 1) * 128],
                                      comb[:, tj * 1024 + ci * 128:tj * 1024 + (ci + 1) * 128],
                                      idt)
                return ins
            tt = q.on("tensor", trp, after=[after, tEv])
            eng = "scalar" if (2 * ci + h2) % 2 == 0 else "vector"

            def evp(e, h2=h2, ci=ci, eng=eng):
                cp = e.copy if eng == "scalar" else e.tensor_copy
                for jj in range(8):
                    cp(ctr[:, h2 * 1024 + jj * 128:h2 * 1024 + (jj + 1) * 128],
                       psTr[:, jj * 128:(jj + 1) * 128])
                return cp(fscr[:, 4:5], ctr[:, h2 * 1024 + 1023:h2 * 1024 + 1024])
            tEv = q.on(eng, evp, after=[tt, tSt, tEv])

        def st(e, sem, ci=ci):
            e.dma_start(out=agin_d[ci * 128:(ci + 1) * 128, :],
                        in_=ctr[:, 0:1024]).then_inc(sem, 16)
            e.dma_start(out=agin_d[1024 + ci * 128:1024 + (ci + 1) * 128, :],
                        in_=ctr[:, 1024:2048]).then_inc(sem, 16)
        tSt = q.on("sync", st, after=[tEv], ndma=2)
        stores.append(tSt)
    return stores


class XTrOut:

    def __init__(self, q, idt, psTr, xtT, fscr, agin_d):
        self.q, self.idt, self.psTr, self.xtT = q, idt, psTr, xtT
        self.fscr, self.agin_d = fscr, agin_d
        self.tEv = None
        self.done = [None] * 8

    def add_tile(self, jt, src_tile, after):
        for h2 in range(2):
            def trp(e, h2=h2, src_tile=src_tile):
                for jj in range(8):
                    hi = h2 * 8 + jj
                    ins = e.transpose(self.psTr[:, jj * 128:(jj + 1) * 128],
                                      src_tile[:, hi * 128:(hi + 1) * 128], self.idt)
                return ins
            tt = self.q.on("tensor", trp, after=[after, self.tEv])
            eng = "scalar" if h2 == 0 else "vector"

            def evp(e, h2=h2, jt=jt, eng=eng):
                cp = e.copy if eng == "scalar" else e.tensor_copy
                for jj in range(8):
                    hi = h2 * 8 + jj
                    cp(self.xtT[:, hi * 1024 + jt * 128:hi * 1024 + (jt + 1) * 128],
                       self.psTr[:, jj * 128:(jj + 1) * 128])
                return cp(self.fscr[:, 4:5],
                          self.xtT[:, (h2 * 8 + 7) * 1024 + (jt + 1) * 128 - 1:(h2 * 8 + 7) * 1024 + (jt + 1) * 128])
            self.tEv = self.q.on(eng, evp, after=[tt, self.tEv])
        self.done[jt] = self.tEv

    def store(self):
        def st(e, sem):
            for hi in range(16):
                e.dma_start(out=self.agin_d[hi * 128:(hi + 1) * 128, :],
                            in_=self.xtT[:, hi * 1024:(hi + 1) * 1024]).then_inc(sem, 16)
        return self.q.on("sync", st, after=[self.done[7]], ndma=16)



def emit_l0_qkv(nc, q, pool, dr, G):
    pool.reset()
    sb = pool.take
    xT = sb("xT", [128, 16 * S], BF16)
    wb = sb("wb", [128, 16 * 1024], BF16)
    cosT = sb("cosT", [128, 4 * S], F32)
    sinT = sb("sinT", [128, 4 * S], F32)
    scr = sb("scr", [128, S], F32)
    scr2 = sb("scr2", [128, S], F32)
    tmp1 = sb("tmp1", [128, S], F32)
    tmp2 = sb("tmp2", [128, S], F32)
    ro = sb("ro", [128, 2 * S], BF16)
    vo = sb("vo", [128, 2 * 1024], BF16)
    fscr = sb("fscr", [128, 8], F32)
    psQ = G["psBIG"]
    psV = G["ps2"]

    def ldtab(e, sem):
        e.dma_start(out=cosT[:], in_=dr["l0cos"]).then_inc(sem, 16)
        e.dma_start(out=sinT[:], in_=dr["l0sin"]).then_inc(sem, 16)
    tTab = q.on("sync", ldtab, ndma=2)

    def ldx(e, sem):
        for i in range(16):
            e.dma_start(out=xT[:, i * S:(i + 1) * S],
                        in_=dr["a1xT"][128 * i:128 * (i + 1), :]).then_inc(sem, 16)
    tX = q.on("gpsimd", ldx, ndma=16)

    tW = [None] * 3
    tLastUse = [None] * 3
    prevEvac = None
    ro_store = []
    vo_store = []
    for w in range(3):
        wd = [dr["a1wq"], dr["a1wk"], dr["a1wv"]][w]

        def ldw(e, sem, wd=wd):
            for i in range(16):
                e.dma_start(out=wb[:, i * 1024:(i + 1) * 1024],
                            in_=wd[128 * i:128 * (i + 1), :]).then_inc(sem, 16)
        tW[w] = q.on("gpsimd", ldw, after=[tLastUse[w - 1] if w else None], ndma=16)

        if w < 2:
            for a in range(4):
                def mmA(e, a=a):
                    for n in range(4):
                        for i in range(16):
                            ins = e.matmul(psQ[:, n * 512:(n + 1) * 512],
                                           wb[:, i * 1024 + a * 128:i * 1024 + (a + 1) * 128],
                                           xT[:, i * S + n * 512:i * S + (n + 1) * 512],
                                           start=(i == 0), stop=(i == 15))
                    return ins
                tA_ = q.on("tensor", mmA, after=[tX, tW[w], prevEvac])

                def evA(e):
                    e.tensor_copy(scr[:], psQ[:, 0:S])
                    return e.tensor_copy(fscr[:, 1:2], scr[:, S - 1:S])
                prevEvac = q.on("vector", evA, after=[tA_])

                def mmB(e, a=a):
                    for n in range(4):
                        for i in range(16):
                            ins = e.matmul(psQ[:, n * 512:(n + 1) * 512],
                                           wb[:, i * 1024 + (a + 4) * 128:i * 1024 + (a + 5) * 128],
                                           xT[:, i * S + n * 512:i * S + (n + 1) * 512],
                                           start=(i == 0), stop=(i == 15))
                    return ins
                tB_ = q.on("tensor", mmB, after=[prevEvac])
                tLastUse[w] = tB_

                def evB(e):
                    e.tensor_copy(scr2[:], psQ[:, 0:S])
                    return e.tensor_copy(fscr[:, 3:4], scr2[:, S - 1:S])
                prevEvac = q.on("vector", evB, after=[tB_])

                def rope(e, a=a):
                    e.tensor_tensor(tmp1[:], scr[:], cosT[:, a * S:(a + 1) * S], ALU.mult)
                    e.tensor_tensor(tmp2[:], scr2[:], sinT[:, a * S:(a + 1) * S], ALU.mult)
                    e.tensor_tensor(ro[:, 0:S], tmp1[:], tmp2[:], ALU.subtract)
                    e.tensor_tensor(tmp1[:], scr2[:], cosT[:, a * S:(a + 1) * S], ALU.mult)
                    e.tensor_tensor(tmp2[:], scr[:], sinT[:, a * S:(a + 1) * S], ALU.mult)
                    e.tensor_tensor(ro[:, S:2 * S], tmp1[:], tmp2[:], ALU.add)
                    return e.tensor_copy(fscr[:, 2:3], ro[:, 2 * S - 1:2 * S])
                tRope = q.on("vector", rope, after=[prevEvac, tTab,
                                                    ro_store[-1] if ro_store else None])

                od = [dr["qT0"], dr["kT0"]][w]

                def st2(e, sem, a=a, od=od):
                    e.dma_start(out=od[a * 128:(a + 1) * 128, :], in_=ro[:, 0:S]).then_inc(sem, 16)
                    e.dma_start(out=od[(a + 4) * 128:(a + 5) * 128, :], in_=ro[:, S:2 * S]).then_inc(sem, 16)
                ro_store.append(q.on("sync", st2, after=[tRope], ndma=2))
        else:
            prevV = None
            for j in range(16):
                def mmV(e, j=j):
                    for n in range(2):
                        for i in range(16):
                            ins = e.matmul(psV[:, n * 512:(n + 1) * 512],
                                           xT[:, i * S + j * 128:i * S + j * 128 + 128],
                                           wb[:, i * 1024 + n * 512:i * 1024 + (n + 1) * 512],
                                           start=(i == 0), stop=(i == 15))
                    return ins
                tV = q.on("tensor", mmV, after=[tW[2], prevV])
                tLastUse[2] = tV

                def evV(e, j=j):
                    e.tensor_copy(vo[:, (j % 2) * 1024:(j % 2 + 1) * 1024], psV[:])
                    return e.tensor_copy(fscr[:, 3:4], vo[:, (j % 2) * 1024 + 1023:(j % 2) * 1024 + 1024])
                prevV = q.on("vector", evV, after=[tV, vo_store[-2] if len(vo_store) >= 2 else None])

                def stV(e, sem, j=j):
                    e.dma_start(out=dr["vt0"][128 * j:128 * (j + 1), :],
                                in_=vo[:, (j % 2) * 1024:(j % 2 + 1) * 1024]).then_inc(sem, 16)
                vo_store.append(q.on("sync", stV, after=[prevV], ndma=1))



def emit_l0_attn(nc, q, pool, dr, G):
    pool.reset()
    sb = pool.take
    SCALE = float(1.0 / np.sqrt(128.0))
    qT = sb("qT_s", [128, 8 * S], BF16)
    kT = sb("kT_s", [128, 8 * S], BF16)
    vtok = sb("vt_s", [128, 16 * 1024], BF16)
    comb = sb("comb", [128, 16 * 1024], BF16)
    qs = sb("qs", [128, S], BF16)
    qf = sb("qf", [128, S], BF16)
    kf = sb("kf", [128, S], BF16)
    kft = sb("kft", [128, 16 * 128], BF16)
    tA = sb("tA", [128, S], F32)
    tB = sb("tB", [128, S], F32)
    P = sb("P", [128, S], BF16)
    PT = sb("PT", [128, 16 * 512], BF16)
    mem = sb("mem", [128, 132], BF16)
    msc = sb("msc", [128, 512], F32)
    mtmp = sb("mtmp", [128, 128], F32)
    onec = sb("onec", [128, 4], BF16)
    gbc = sb("gbc", [128, 8], F32)
    ombc = sb("ombc", [128, 8], F32)
    cols = sb("cols", [128, 8], F32)
    wcols = sb("wcols", [128, 4], F32)
    lcols = sb("lcols", [128, 4], F32)
    ctr = sb("ctr", [128, 2048], BF16)
    fscr = sb("fscr", [128, 8], F32)
    idt = G["idt"]
    psS = G["psBIG"]
    psT = G["psTr"]
    psM = G["ps3"]
    psD = G["ps2"][:, 0:512]
    psA = G["ps2"][:, 512:1024]

    def ld(e, sem):
        for lh in range(8):
            e.dma_start(out=qT[:, lh * S:(lh + 1) * S],
                        in_=dr["qT0"][128 * lh:128 * (lh + 1), :]).then_inc(sem, 16)
            e.dma_start(out=kT[:, lh * S:(lh + 1) * S],
                        in_=dr["kT0"][128 * lh:128 * (lh + 1), :]).then_inc(sem, 16)
        for j in range(16):
            e.dma_start(out=vtok[:, j * 1024:(j + 1) * 1024],
                        in_=dr["vt0"][128 * j:128 * (j + 1), :]).then_inc(sem, 16)
        e.dma_start(out=gbc[:], in_=dr["gate"].broadcast_to([128, 8])).then_inc(sem, 16)
    tLD = q.on("sync", ld, ndma=33)

    def prep_g(e):
        e.activation(gbc[:], gbc[:], AF.Sigmoid)
        return e.copy(fscr[:, 0:1], gbc[:, 7:8])
    tG = q.on("scalar", prep_g, after=[tLD])

    def prep_omg(e):
        e.tensor_scalar(ombc[:], gbc[:], -1.0, 1.0, ALU.mult, ALU.add)
        e.memset(onec[:], 1.0)
        return e.tensor_copy(fscr[:, 1:2], ombc[:, 7:8])
    tOMG = q.on("vector", prep_omg, after=[tG])

    tPTev = None
    tMemEv = None
    tAttEv = None
    tDev = None
    for lh in range(8):
        hOFF = lh * S

        def qsv(e, hOFF=hOFF):
            e.tensor_scalar_mul(qs[:], qT[:, hOFF:hOFF + S], SCALE)
            return e.tensor_copy(fscr[:, 2:3], qs[:, S - 1:S])
        tQS = q.on("vector", qsv, after=[tLD])

        def kf1(e, hOFF=hOFF):
            e.tensor_single_scalar(tA[:], kT[:, hOFF:hOFF + S], 0.0, ALU.min)
            return e.tensor_copy(fscr[:, 2:3], tA[:, S - 1:S])
        t1 = q.on("vector", kf1, after=[tLD])

        def kf2(e):
            e.activation(tB[:], tA[:], AF.Exp)
            return e.copy(fscr[:, 3:4], tB[:, S - 1:S])
        t2 = q.on("scalar", kf2, after=[t1])

        def kf3(e, hOFF=hOFF):
            e.tensor_single_scalar(tA[:], kT[:, hOFF:hOFF + S], 0.0, ALU.max)
            e.tensor_tensor(kf[:], tB[:], tA[:], ALU.add)
            return e.tensor_copy(fscr[:, 2:3], kf[:, S - 1:S])
        t3 = q.on("vector", kf3, after=[t2])

        for bt in range(2):
            def trk(e, bt=bt):
                for i in range(8):
                    ins = e.transpose(psT[:, i * 128:(i + 1) * 128],
                                      kf[:, (bt * 8 + i) * 128:(bt * 8 + i + 1) * 128], idt)
                return ins
            tt = q.on("tensor", trk, after=[t3, tPTev])
            eng = "scalar" if bt == 0 else "vector"

            def evk(e, bt=bt, eng=eng):
                cp = e.copy if eng == "scalar" else e.tensor_copy
                for i in range(8):
                    cp(kft[:, (bt * 8 + i) * 128:(bt * 8 + i + 1) * 128],
                       psT[:, i * 128:(i + 1) * 128])
                return cp(fscr[:, 4:5], kft[:, (bt * 8 + 7) * 128 + 127:(bt * 8 + 7) * 128 + 128])
            tPTev = q.on(eng, evk, after=[tt, tPTev])
        tTrKdone = tPTev

        def qf1(e, hOFF=hOFF):
            e.tensor_single_scalar(tA[:], qT[:, hOFF:hOFF + S], 0.0, ALU.min)
            return e.tensor_copy(fscr[:, 2:3], tA[:, S - 1:S])
        t4 = q.on("vector", qf1, after=[t3])

        def qf2(e):
            e.activation(tB[:], tA[:], AF.Exp)
            return e.copy(fscr[:, 3:4], tB[:, S - 1:S])
        t5 = q.on("scalar", qf2, after=[t4])

        def qf3(e, hOFF=hOFF):
            e.tensor_single_scalar(tA[:], qT[:, hOFF:hOFF + S], 0.0, ALU.max)
            e.tensor_tensor(qf[:], tB[:], tA[:], ALU.add)
            return e.tensor_copy(fscr[:, 2:3], qf[:, S - 1:S])
        t6 = q.on("vector", qf3, after=[t5])

        def memmm(e, lh=lh):
            for j in range(16):
                e.matmul(psM[:, 0:128], kft[:, j * 128:(j + 1) * 128],
                         vtok[:, j * 1024 + lh * 128:j * 1024 + (lh + 1) * 128],
                         start=(j == 0), stop=(j == 15))
            for j in range(16):
                ins = e.matmul(psM[:, 128:129], kft[:, j * 128:(j + 1) * 128],
                               onec[:, 0:1], start=(j == 0), stop=(j == 15))
            return ins
        tMM = q.on("tensor", memmm, after=[tTrKdone, tOMG, tMemEv])

        def memev(e):
            e.copy(mem[:, 0:129], psM[:, 0:129])
            return e.copy(fscr[:, 5:6], mem[:, 128:129])
        tMemEv = q.on("scalar", memev, after=[tMM])

        for qg in range(4):
            tPh = []
            tEx = [None] * 4
            tWS = [None] * 4
            tLS = [None] * 4
            for qc in range(4):
                jq = qg * 4 + qc

                def smm(e, hOFF=hOFF, jq=jq):
                    for n in range(4):
                        ins = e.matmul(psS[:, n * 512:(n + 1) * 512],
                                       qs[:, jq * 128:(jq + 1) * 128],
                                       kT[:, hOFF + n * 512:hOFF + (n + 1) * 512],
                                       start=True, stop=True)
                    return ins
                tS = q.on("tensor", smm, after=[tQS, tEx[qc - 1] if qc else tPTev, None if qc else tAttEv])

                def smax(e):
                    e.reduce_max(cols[:, 0:1], psS[:], axis=AX, negate=True)
                    return e.tensor_copy(cols[:, 1:2], cols[:, 0:1])
                tMx = q.on("vector", smax, after=[tS])

                def sexp(e):
                    e.activation(P[:], psS[:], AF.Exp, bias=cols[:, 0:1], scale=1.0,
                                 accum_out=cols[:, 3:4])
                    e.copy(fscr[:, 6:7], P[:, S - 1:S])
                    return e.copy(fscr[:, 6:7], cols[:, 3:4])
                tEx[qc] = q.on("scalar", sexp, after=[tMx])

                def vrec(e):
                    e.reciprocal(cols[:, 4:5], cols[:, 3:4])
                    return e.tensor_copy(cols[:, 5:6], cols[:, 4:5])
                tR = q.on("vector", vrec, after=[tEx[qc]])

                def swc(e, lh=lh, qc=qc):
                    e.mul(wcols[:, qc:qc + 1], cols[:, 4:5], ombc[:, lh:lh + 1])
                    return e.copy(fscr[:, 7:8], wcols[:, qc:qc + 1])
                tWS[qc] = q.on("scalar", swc, after=[tR])

                for bt in range(2):
                    def trp(e, bt=bt):
                        for i in range(8):
                            ins = e.transpose(psT[:, i * 128:(i + 1) * 128],
                                              P[:, (bt * 8 + i) * 128:(bt * 8 + i + 1) * 128], idt)
                        return ins
                    tt = q.on("tensor", trp, after=[tEx[qc], tPTev])
                    eng = "scalar" if bt == 0 else "vector"

                    def evp(e, bt=bt, qc=qc, eng=eng):
                        cp = e.copy if eng == "scalar" else e.tensor_copy
                        for i in range(8):
                            j = bt * 8 + i
                            cp(PT[:, j * 512 + qc * 128:j * 512 + (qc + 1) * 128],
                               psT[:, i * 128:(i + 1) * 128])
                        return cp(fscr[:, 4:5], PT[:, (bt * 8 + 7) * 512 + qc * 128 + 127:(bt * 8 + 7) * 512 + qc * 128 + 128])
                    tPTev = q.on(eng, evp, after=[tt, tPTev])
                    tPh.append(tPTev)

                def dmm(e, jq=jq):
                    return e.matmul(psD[:, 0:129], qf[:, jq * 128:(jq + 1) * 128],
                                    mem[:, 0:129], start=True, stop=True)
                tD = q.on("tensor", dmm, after=[tMemEv, t6, tDev, None if qc else tAttEv])

                def dev(e, qc=qc):
                    e.tensor_copy(msc[:, qc * 128:(qc + 1) * 128], psD[:, 0:128])
                    e.reciprocal(cols[:, 6:7], psD[:, 128:129])
                    return e.tensor_copy(cols[:, 7:8], cols[:, 6:7])
                tDev = q.on("vector", dev, after=[tD])

                def slc(e, lh=lh, qc=qc):
                    e.mul(lcols[:, qc:qc + 1], cols[:, 6:7], gbc[:, lh:lh + 1])
                    return e.copy(fscr[:, 7:8], lcols[:, qc:qc + 1])
                tLS[qc] = q.on("scalar", slc, after=[tDev])

            def amm(e, lh=lh):
                for qc in range(4):
                    for j in range(16):
                        ins = e.matmul(psA[:, qc * 128:(qc + 1) * 128],
                                       PT[:, j * 512 + qc * 128:j * 512 + (qc + 1) * 128],
                                       vtok[:, j * 1024 + lh * 128:j * 1024 + (lh + 1) * 128],
                                       start=(j == 0), stop=(j == 15))
                return ins
            tAm = q.on("tensor", amm, after=[tPh[-1], tAttEv])

            def combv(e, lh=lh, qg=qg):
                for qc in range(4):
                    jq = qg * 4 + qc
                    e.tensor_scalar(mtmp[:], msc[:, qc * 128:(qc + 1) * 128],
                                    lcols[:, qc:qc + 1], None, ALU.mult)
                    e.scalar_tensor_tensor(
                        comb[:, jq * 1024 + lh * 128:jq * 1024 + (lh + 1) * 128],
                        psA[:, qc * 128:(qc + 1) * 128], wcols[:, qc:qc + 1], mtmp[:],
                        ALU.mult, ALU.add)
                return e.tensor_copy(fscr[:, 2:3], comb[:, (qg * 4 + 3) * 1024 + lh * 128 + 127:(qg * 4 + 3) * 1024 + lh * 128 + 128])
            tAttEv = q.on("vector", combv, after=[tAm, tWS[3], tLS[3], tDev])

    stores = emit_comb_out(q, comb, idt, psT, ctr, fscr, dr["agin_comb"], tAttEv)
    tAG = q.cc(lambda e: e.collective_compute(
        "AllGather", ALU.bypass, replica_groups=G["rg8"],
        ins=[dr["agin_comb"].opt()], outs=[dr["agout_comb"].opt()]),
        after=stores)
    return tAG



def emit_oproj_ln(nc, q, pool, dr, G):
    pool.reset()
    sb = pool.take
    cbT = sb("cbT_s", [128, 16 * 1024], BF16)
    woT = sb("woT_s", [128, 16 * H], BF16)
    xres = sb("xres", [128, 2 * H], F32)
    xo = sb("xo", [128, H], F32)
    xc = sb("xc", [128, H], F32)
    scrap = sb("scrap", [128, H], F32)
    x1f = sb("x1f_s", [128, 2 * H], F32)
    x1b = sb("x1b_s", [128, 2 * H], BF16)
    xtT = sb("xtT", [128, 16 * 1024], BF16)
    cols = sb("cols", [128, 8], F32)
    fscr = sb("fscr", [128, 8], F32)
    psO = G["psBIG"]
    tr = XTrOut(q, G["idt"], G["psTr"], xtT, fscr, dr["agin_x1"])

    def ldc(e, sem):
        for k in range(16):
            e.indirect_dma_start(
                out=cbT[:, k * 1024:(k + 1) * 1024], out_offset=None,
                in_=dr["agout_comb"],
                in_offset=bass.IndirectOffsetOnAxis(G["idxB"][:, k:k + 1], axis=0),
            ).then_inc(sem, 16)
    tLD = q.on("gpsimd", ldc, after=[dr["tAGcomb"]], ndma=16)

    def ldw(e, sem):
        for i in range(16):
            e.dma_start(out=woT[:, i * H:(i + 1) * H],
                        in_=dr["bwoT"][128 * i:128 * (i + 1), :]).then_inc(sem, 16)
    tW = q.on("gpsimd", ldw, ndma=16)

    tXO = [None] * 8
    tAPP = [None] * 8
    tStore = [None] * 8
    for jt in range(8):
        slot = jt % 2

        def ldx(e, sem, jt=jt, slot=slot):
            e.dma_start(out=xres[:, slot * H:(slot + 1) * H],
                        in_=dr["bx"][128 * jt:128 * (jt + 1), :]).then_inc(sem, 16)
        tX = q.on("sync", ldx, after=[tXO[jt - 2] if jt >= 2 else None], ndma=1)

        def mm(e, jt=jt):
            for n in range(4):
                for i in range(16):
                    ins = e.matmul(psO[:, n * 512:(n + 1) * 512],
                                   cbT[:, i * 1024 + jt * 128:i * 1024 + (jt + 1) * 128],
                                   woT[:, i * H + n * 512:i * H + (n + 1) * 512],
                                   start=(i == 0), stop=(i == 15))
            return ins
        tM = q.on("tensor", mm, after=[tLD, tW, tXO[jt - 1] if jt >= 1 else None])

        def v_xo(e, slot=slot):
            e.scalar_tensor_tensor(xo[:], psO[:], 1.0, xres[:, slot * H:(slot + 1) * H],
                                   ALU.mult, ALU.add, accum_out=cols[:, 0:1])
            return e.tensor_copy(fscr[:, 1:2], cols[:, 0:1])
        tXO[jt] = q.on("vector", v_xo, after=[tM, tX])

        def apply(e, jt=jt, slot=slot):
            e.activation(x1f[:, slot * H:(slot + 1) * H], xc[:], AF.Copy,
                         scale=cols[:, 5:6])
            e.copy(x1b[:, slot * H:(slot + 1) * H],
                   x1f[:, slot * H:(slot + 1) * H])
            e.copy(fscr[:, 0:1], x1f[:, slot * H + H - 1:slot * H + H])
            return e.copy(fscr[:, 0:1], x1b[:, slot * H + H - 1:slot * H + H])
        tAPP[jt] = ln_zigzag(q, tXO[jt], xo, xc, scrap, cols, fscr, 0, H, 1e-5, apply,
                             extra_after=[tStore[jt - 2] if jt >= 2 else None,
                                          tr.done[jt - 2] if jt >= 2 else None])

        def st(e, sem, jt=jt, slot=slot):
            e.dma_start(out=dr["x1f"][128 * jt:128 * (jt + 1), :],
                        in_=x1f[:, slot * H:(slot + 1) * H]).then_inc(sem, 16)
        tStore[jt] = q.on("sync", st, after=[tAPP[jt]], ndma=1)

        tr.add_tile(jt, x1b[:, slot * H:(slot + 1) * H], tAPP[jt])

    tSt = tr.store()
    tAG = q.cc(lambda e: e.collective_compute(
        "AllGather", ALU.bypass, replica_groups=G["rg8"],
        ins=[dr["agin_x1"].opt()], outs=[dr["agout_x1"].opt()]),
        after=[tSt, tStore[6], tStore[7]])
    return tAG



def _emit_decode(q, tiles, src_u8, scl, dst, dst_off, after, m75):
    tA, tB, tBi, nb1, nb2, d1, d2, sgt = tiles
    NC = 2048

    def v_bytes(e):
        e.tensor_copy(tA[:], src_u8[:])
        e.tensor_scalar(tB[:], tA[:], 0.0625, -0.46875, ALU.mult, ALU.add)
        e.tensor_copy(tBi[:], tB[:])
        e.tensor_copy(tB[:], tBi[:])
        e.scalar_tensor_tensor(nb1[:], tB[:], -16.0, tA[:], ALU.mult, ALU.add)
        e.tensor_copy(nb2[:], tB[:])
        return e.tensor_copy(tA[:, 0:1], nb2[:, NC - 1:NC])
    tV = q.on("vector", v_bytes, after=after)

    outs = [None]
    for pi, nb in ((0, nb1), (1, nb2)):
        def s_sign(e, nb=nb):
            e.activation(sgt[:], nb[:], AF.Sign, bias=m75[:, 0:1])
            return e.copy(tA[:, 1:2], sgt[:, NC - 1:NC])
        tS = q.on("scalar", s_sign, after=[tV, outs[-1]])

        def v_dec(e, nb=nb, pi=pi):
            e.scalar_tensor_tensor(d1[:], sgt[:], -4.0, nb[:], ALU.mult, ALU.add)
            e.tensor_scalar(d2[:], d1[:], -8.0, 0.0, ALU.add, ALU.max)
            e.tensor_scalar(nb[:], d1[:], -10.0, 0.0, ALU.add, ALU.max)
            e.scalar_tensor_tensor(d2[:], nb[:], 2.0, d2[:], ALU.mult, ALU.add)
            e.tensor_tensor(d2[:], d2[:], d1[:], ALU.add)
            e.tensor_scalar(d2[:], d2[:], -4.0, None, ALU.add)
            e.tensor_tensor(d2[:], d2[:], sgt[:], ALU.mult)
            e.scalar_tensor_tensor(dst[:, dst_off + pi * NC:dst_off + (pi + 1) * NC],
                                   d2[:], -0.5, scl[:], ALU.mult, ALU.mult)
            return e.tensor_copy(tA[:, 2:3],
                                 dst[:, dst_off + (pi + 1) * NC - 1:dst_off + (pi + 1) * NC])
        outs.append(q.on("vector", v_dec, after=[tS]))
    return outs[-1]


def emit_l0_ffn(nc, q, pool, dr, G):
    pool.reset()
    sb = pool.take
    wg = sb("wg", [128, 16 * 2048], BF16)
    wu = sb("wu", [128, 16 * 2048], BF16)
    u8t = sb("u8t", [128, 2048], U8)
    sclt = sb("sclt", [128, 2048], F32)
    tA = sb("tA", [128, 2048], F32)
    tB = sb("tB", [128, 2048], F32)
    tBi = sb("tBi", [128, 2048], I32)
    obuf = tA.bitcast(BF16)
    nb1 = sb("nb1", [128, 2048], BF16)
    nb2 = sb("nb2", [128, 2048], BF16)
    d1 = sb("d1", [128, 2048], BF16)
    d2 = sb("d2", [128, 2048], BF16)
    sgt = sb("sgt", [128, 2048], BF16)
    sq = sb("sq", [128, 512], F32)
    cc_ = sb("cc_", [128, 512], F32)
    arg = sb("arg", [128, 512], F32)
    th = sq
    dd = cc_
    e1 = arg
    ibuf = sb("ibuf", [128, 2 * 512], BF16)
    fscr = sb("fscr", [128, 8], F32)
    m75c = sb("m75c", [128, 1], F32)
    xcb = sb("xcb", [128, 16 * 512], BF16)
    psG = G["ps2"]
    psU = G["psBIG"][:, 0:1024]
    psBIG = G["psBIG"]
    dec_tiles = (tA, tB, tBi, nb1, nb2, d1, d2, sgt)
    xg = dr["agout_x1"]

    def mkc(e):
        e.memset(m75c[:], -7.5)
        return e.tensor_copy(fscr[:, 3:4], m75c[:])
    tM75 = q.on("vector", mkc)

    tDecDone = None
    tLdPrev = [None, None]
    tDecQ = []
    for a in range(8):
        for mi, (wdst, t_d, s_d) in enumerate([(wg, dr["tg"], dr["gsT"]), (wu, dr["tu"], dr["usT"])]):
            slot = (2 * a + mi) % 2

            def ldt(e, sem, a=a, t_d=t_d, s_d=s_d):
                e.dma_start(out=u8t[:],
                            in_=t_d[128 * a:128 * (a + 1), :]).then_inc(sem, 16)
                e.dma_start(out=sclt[:],
                            in_=s_d[8 * a:8 * (a + 1), :].unsqueeze(1).broadcast_to([8, 16, 2048])).then_inc(sem, 16)
            tL = q.on("sync", ldt, after=[tLdPrev[slot], tDecDone], ndma=2)
            tDecDone = _emit_decode(q, dec_tiles, u8t[:], sclt[:],
                                    wdst, (2 * a) * 2048, after=[tL, tDecDone, tM75], m75=m75c)
            tLdPrev[slot] = tDecDone
        if a % 2 == 1:
            tDecQ.append(tDecDone)

    tGU = None
    tIst = [None] * 4
    prev_mm = [None, None]
    for tc in range(16):
        r, u = tc // 2, tc % 2

        def ldxc(e, sem, r=r, u=u):
            for i in range(16):
                base = 2048 * r + 256 * (i // 2) + (i % 2)
                e.dma_start(out=xcb[:, i * 512:(i + 1) * 512],
                            in_=xg[base:base + 255:2,
                                   512 * u:512 * (u + 1)]).then_inc(sem, 16)
        tXc = q.on("sync", ldxc, after=[tGU, dr["tAGx1"] if tc == 0 else None], ndma=16)

        for nt in range(16):
            slot = nt % 2

            if tc == 0:
                for qk in range(4):
                    def gummq(e, nt=nt, slot=slot, qk=qk):
                        for i in range(4 * qk, 4 * qk + 4):
                            e.matmul(psG[:, slot * 512:(slot + 1) * 512],
                                     wg[:, i * 2048 + nt * 128:i * 2048 + (nt + 1) * 128],
                                     xcb[:, i * 512:(i + 1) * 512],
                                     start=(i == 0), stop=(i == 15))
                        for i in range(4 * qk, 4 * qk + 4):
                            ins = e.matmul(psU[:, slot * 512:(slot + 1) * 512],
                                           wu[:, i * 2048 + nt * 128:i * 2048 + (nt + 1) * 128],
                                           xcb[:, i * 512:(i + 1) * 512],
                                           start=(i == 0), stop=(i == 15))
                        return ins
                    tGU = q.on("tensor", gummq,
                               after=[tXc, tDecQ[qk],
                                      prev_mm[slot] if qk == 0 else None])
            else:
                def gumm(e, nt=nt, slot=slot):
                    for i in range(16):
                        e.matmul(psG[:, slot * 512:(slot + 1) * 512],
                                 wg[:, i * 2048 + nt * 128:i * 2048 + (nt + 1) * 128],
                                 xcb[:, i * 512:(i + 1) * 512],
                                 start=(i == 0), stop=(i == 15))
                    for i in range(16):
                        ins = e.matmul(psU[:, slot * 512:(slot + 1) * 512],
                                       wu[:, i * 2048 + nt * 128:i * 2048 + (nt + 1) * 128],
                                       xcb[:, i * 512:(i + 1) * 512],
                                       start=(i == 0), stop=(i == 15))
                    return ins
                tGU = q.on("tensor", gumm, after=[tXc, tDecDone, prev_mm[slot]])

            def gelu1(e, slot=slot):
                e.activation(sq[:], psG[:, slot * 512:(slot + 1) * 512], AF.Square)
                return e.copy(fscr[:, 0:1], sq[:, 511:512])
            tq1 = q.on("scalar", gelu1, after=[tGU])

            def gelu2(e, slot=slot):
                e.tensor_scalar(cc_[:], sq[:], 0.044715 * 0.7978845608028654,
                                0.7978845608028654, ALU.mult, ALU.add)
                e.tensor_tensor(arg[:], cc_[:], psG[:, slot * 512:(slot + 1) * 512], ALU.mult)
                return e.tensor_copy(fscr[:, 1:2], arg[:, 511:512])
            tq2 = q.on("vector", gelu2, after=[tq1])

            def gelu3(e):
                e.activation(th[:], arg[:], AF.Tanh)
                return e.copy(fscr[:, 0:1], th[:, 511:512])
            tq3 = q.on("scalar", gelu3, after=[tq2])

            def gelu4(e, slot=slot):
                e.tensor_scalar(dd[:], th[:], 1.0, 0.5, ALU.add, ALU.mult)
                e.tensor_tensor(e1[:], dd[:], psG[:, slot * 512:(slot + 1) * 512], ALU.mult)
                e.tensor_tensor(ibuf[:, slot * 512:(slot + 1) * 512], e1[:],
                                psU[:, slot * 512:(slot + 1) * 512], ALU.mult)
                return e.tensor_copy(fscr[:, 1:2], ibuf[:, slot * 512 + 511:slot * 512 + 512])
            tq4 = q.on("vector", gelu4, after=[tq3, tIst[slot] if nt >= 2 else None])
            prev_mm[slot] = tq4

            def ist(e, sem, nt=nt, tc=tc, slot=slot):
                e.dma_start(out=dr["inter"][128 * nt:128 * (nt + 1), 512 * tc:512 * (tc + 1)],
                            in_=ibuf[:, slot * 512:(slot + 1) * 512]).then_inc(sem, 16)
            tIst[slot] = q.on("sync", ist, after=[tq4], ndma=1)
    tPh1St = [tIst[0], tIst[1]]

    wd = wg
    inT = xcb
    tDec2 = tGU
    tLdPrev2 = [tPh1St[0], tPh1St[1]]
    for a in range(8):
        slot = a % 2

        def ldt2(e, sem, a=a):
            e.dma_start(out=u8t[:],
                        in_=dr["td"][128 * a:128 * (a + 1), :]).then_inc(sem, 16)
            e.dma_start(out=sclt[:],
                        in_=dr["dsT"][8 * a:8 * (a + 1), :].unsqueeze(1).broadcast_to([8, 16, 2048])).then_inc(sem, 16)
        tL2 = q.on("sync", ldt2, after=[tLdPrev2[slot], tPh1St[slot], tDec2], ndma=2)
        tDec2 = _emit_decode(q, dec_tiles, u8t[:], sclt[:],
                             wd, (2 * a) * 2048, after=[tL2, tDec2, tGU], m75=m75c)
        tLdPrev2[slot] = tDec2
        if a == 3:
            tDec2H = tDec2

    tOev = [None, None]
    tOevLast = None
    tOst = [None, None]
    tMM2 = None
    for tc in range(16):
        def ldin(e, sem, tc=tc):
            for i in range(16):
                e.dma_start(out=inT[:, i * 512:(i + 1) * 512],
                            in_=dr["inter"][128 * i:128 * (i + 1), 512 * tc:512 * (tc + 1)]).then_inc(sem, 16)
        tI = q.on("gpsimd", ldin, after=[tMM2, tPh1St[0], tPh1St[1]], ndma=16)

        for tq_ in range(4):
            slot = tq_ % 2

            if tc == 0:
                def dmm2a(e, tq_=tq_):
                    for hq in range(4):
                        for ntx in range(8):
                            ins = e.matmul(psBIG[:, hq * 512:(hq + 1) * 512],
                                           inT[:, ntx * 512 + tq_ * 128:ntx * 512 + (tq_ + 1) * 128],
                                           wd[:, ntx * 2048 + hq * 512:ntx * 2048 + (hq + 1) * 512],
                                           start=(ntx == 0), stop=False)
                    return ins
                tMa = q.on("tensor", dmm2a, after=[tI, tDec2H, tOevLast])

                def dmm2b(e, tq_=tq_):
                    for hq in range(4):
                        for ntx in range(8, 16):
                            ins = e.matmul(psBIG[:, hq * 512:(hq + 1) * 512],
                                           inT[:, ntx * 512 + tq_ * 128:ntx * 512 + (tq_ + 1) * 128],
                                           wd[:, ntx * 2048 + hq * 512:ntx * 2048 + (hq + 1) * 512],
                                           start=False, stop=(ntx == 15))
                    return ins
                tMM2 = q.on("tensor", dmm2b, after=[tMa, tDec2])
            else:
                def dmm2(e, tq_=tq_):
                    for hq in range(4):
                        for ntx in range(16):
                            ins = e.matmul(psBIG[:, hq * 512:(hq + 1) * 512],
                                           inT[:, ntx * 512 + tq_ * 128:ntx * 512 + (tq_ + 1) * 128],
                                           wd[:, ntx * 2048 + hq * 512:ntx * 2048 + (hq + 1) * 512],
                                           start=(ntx == 0), stop=(ntx == 15))
                    return ins
                tMM2 = q.on("tensor", dmm2, after=[tI, tDec2, tOevLast])

            def oev(e, slot=slot):
                e.tensor_copy(obuf[:, slot * 2048:(slot + 1) * 2048], psBIG[:])
                return e.tensor_copy(fscr[:, 1:2], obuf[:, slot * 2048 + 2047:slot * 2048 + 2048])
            tOev[slot] = q.on("vector", oev, after=[tMM2, tOst[slot]])
            tOevLast = tOev[slot]

            def ost(e, sem, tc=tc, tq_=tq_, slot=slot):
                e.dma_start(out=dr["part1"][512 * tc + 128 * tq_:512 * tc + 128 * (tq_ + 1), :],
                            in_=obuf[:, slot * 2048:(slot + 1) * 2048]).then_inc(sem, 16)
            tOst[slot] = q.on("sync", ost, after=[tOev[slot]], ndma=1)

    tRS = q.cc(lambda e: e.collective_compute(
        "ReduceScatter", ALU.add, replica_groups=G["rg8"],
        ins=[dr["part1"].opt()], outs=[dr["rs1"].opt()]),
        after=[tOst[0], tOst[1]])
    return tRS



def emit_postffn_ln(nc, q, pool, dr, G):
    pool.reset()
    sb = pool.take
    pt = sb("pt", [128, 2 * H], BF16)
    xres = sb("xres", [128, 2 * H], F32)
    xo = sb("xo", [128, H], F32)
    xc = sb("xc", [128, H], F32)
    scrap = sb("scrap", [128, H], F32)
    yf = sb("yf_s", [128, 2 * H], F32)
    yb = sb("yb_s", [128, 2 * H], BF16)
    xtT = sb("xtT", [128, 16 * 1024], BF16)
    cols = sb("cols", [128, 8], F32)
    fscr = sb("fscr", [128, 8], F32)
    tr = XTrOut(q, G["idt"], G["psTr"], xtT, fscr, dr["agin_x2"])

    tSum = [None] * 8
    tStore = [None] * 8
    for jt in range(8):
        slot = jt % 2

        def ld(e, sem, jt=jt, slot=slot):
            e.dma_start(out=pt[:, slot * H:(slot + 1) * H],
                        in_=dr["rs1"][128 * jt:128 * (jt + 1), :]).then_inc(sem, 16)
            e.dma_start(out=xres[:, slot * H:(slot + 1) * H],
                        in_=dr["x1f"][128 * jt:128 * (jt + 1), :]).then_inc(sem, 16)
        tL = q.on("sync", ld, after=[tSum[jt - 2] if jt >= 2 else None,
                                     dr["tRS1"] if jt == 0 else None], ndma=2)

        def vsum(e, slot=slot):
            e.scalar_tensor_tensor(xo[:], pt[:, slot * H:(slot + 1) * H], 1.0,
                                   xres[:, slot * H:(slot + 1) * H],
                                   ALU.mult, ALU.add, accum_out=cols[:, 0:1])
            return e.tensor_copy(fscr[:, 1:2], cols[:, 0:1])
        tSum[jt] = q.on("vector", vsum, after=[tL])

        def apply(e, jt=jt, slot=slot):
            e.activation(yf[:, slot * H:(slot + 1) * H], xc[:], AF.Copy, scale=cols[:, 5:6])
            e.copy(yb[:, slot * H:(slot + 1) * H], yf[:, slot * H:(slot + 1) * H])
            e.copy(fscr[:, 0:1], yf[:, slot * H + H - 1:slot * H + H])
            return e.copy(fscr[:, 0:1], yb[:, slot * H + H - 1:slot * H + H])
        tAPP = ln_zigzag(q, tSum[jt], xo, xc, scrap, cols, fscr, 0, H, 1e-5, apply,
                         extra_after=[tStore[jt - 2] if jt >= 2 else None,
                                      tr.done[jt - 2] if jt >= 2 else None])

        def st(e, sem, jt=jt, slot=slot):
            e.dma_start(out=dr["x2f"][128 * jt:128 * (jt + 1), :],
                        in_=yf[:, slot * H:(slot + 1) * H]).then_inc(sem, 16)
        tStore[jt] = q.on("sync", st, after=[tAPP], ndma=1)

        tr.add_tile(jt, yb[:, slot * H:(slot + 1) * H], tAPP)

    tSt = tr.store()
    tAG = q.cc(lambda e: e.collective_compute(
        "AllGather", ALU.bypass, replica_groups=G["rg8"],
        ins=[dr["agin_x2"].opt()], outs=[dr["agout_x2"].opt()]),
        after=[tSt, tStore[6], tStore[7]])
    return tAG



def emit_l1_qkv(nc, q, pool, dr, G):
    pool.reset()
    sb = pool.take
    xT = sb("xT_s", [128, 16 * S], BF16)
    wb = sb("wb", [128, 16 * 1024], BF16)
    cosq = sb("cosq_s", [128, 16 * 64], F32)
    sinq = sb("sinq_s", [128, 16 * 64], F32)
    qgc = sb("qgc", [128, 8], F32)
    qn = sb("qn", [128, 2 * 1024], F32)
    qro = sb("qro", [128, 16 * 1024], BF16)
    kro = sb("kro", [128, 16 * 256], BF16)
    vo = sb("vo", [128, 2 * 256], BF16)
    orow = sb("orow", [128, S], BF16)
    sqc = sb("sqc", [128, 64], F32)
    scrap = sb("scrap", [128, 128], F32)
    vtmp = sb("vtmp", [128, 128], F32)
    fscr = sb("fscr", [128, 8], F32)
    idt = G["idt"]
    psQ = G["ps2"]
    psQb = G["psBIG"][:, 0:1024]
    psT = G["psTr"]

    def ld(e, sem):
        e.dma_start(out=cosq[:], in_=dr["l1cos"]).then_inc(sem, 16)
        e.dma_start(out=sinq[:], in_=dr["l1sin"]).then_inc(sem, 16)
        e.dma_start(out=qgc[:], in_=dr["qg"].broadcast_to([128, 8])).then_inc(sem, 16)
    tLD = q.on("sync", ld, ndma=3)

    def ldx(e, sem):
        for i in range(16):
            for blk in range(2):
                e.indirect_dma_start(
                    out=xT[:, i * S + blk * 1024:i * S + (blk + 1) * 1024], out_offset=None,
                    in_=dr["agout_x2"],
                    in_offset=bass.IndirectOffsetOnAxis(G["idxPair"][:, 2 * i + blk:2 * i + blk + 1], axis=0),
                ).then_inc(sem, 16)
    tX = q.on("gpsimd", ldx, after=[dr["tAGx2"]], ndma=32)

    def emit_proj_norm_rope(w_d, ncols, nheads, out_tiles, tPrevUse, eps, prev_t5):
        def ldw(e, sem):
            for i in range(16):
                e.dma_start(out=wb[:, i * ncols:(i + 1) * ncols],
                            in_=w_d[128 * i:128 * (i + 1), :]).then_inc(sem, 16)
        tW = q.on("gpsimd", ldw, after=[tPrevUse], ndma=16)

        toks = [None] * 16
        t5H = [None] * 16
        tLastMM = None
        for j in range(16):
            psq = psQ if j % 2 == 0 else psQb
            sq0 = 32 * (j % 2)
            qo = (j % 2) * 1024

            def pmm(e, j=j, ncols=ncols, psq=psq):
                nch = max(1, ncols // 512)
                for n in range(nch):
                    w512 = min(512, ncols)
                    for i in range(16):
                        ins = e.matmul(psq[:, n * 512:n * 512 + w512],
                                       xT[:, i * S + j * 128:i * S + j * 128 + 128],
                                       wb[:, i * ncols + n * w512:i * ncols + (n + 1) * w512],
                                       start=(i == 0), stop=(i == 15))
                return ins
            tMM = q.on("tensor", pmm,
                       after=[tX, tW, t5H[j - 2] if j >= 2 else prev_t5[j % 2]])
            tLastMM = tMM

            def ssq(e, nheads=nheads, psq=psq, sq0=sq0):
                for lh in range(nheads):
                    ins = e.activation(scrap[:], psq[:, lh * 128:(lh + 1) * 128], AF.Square,
                                       accum_out=sqc[:, sq0 + lh:sq0 + lh + 1])
                return e.copy(fscr[:, 0:1], sqc[:, sq0 + nheads - 1:sq0 + nheads])
            t1 = q.on("scalar", ssq, after=[tMM])

            def vw(e, nheads=nheads, eps=eps, sq0=sq0):
                e.tensor_scalar(sqc[:, sq0 + 8:sq0 + 8 + nheads], sqc[:, sq0:sq0 + nheads],
                                1.0 / 128.0, eps, ALU.mult, ALU.add)
                return e.tensor_copy(fscr[:, 1:2], sqc[:, sq0 + 8 + nheads - 1:sq0 + 8 + nheads])
            t2 = q.on("vector", vw, after=[t1])

            def s3(e, nheads=nheads, sq0=sq0):
                e.activation(sqc[:, sq0 + 16:sq0 + 16 + nheads], sqc[:, sq0 + 8:sq0 + 8 + nheads], AF.Sqrt)
                return e.copy(fscr[:, 0:1], sqc[:, sq0 + 16 + nheads - 1:sq0 + 16 + nheads])
            t3 = q.on("scalar", s3, after=[t2])

            def v4(e, nheads=nheads, sq0=sq0):
                e.reciprocal(sqc[:, sq0 + 24:sq0 + 24 + nheads], sqc[:, sq0 + 16:sq0 + 16 + nheads])
                return e.tensor_copy(fscr[:, 1:2], sqc[:, sq0 + 24 + nheads - 1:sq0 + 24 + nheads])
            t4 = q.on("vector", v4, after=[t3])

            def s5(e, nheads=nheads, psq=psq, sq0=sq0, qo=qo):
                for lh in range(nheads):
                    ins = e.activation(qn[:, qo + lh * 128:qo + (lh + 1) * 128],
                                       psq[:, lh * 128:(lh + 1) * 128], AF.Copy,
                                       scale=sqc[:, sq0 + 24 + lh:sq0 + 25 + lh])
                return e.copy(fscr[:, 0:1], qn[:, qo + nheads * 128 - 1:qo + nheads * 128])
            t5 = q.on("scalar", s5, after=[t4, toks[j - 2] if j >= 2 else None])
            t5H[j] = t5

            def vrope(e, j=j, nheads=nheads, ncols=ncols, qo=qo):
                c = cosq[:, j * 64:(j + 1) * 64]
                s = sinq[:, j * 64:(j + 1) * 64]
                for lh in range(nheads):
                    a = qn[:, qo + lh * 128:qo + lh * 128 + 64]
                    b2 = qn[:, qo + lh * 128 + 64:qo + lh * 128 + 128]
                    e.tensor_tensor(vtmp[:, 0:64], a, c, ALU.mult)
                    e.tensor_tensor(vtmp[:, 64:128], b2, s, ALU.mult)
                    e.tensor_tensor(out_tiles[:, j * ncols + lh * 128:j * ncols + lh * 128 + 64],
                                    vtmp[:, 0:64], vtmp[:, 64:128], ALU.add)
                    e.tensor_tensor(vtmp[:, 0:64], b2, c, ALU.mult)
                    e.tensor_tensor(vtmp[:, 64:128], a, s, ALU.mult)
                    e.tensor_tensor(out_tiles[:, j * ncols + lh * 128 + 64:j * ncols + (lh + 1) * 128],
                                    vtmp[:, 0:64], vtmp[:, 64:128], ALU.subtract)
                return e.tensor_copy(fscr[:, 1:2],
                                     out_tiles[:, (j + 1) * ncols - 1:(j + 1) * ncols])
            toks[j] = q.on("vector", vrope, after=[t5, tLD])
        return tLastMM, toks, (t5H[14], t5H[15])

    tQuse, tQtok, qt5 = emit_proj_norm_rope(dr["e1wq"], 1024, 8, qro, None, RMS_EPS_L1,
                                            (None, None))
    tKuse, tKtok, kt5 = emit_proj_norm_rope(dr["e1wk"], 256, 2, kro, tQuse, RMS_EPS_L1, qt5)

    tTr = None
    tQst = None
    for a in range(8):
        for half in range(2):
            def trq(e, a=a, half=half):
                for jj in range(8):
                    j = 8 * half + jj
                    ins = e.transpose(psT[:, jj * 128:(jj + 1) * 128],
                                      qro[:, j * 1024 + a * 128:j * 1024 + (a + 1) * 128], idt)
                return ins
            tt = q.on("tensor", trq, after=[tQtok[15], tTr])

            eng = "scalar" if half == 0 else "vector"

            def evq(e, half=half, eng=eng):
                cp = e.copy if eng == "scalar" else e.tensor_copy
                for jj in range(8):
                    cp(orow[:, half * 1024 + jj * 128:half * 1024 + (jj + 1) * 128],
                       psT[:, jj * 128:(jj + 1) * 128])
                return cp(fscr[:, 2:3], orow[:, half * 1024 + 1023:half * 1024 + 1024])
            tTr = q.on(eng, evq, after=[tt, tTr, tQst if half == 0 else None])

        def sgq(e, a=a):
            e.tensor_scalar(orow[:], orow[:], qgc[:, a:a + 1], None, ALU.mult)
            return e.tensor_copy(fscr[:, 3:4], orow[:, S - 1:S])
        tG = q.on("vector", sgq, after=[tTr])

        def stq(e, sem, a=a):
            e.dma_start(out=dr["qT1"][a * 128:(a + 1) * 128, :], in_=orow[:]).then_inc(sem, 16)
        tQst = q.on("sync", stq, after=[tG], ndma=1)

    for a in range(2):
        for half in range(2):
            def trk2(e, a=a, half=half):
                for jj in range(8):
                    j = 8 * half + jj
                    ins = e.transpose(psT[:, jj * 128:(jj + 1) * 128],
                                      kro[:, j * 256 + a * 128:j * 256 + (a + 1) * 128], idt)
                return ins
            tt = q.on("tensor", trk2, after=[tKtok[15], tTr])

            eng = "scalar" if half == 0 else "vector"

            def evk2(e, half=half, eng=eng):
                cp = e.copy if eng == "scalar" else e.tensor_copy
                for jj in range(8):
                    cp(orow[:, half * 1024 + jj * 128:half * 1024 + (jj + 1) * 128],
                       psT[:, jj * 128:(jj + 1) * 128])
                return cp(fscr[:, 2:3], orow[:, half * 1024 + 1023:half * 1024 + 1024])
            tTr = q.on(eng, evk2, after=[tt, tTr, tQst if half == 0 else None])

        def stk(e, sem, a=a):
            e.dma_start(out=dr["kT1"][a * 128:(a + 1) * 128, :], in_=orow[:]).then_inc(sem, 16)
        tQst = q.on("sync", stk, after=[tTr], ndma=1)

    def ldwv(e, sem):
        for i in range(16):
            e.dma_start(out=wb[:, i * 256:(i + 1) * 256],
                        in_=dr["e1wv"][128 * i:128 * (i + 1), :]).then_inc(sem, 16)
    tWv = q.on("gpsimd", ldwv, after=[tKuse], ndma=16)

    tVev = None
    tVst = [None, None]
    for j in range(16):
        def vmm(e, j=j):
            for i in range(16):
                ins = e.matmul(psQ[:, 0:256],
                               xT[:, i * S + j * 128:i * S + j * 128 + 128],
                               wb[:, i * 256:(i + 1) * 256],
                               start=(i == 0), stop=(i == 15))
            return ins
        tV = q.on("tensor", vmm, after=[tWv, tVev, kt5[0] if j == 0 else None])

        def vev(e, j=j):
            e.tensor_copy(vo[:, (j % 2) * 256:(j % 2 + 1) * 256], psQ[:, 0:256])
            return e.tensor_copy(fscr[:, 4:5], vo[:, (j % 2) * 256 + 255:(j % 2) * 256 + 256])
        tVev = q.on("vector", vev, after=[tV, tVst[j % 2]])

        def stv(e, sem, j=j):
            e.dma_start(out=dr["vt1"][128 * j:128 * (j + 1), :],
                        in_=vo[:, (j % 2) * 256:(j % 2 + 1) * 256]).then_inc(sem, 16)
        tVst[j % 2] = q.on("sync", stv, after=[tVev], ndma=1)



def emit_causal_attn(nc, q, pool, dr, G, kvc, kvdiv, qTd, kTd, vtd, agin, agout):
    pool.reset()
    sb = pool.take
    SCALE = float(1.0 / np.sqrt(128.0))
    qT = sb("qT_s", [128, 8 * S], BF16)
    kT = sb("kT_s", [128, (kvc // 128) * S], BF16)
    vtok = sb("vt_s", [128, 16 * kvc], BF16)
    comb = sb("comb", [128, 16 * 1024], BF16)
    qs = sb("qs", [128, S], BF16)
    SS = sb("SS", [128, 2 * S], BF16)
    P = sb("P", [128, 2 * S], BF16)
    PT = sb("PT", [128, 16 * 128], BF16)
    ctr = sb("ctr", [128, 2048], BF16)
    cols = sb("cols", [128, 16], F32)
    fscr = sb("fscr", [128, 8], F32)
    idt = G["idt"]
    maskt = G["maskt"]
    psS = G["psBIG"]
    psT = G["psTr"]
    psA = G["ps3"]

    def ld(e, sem):
        for lh in range(8):
            e.dma_start(out=qT[:, lh * S:(lh + 1) * S],
                        in_=qTd[128 * lh:128 * (lh + 1), :]).then_inc(sem, 16)
        for kk in range(kvc // 128):
            e.dma_start(out=kT[:, kk * S:(kk + 1) * S],
                        in_=kTd[128 * kk:128 * (kk + 1), :]).then_inc(sem, 16)
        for j in range(16):
            e.dma_start(out=vtok[:, j * kvc:(j + 1) * kvc],
                        in_=vtd[128 * j:128 * (j + 1), :]).then_inc(sem, 16)
    tLD = q.on("sync", ld, ndma=24 + kvc // 128)

    tC = None
    tPTev = None
    tV1H = [None] * 16
    tV1A = [None] * 16
    tExH = [None] * 16
    tTrpH = [None] * 16
    tRH = [None] * 16
    tWSH = [None] * 16
    tWSPrev = None
    for lh in range(8):
        hOFF = lh * S
        kOFF = (lh // kvdiv) * S if kvc != 1024 else lh * S
        voff = (lh // kvdiv) * 128 if kvc != 1024 else lh * 128

        def qsv(e, hOFF=hOFF):
            e.tensor_scalar_mul(qs[:], qT[:, hOFF:hOFF + S], SCALE)
            return e.tensor_copy(fscr[:, 2:3], qs[:, S - 1:S])
        tQS = q.on("vector", qsv, after=[tLD, tC])

        tV1Head = list(tV1H)

        def flush(j):
            nonlocal tC, tPTev, tWSPrev
            pb = 8 * (j % 2)
            tEx = tExH[j]

            def vrec(e, pb=pb):
                e.reciprocal(cols[:, pb + 4:pb + 5], cols[:, pb + 3:pb + 4])
                return e.tensor_copy(cols[:, pb + 5:pb + 6], cols[:, pb + 4:pb + 5])
            tR = q.on("vector", vrec, after=[tEx])

            def swc(e, pb=pb):
                e.copy(cols[:, pb + 6:pb + 7], cols[:, pb + 4:pb + 5])
                return e.copy(fscr[:, 7:8], cols[:, pb + 6:pb + 7])
            tWS = q.on("scalar", swc, after=[tR, tWSPrev, tC])
            tWSPrev = tWS

            nblk = j + 1
            tPh = None
            for b0 in range(0, nblk, 8):
                bn = min(8, nblk - b0)

                def trp(e, b0=b0, bn=bn, j=j):
                    for i in range(bn):
                        ins = e.transpose(psT[:, i * 128:(i + 1) * 128],
                                          P[:, (j % 2) * S + (b0 + i) * 128:(j % 2) * S + (b0 + i + 1) * 128], idt)
                    return ins
                tt = q.on("tensor", trp, after=[tEx, tPTev])
                tTrpH[j] = tt
                eng = "scalar" if (b0 // 8) % 2 == 0 else "vector"

                def evp(e, b0=b0, bn=bn, eng=eng):
                    cp = e.copy if eng == "scalar" else e.tensor_copy
                    for i in range(bn):
                        cp(PT[:, (b0 + i) * 128:(b0 + i + 1) * 128],
                           psT[:, i * 128:(i + 1) * 128])
                    return cp(fscr[:, 4:5], PT[:, (b0 + bn) * 128 - 1:(b0 + bn) * 128])
                tPTev = q.on(eng, evp, after=[tt, tPTev])
                tPh = tPTev

            def amm(e, j=j, voff=voff, nblk=nblk):
                for jj in range(nblk):
                    ins = e.matmul(psA[:, 0:128],
                                   PT[:, jj * 128:(jj + 1) * 128],
                                   vtok[:, jj * kvc + voff:jj * kvc + voff + 128],
                                   start=(jj == 0), stop=(jj == nblk - 1))
                return ins
            tAm = q.on("tensor", amm, after=[tPh, tC])

            def combv(e, lh=lh, j=j, pb=pb):
                e.tensor_scalar(comb[:, j * 1024 + lh * 128:j * 1024 + (lh + 1) * 128],
                                psA[:, 0:128], cols[:, pb + 6:pb + 7], None, ALU.mult)
                return e.tensor_copy(fscr[:, 2:3],
                                     comb[:, j * 1024 + (lh + 1) * 128 - 1:j * 1024 + (lh + 1) * 128])
            tC = q.on("vector", combv, after=[tAm, tWS])

        for jq in range(16):
            kmax = (jq + 1) * 128
            pb = 8 * (jq % 2)
            if jq < 8:
                buf = psS[:, 1024 * (jq % 2):1024 * (jq % 2) + 1024]
                if jq >= 2:
                    sdep = [tV1H[jq - 2]]
                else:
                    sdep = [tV1Head[15], tV1Head[14]]
            else:
                buf = psS
                sdep = [tV1H[6], tV1H[7]] if jq == 8 else [tV1H[jq - 1]]

            if jq < 8:
                def smm(e, jq=jq, kOFF=kOFF, buf=buf):
                    kx = (jq + 1) * 128
                    n = 0
                    while n * 512 < kx:
                        w = min(512, kx - n * 512)
                        ins = e.matmul(buf[:, n * 512:n * 512 + w],
                                       qs[:, jq * 128:(jq + 1) * 128],
                                       kT[:, kOFF + n * 512:kOFF + n * 512 + w],
                                       start=True, stop=True)
                        n += 1
                    return ins
                tS = q.on("tensor", smm, after=[tQS] + sdep)

                def vss(e, jq=jq, kmax=kmax, buf=buf, pb=pb):
                    po = (jq % 2) * S
                    if jq:
                        e.tensor_copy(SS[:, po:po + jq * 128], buf[:, 0:jq * 128])
                    e.scalar_tensor_tensor(SS[:, po + jq * 128:po + kmax], buf[:, jq * 128:kmax],
                                           1.0, maskt, ALU.mult, ALU.add)
                    e.reduce_max(cols[:, pb:pb + 1], SS[:, po:po + kmax], axis=AX, negate=True)
                    return e.tensor_copy(cols[:, pb + 1:pb + 2], cols[:, pb:pb + 1])
                tV1 = q.on("vector", vss, after=[tS, tExH[jq - 2] if jq >= 2 else None])
                tV1H[jq] = tV1
                tV1A[jq] = tV1
            else:
                sdepA = [tV1A[6]] if jq == 8 else [tV1A[jq - 1]]
                sdepB = [tV1H[7]] if jq == 8 else [tV1H[jq - 1]]

                def smmA(e, jq=jq, kOFF=kOFF, buf=buf):
                    for n in range(2):
                        ins = e.matmul(buf[:, n * 512:n * 512 + 512],
                                       qs[:, jq * 128:(jq + 1) * 128],
                                       kT[:, kOFF + n * 512:kOFF + n * 512 + 512],
                                       start=True, stop=True)
                    return ins
                tSA = q.on("tensor", smmA, after=[tQS] + sdepA)

                def smmB(e, jq=jq, kOFF=kOFF, buf=buf):
                    kx = (jq + 1) * 128
                    n = 2
                    while n * 512 < kx:
                        w = min(512, kx - n * 512)
                        ins = e.matmul(buf[:, n * 512:n * 512 + w],
                                       qs[:, jq * 128:(jq + 1) * 128],
                                       kT[:, kOFF + n * 512:kOFF + n * 512 + w],
                                       start=True, stop=True)
                        n += 1
                    return ins
                tSB = q.on("tensor", smmB, after=sdepB)

                def vssA(e, jq=jq, buf=buf):
                    po = (jq % 2) * S
                    e.tensor_copy(SS[:, po:po + 1024], buf[:, 0:1024])
                    return e.tensor_copy(fscr[:, 5:6], SS[:, po + 1023:po + 1024])
                tV1A[jq] = q.on("vector", vssA, after=[tSA, tExH[jq - 2]])

                def vssB(e, jq=jq, kmax=kmax, buf=buf, pb=pb):
                    po = (jq % 2) * S
                    if jq * 128 > 1024:
                        e.tensor_copy(SS[:, po + 1024:po + jq * 128], buf[:, 1024:jq * 128])
                    e.scalar_tensor_tensor(SS[:, po + jq * 128:po + kmax], buf[:, jq * 128:kmax],
                                           1.0, maskt, ALU.mult, ALU.add)
                    e.reduce_max(cols[:, pb:pb + 1], SS[:, po:po + kmax], axis=AX, negate=True)
                    return e.tensor_copy(cols[:, pb + 1:pb + 2], cols[:, pb:pb + 1])
                tV1 = q.on("vector", vssB, after=[tSB])
                tV1H[jq] = tV1

            def sexp(e, kmax=kmax, jq=jq, pb=pb):
                po = (jq % 2) * S
                e.activation(P[:, po:po + kmax], SS[:, po:po + kmax], AF.Exp,
                             bias=cols[:, pb:pb + 1], scale=1.0,
                             accum_out=cols[:, pb + 3:pb + 4])
                e.copy(fscr[:, 6:7], P[:, po + kmax - 1:po + kmax])
                return e.copy(fscr[:, 6:7], cols[:, pb + 3:pb + 4])
            tEx = q.on("scalar", sexp, after=[tV1, tTrpH[jq - 2] if jq >= 2 else tTrpH[14 + (jq % 2)]])
            tExH[jq] = tEx

            if jq:
                flush(jq - 1)
        flush(15)

    stores = emit_comb_out(q, comb, idt, psT, ctr, fscr, agin, tC)
    tAG = q.cc(lambda e: e.collective_compute(
        "AllGather", ALU.bypass, replica_groups=G["rg8"],
        ins=[agin.opt()], outs=[agout.opt()]),
        after=stores)
    return tAG



def emit_oproj_block(nc, q, pool, dr, G, apply_ln, agout_src, tAGsrc, woT_d, x_d,
                     lnw_d, yf_d, agin, agout):
    pool.reset()
    sb = pool.take
    cbT = sb("cbT_s", [128, 16 * 1024], BF16)
    woT = sb("woT_s", [128, 16 * H], BF16)
    lnw = sb("lnw_s", [128, H], F32)
    xres = sb("xres", [128, 2 * H], F32)
    xo = sb("xo", [128, H], F32)
    xc = sb("xc", [128, H], F32)
    scrap = sb("scrap", [128, H], F32)
    yf = sb("yf_s", [128, 2 * H], F32)
    rn = sb("rn_s", [128, H], BF16)
    xtT = sb("xtT", [128, 16 * 1024], BF16)
    cols = sb("cols", [128, 16], F32)
    fscr = sb("fscr", [128, 8], F32)
    psO = G["psBIG"]
    tr = XTrOut(q, G["idt"], G["psTr"], xtT, fscr, agin)

    def ldc(e, sem):
        for k in range(16):
            e.indirect_dma_start(
                out=cbT[:, k * 1024:(k + 1) * 1024], out_offset=None,
                in_=agout_src,
                in_offset=bass.IndirectOffsetOnAxis(G["idxB"][:, k:k + 1], axis=0),
            ).then_inc(sem, 16)
    tLD = q.on("gpsimd", ldc, after=[tAGsrc], ndma=16)

    def ldw(e, sem):
        for i in range(16):
            e.dma_start(out=woT[:, i * H:(i + 1) * H],
                        in_=woT_d[128 * i:128 * (i + 1), :]).then_inc(sem, 16)
        e.dma_start(out=lnw[:], in_=lnw_d.broadcast_to([128, H])).then_inc(sem, 16)
    tW = q.on("gpsimd", ldw, ndma=17)

    tXO = [None] * 8
    tStore = [None] * 8
    for jt in range(8):
        slot = jt % 2

        def ldx(e, sem, jt=jt, slot=slot):
            e.dma_start(out=xres[:, slot * H:(slot + 1) * H],
                        in_=x_d[128 * jt:128 * (jt + 1), :]).then_inc(sem, 16)
        tX = q.on("sync", ldx, after=[tXO[jt - 2] if jt >= 2 else None], ndma=1)

        def mm(e, jt=jt):
            for n in range(4):
                for i in range(16):
                    ins = e.matmul(psO[:, n * 512:(n + 1) * 512],
                                   cbT[:, i * 1024 + jt * 128:i * 1024 + (jt + 1) * 128],
                                   woT[:, i * H + n * 512:i * H + (n + 1) * 512],
                                   start=(i == 0), stop=(i == 15))
            return ins
        tM = q.on("tensor", mm, after=[tLD, tW, tXO[jt - 1] if jt >= 1 else None])

        def v_xo(e, slot=slot):
            e.scalar_tensor_tensor(xo[:], psO[:], 1.0, xres[:, slot * H:(slot + 1) * H],
                                   ALU.mult, ALU.add, accum_out=cols[:, 0:1])
            return e.tensor_copy(fscr[:, 1:2], cols[:, 0:1])
        tXO[jt] = q.on("vector", v_xo, after=[tM, tX])

        if apply_ln:
            def apply(e, slot=slot):
                e.activation(yf[:, slot * H:(slot + 1) * H], xc[:], AF.Copy,
                             scale=cols[:, 5:6])
                e.activation(scrap[:], yf[:, slot * H:(slot + 1) * H], AF.Square,
                             accum_out=cols[:, 8:9])
                return e.copy(fscr[:, 0:1], cols[:, 8:9])
            tA1 = ln_zigzag(q, tXO[jt], xo, xc, scrap, cols, fscr, 0, H, 1e-5, apply,
                            extra_after=[tStore[jt - 2] if jt >= 2 else None,
                                         tr.done[jt - 2] if jt >= 2 else None])
        else:
            def apply(e, slot=slot):
                e.copy(yf[:, slot * H:(slot + 1) * H], xo[:])
                e.activation(scrap[:], xo[:], AF.Square, accum_out=cols[:, 8:9])
                return e.copy(fscr[:, 0:1], cols[:, 8:9])
            tA1 = q.on("scalar", apply, after=[tXO[jt],
                                              tStore[jt - 2] if jt >= 2 else None,
                                              tr.done[jt - 2] if jt >= 2 else None])

        def v_rw(e):
            e.tensor_scalar(cols[:, 9:10], cols[:, 8:9], 1.0 / H, 1e-5, ALU.mult, ALU.add)
            return e.tensor_copy(fscr[:, 1:2], cols[:, 9:10])
        t6 = q.on("vector", v_rw, after=[tA1])

        def s_rs(e):
            e.activation(cols[:, 10:11], cols[:, 9:10], AF.Sqrt)
            return e.copy(fscr[:, 0:1], cols[:, 10:11])
        t7 = q.on("scalar", s_rs, after=[t6])

        def v_rr(e):
            e.reciprocal(cols[:, 11:12], cols[:, 10:11])
            return e.tensor_copy(fscr[:, 1:2], cols[:, 11:12])
        t8 = q.on("vector", v_rr, after=[t7])

        def s_rn(e, slot=slot):
            e.activation(scrap[:], yf[:, slot * H:(slot + 1) * H], AF.Copy,
                         scale=cols[:, 11:12])
            return e.copy(fscr[:, 0:1], scrap[:, H - 1:H])
        t9 = q.on("scalar", s_rn, after=[t8])

        def v_rn(e):
            e.tensor_tensor(rn[:], scrap[:], lnw[:], ALU.mult)
            return e.tensor_copy(fscr[:, 1:2], rn[:, H - 1:H])
        t10 = q.on("vector", v_rn, after=[t9, tW, tr.done[jt - 1] if jt >= 1 else None])

        def st(e, sem, jt=jt, slot=slot):
            e.dma_start(out=yf_d[128 * jt:128 * (jt + 1), :],
                        in_=yf[:, slot * H:(slot + 1) * H]).then_inc(sem, 16)
        tStore[jt] = q.on("sync", st, after=[t10], ndma=1)

        tr.add_tile(jt, rn[:], t10)

    tSt = tr.store()
    tAG = q.cc(lambda e: e.collective_compute(
        "AllGather", ALU.bypass, replica_groups=G["rg8"],
        ins=[agin.opt()], outs=[agout.opt()]),
        after=[tSt, tStore[6], tStore[7]])
    return tAG



def emit_mla_proj(nc, q, pool, dr, G):
    pool.reset()
    sb = pool.take
    xcol = sb("xcol", [128, 16 * 512], BF16)
    kvdb = sb("kvdb", [128, 16 * 512], BF16)
    qdb = sb("qdb", [128, 16 * 512], BF16)
    kvdT = sb("kvdT", [128, 4 * S], BF16)
    qdT = sb("qdT", [128, 4 * S], BF16)
    kub = sb("kub", [128, 4 * 512], BF16)
    qub = sb("qub", [128, 4 * 512], BF16)
    vub = sb("vub", [128, 4 * 1024], BF16)
    rkb = sb("rkb", [128, 16 * 512], BF16)
    rqb = sb("rqb", [128, 4 * 512], BF16)
    ktot = sb("ktot", [128, 16 * 1024], BF16)
    qtot = sb("qtot", [128, 16 * 1024], BF16)
    vtot = sb("vtot", [128, 2 * 1024], BF16)
    cosm = sb("cosm_s", [128, 16 * 32], F32)
    sinm = sb("sinm_s", [128, 16 * 32], F32)
    rf = sb("rf", [128, 512], F32)
    rs = sb("rs", [128, 64], F32)
    orow = sb("orow", [128, S], BF16)
    fscr = sb("fscr", [128, 8], F32)
    idt = G["idt"]
    psA = G["ps3"]
    psB = G["ps2"][:, 0:512]
    psV = G["psBIG"][:, 0:1024]
    psT = G["psTr"]

    def ld(e, sem):
        e.dma_start(out=cosm[:], in_=dr["mcos"]).then_inc(sem, 16)
        e.dma_start(out=sinm[:], in_=dr["msin"]).then_inc(sem, 16)
    tLD = q.on("sync", ld, ndma=2)

    def ldw(e, sem):
        for i in range(16):
            e.dma_start(out=kvdb[:, i * 512:(i + 1) * 512],
                        in_=dr["kvd"][128 * i:128 * (i + 1), :]).then_inc(sem, 16)
        for i in range(16):
            e.dma_start(out=qdb[:, i * 512:(i + 1) * 512],
                        in_=dr["qd"][128 * i:128 * (i + 1), :]).then_inc(sem, 16)
        for i in range(16):
            e.dma_start(out=rkb[:, i * 512:(i + 1) * 512],
                        in_=dr["rk"][128 * i:128 * (i + 1), :]).then_inc(sem, 16)
        for lt in range(4):
            e.dma_start(out=kub[:, lt * 512:(lt + 1) * 512],
                        in_=dr["ku"][128 * lt:128 * (lt + 1), :]).then_inc(sem, 16)
            e.dma_start(out=qub[:, lt * 512:(lt + 1) * 512],
                        in_=dr["qu"][128 * lt:128 * (lt + 1), :]).then_inc(sem, 16)
            e.dma_start(out=vub[:, lt * 1024:(lt + 1) * 1024],
                        in_=dr["vu"][128 * lt:128 * (lt + 1), :]).then_inc(sem, 16)
            e.dma_start(out=rqb[:, lt * 512:(lt + 1) * 512],
                        in_=dr["rq"][128 * lt:128 * (lt + 1), :]).then_inc(sem, 16)
    tW = q.on("gpsimd", ldw, ndma=64)

    tXc = None
    tEvPrev = None
    tRopeK = None
    tKrLast = None
    for sc in range(4):
        def ldxc(e, sem, sc=sc):
            for i in range(16):
                e.indirect_dma_start(
                    out=xcol[:, i * 512:(i + 1) * 512], out_offset=None,
                    in_=dr["agout_xn1"],
                    in_offset=bass.IndirectOffsetOnAxis(
                        G["idxPair"][:, 2 * i + sc // 2:2 * i + sc // 2 + 1], axis=0),
                    element_offset=512 * (sc % 2),
                ).then_inc(sem, 16)
        tXc = q.on("gpsimd", ldxc, after=[tEvPrev, tKrLast, dr["tAGxn1"] if sc == 0 else None], ndma=16)

        for which, (wsrc, dstT) in enumerate([(kvdb, kvdT), (qdb, qdT)]):
            for lt in range(4):
                def dmm(e, sc=sc, lt=lt, wsrc=wsrc):
                    for i in range(16):
                        ins = e.matmul(psA[:, 0:512],
                                       wsrc[:, i * 512 + lt * 128:i * 512 + (lt + 1) * 128],
                                       xcol[:, i * 512:(i + 1) * 512],
                                       start=(i == 0), stop=(i == 15))
                    return ins
                tM = q.on("tensor", dmm, after=[tXc, tW, tEvPrev])

                def ev(e, sc=sc, lt=lt, dstT=dstT):
                    e.tensor_copy(dstT[:, lt * S + sc * 512:lt * S + (sc + 1) * 512], psA[:, 0:512])
                    return e.tensor_copy(fscr[:, 0:1],
                                         dstT[:, lt * S + (sc + 1) * 512 - 1:lt * S + (sc + 1) * 512])
                tEvPrev = q.on("vector", ev, after=[tM])

        for jj in range(4):
            j = sc * 4 + jj

            def krmm(e, jj=jj):
                for i in range(16):
                    ins = e.matmul(psB[:, 0:512],
                                   xcol[:, i * 512 + jj * 128:i * 512 + (jj + 1) * 128],
                                   rkb[:, i * 512:(i + 1) * 512],
                                   start=(i == 0), stop=(i == 15))
                return ins
            tKr = q.on("tensor", krmm, after=[tXc, tW, tRopeK])
            tKrLast = tKr

            def evkr(e):
                e.tensor_copy(rf[:], psB[:, 0:512])
                return e.tensor_copy(fscr[:, 1:2], rf[:, 511:512])
            tEvK = q.on("vector", evkr, after=[tKr])

            def ropek(e, j=j):
                c = cosm[:, j * 32:(j + 1) * 32]
                s = sinm[:, j * 32:(j + 1) * 32]
                for n in range(8):
                    x1 = rf[:, n * 64:n * 64 + 32]
                    x2 = rf[:, n * 64 + 32:n * 64 + 64]
                    e.tensor_tensor(rs[:, 0:32], x1, c, ALU.mult)
                    e.tensor_tensor(rs[:, 32:64], x2, s, ALU.mult)
                    e.tensor_tensor(ktot[:, j * 1024 + n * 128 + 64:j * 1024 + n * 128 + 96],
                                    rs[:, 0:32], rs[:, 32:64], ALU.subtract)
                    e.tensor_tensor(rs[:, 0:32], x2, c, ALU.mult)
                    e.tensor_tensor(rs[:, 32:64], x1, s, ALU.mult)
                    e.tensor_tensor(ktot[:, j * 1024 + n * 128 + 96:j * 1024 + (n + 1) * 128],
                                    rs[:, 0:32], rs[:, 32:64], ALU.add)
                return e.tensor_copy(fscr[:, 1:2], ktot[:, (j + 1) * 1024 - 1:(j + 1) * 1024])
            tRopeK = q.on("vector", ropek, after=[tEvK, tLD])

    tVst = [None, None]
    tPrevA = tRopeK
    tPrevB = tRopeK
    tPrevV = None
    for j in range(16):
        def kpmm(e, j=j):
            for lt in range(4):
                ins = e.matmul(psA[:, 0:512],
                               kvdT[:, lt * S + j * 128:lt * S + (j + 1) * 128],
                               kub[:, lt * 512:(lt + 1) * 512],
                               start=(lt == 0), stop=(lt == 3))
            return ins
        tKp = q.on("tensor", kpmm, after=[tEvPrev, tPrevA])

        def evkp(e, j=j):
            for n in range(8):
                e.tensor_copy(ktot[:, j * 1024 + n * 128:j * 1024 + n * 128 + 64],
                              psA[:, n * 64:(n + 1) * 64])
            return e.tensor_copy(fscr[:, 2:3], ktot[:, j * 1024 + 7 * 128 + 63:j * 1024 + 7 * 128 + 64])
        tPrevA = q.on("vector", evkp, after=[tKp])

        def qpmm(e, j=j):
            for lt in range(4):
                ins = e.matmul(psB[:, 0:512],
                               qdT[:, lt * S + j * 128:lt * S + (j + 1) * 128],
                               qub[:, lt * 512:(lt + 1) * 512],
                               start=(lt == 0), stop=(lt == 3))
            return ins
        tQp = q.on("tensor", qpmm, after=[tPrevB])

        def evqp(e, j=j):
            for n in range(8):
                e.tensor_copy(qtot[:, j * 1024 + n * 128:j * 1024 + n * 128 + 64],
                              psB[:, n * 64:(n + 1) * 64])
            return e.tensor_copy(fscr[:, 3:4], qtot[:, j * 1024 + 7 * 128 + 63:j * 1024 + 7 * 128 + 64])
        tPrevB = q.on("vector", evqp, after=[tQp])

        def qrmm(e, j=j):
            for lt in range(4):
                ins = e.matmul(psB[:, 0:512],
                               qdT[:, lt * S + j * 128:lt * S + (j + 1) * 128],
                               rqb[:, lt * 512:(lt + 1) * 512],
                               start=(lt == 0), stop=(lt == 3))
            return ins
        tQr = q.on("tensor", qrmm, after=[tPrevB])

        def evqr(e):
            e.tensor_copy(rf[:], psB[:, 0:512])
            return e.tensor_copy(fscr[:, 1:2], rf[:, 511:512])
        tEvQr = q.on("vector", evqr, after=[tQr])

        def ropeq(e, j=j):
            c = cosm[:, j * 32:(j + 1) * 32]
            s = sinm[:, j * 32:(j + 1) * 32]
            for n in range(8):
                x1 = rf[:, n * 64:n * 64 + 32]
                x2 = rf[:, n * 64 + 32:n * 64 + 64]
                e.tensor_tensor(rs[:, 0:32], x1, c, ALU.mult)
                e.tensor_tensor(rs[:, 32:64], x2, s, ALU.mult)
                e.tensor_tensor(qtot[:, j * 1024 + n * 128 + 64:j * 1024 + n * 128 + 96],
                                rs[:, 0:32], rs[:, 32:64], ALU.subtract)
                e.tensor_tensor(rs[:, 0:32], x2, c, ALU.mult)
                e.tensor_tensor(rs[:, 32:64], x1, s, ALU.mult)
                e.tensor_tensor(qtot[:, j * 1024 + n * 128 + 96:j * 1024 + (n + 1) * 128],
                                rs[:, 0:32], rs[:, 32:64], ALU.add)
            return e.tensor_copy(fscr[:, 1:2], qtot[:, (j + 1) * 1024 - 1:(j + 1) * 1024])
        tPrevB = q.on("vector", ropeq, after=[tEvQr, tLD])

        def vmm(e, j=j):
            for n in range(2):
                for lt in range(4):
                    ins = e.matmul(psV[:, n * 512:(n + 1) * 512],
                                   kvdT[:, lt * S + j * 128:lt * S + (j + 1) * 128],
                                   vub[:, lt * 1024 + n * 512:lt * 1024 + (n + 1) * 512],
                                   start=(lt == 0), stop=(lt == 3))
            return ins
        tV = q.on("tensor", vmm, after=[tEvPrev, tPrevV])

        def vev(e, j=j):
            e.tensor_copy(vtot[:, (j % 2) * 1024:(j % 2 + 1) * 1024], psV[:])
            return e.tensor_copy(fscr[:, 4:5], vtot[:, (j % 2) * 1024 + 1023:(j % 2) * 1024 + 1024])
        tPrevV = q.on("vector", vev, after=[tV, tVst[j % 2]])

        def stv(e, sem, j=j):
            e.dma_start(out=dr["vt2"][128 * j:128 * (j + 1), :],
                        in_=vtot[:, (j % 2) * 1024:(j % 2 + 1) * 1024]).then_inc(sem, 16)
        tVst[j % 2] = q.on("sync", stv, after=[tPrevV], ndma=1)

    tTr = None
    tQst = None
    for src_t, dst_d, tok0 in ((qtot, dr["qT2"], tPrevB), (ktot, dr["kT2"], tPrevA)):
        for a in range(8):
            for half in range(2):
                def trq(e, a=a, half=half, src_t=src_t):
                    for jj in range(8):
                        j = 8 * half + jj
                        ins = e.transpose(psT[:, jj * 128:(jj + 1) * 128],
                                          src_t[:, j * 1024 + a * 128:j * 1024 + (a + 1) * 128], idt)
                    return ins
                tt = q.on("tensor", trq, after=[tok0, tRopeK, tTr])

                eng = "scalar" if half == 0 else "vector"

                def evq(e, half=half, eng=eng):
                    cp = e.copy if eng == "scalar" else e.tensor_copy
                    for jj in range(8):
                        cp(orow[:, half * 1024 + jj * 128:half * 1024 + (jj + 1) * 128],
                           psT[:, jj * 128:(jj + 1) * 128])
                    return cp(fscr[:, 2:3], orow[:, half * 1024 + 1023:half * 1024 + 1024])
                tTr = q.on(eng, evq, after=[tt, tTr, tQst if half == 0 else None])

            def stq(e, sem, a=a, dst_d=dst_d):
                e.dma_start(out=dst_d[a * 128:(a + 1) * 128, :], in_=orow[:]).then_inc(sem, 16)
            tQst = q.on("sync", stq, after=[tTr], ndma=1)



def emit_moe(nc, q, pool, dr, G):
    pool.reset()
    sb = pool.take
    xcb = sb("xcb", [128, 16 * 512], BF16)
    wgb = sb("wgb", [128, 16 * 1024], BF16)
    wub = sb("wub", [128, 16 * 1024], BF16)
    wdb = sb("wdb", [128, 8 * H], BF16)
    rwb = sb("rwb", [128, 16 * 7], BF16)
    rbb = sb("rbb", [128, 7], F32)
    wselc = sb("wselc", [128, 2], F32)
    hT = sb("hT", [128, 8 * 512], BF16)
    sgf = sb("sgf", [128, 512], F32)
    t1f = sb("t1f", [128, 512], F32)
    lgAll = sb("lgAll", [128, 448], F32)
    prAll = sb("prAll", [128, 448], F32)
    m1rep = sb("m1rep", [128, 448], F32)
    eqw = sb("eqw", [128, 448], F32)
    pmAll = sb("pmAll", [128, 448], F32)
    m1 = sb("m1", [128, 64], F32)
    m2 = sb("m2", [128, 64], F32)
    dc = sb("dc", [128, 64], F32)
    sgc = sb("sgc", [128, 64], F32)
    u1 = sb("u1", [128, 64], F32)
    nn = sb("nn", [128, 64], F32)
    s12 = sb("s12", [128, 64], F32)
    invc = sb("invc", [128, 64], F32)
    wAll = sb("wAll", [128, 64], F32)
    dscr = sb("dscr", [128, 512], F32)
    obuf = sb("obuf", [128, 2 * H], BF16)
    fscr = sb("fscr", [128, 64], F32)
    psR = G["ps2"][:, 0:512]
    psG = G["ps2"][:, 512:1024]
    psU = G["ps3"]
    psD = G["psBIG"]
    xsrc = dr["agout_xn2"]

    def xld(e, sem, tc):
        r, u = tc // 2, tc % 2
        for i in range(16):
            e.dma_start(out=xcb[:, i * 512:(i + 1) * 512],
                        in_=xsrc[2048 * r + 128 * i:2048 * r + 128 * (i + 1),
                                 512 * u:512 * (u + 1)]).then_inc(sem, 16)

    def ld(e, sem):
        e.dma_start(out=rbb[:], in_=dr["mrb"].broadcast_to([128, 7])).then_inc(sem, 16)
        e.dma_start(out=wselc[:], in_=dr["mwsel"].broadcast_to([128, 2])).then_inc(sem, 16)
    tLD = q.on("sync", ld, ndma=2)

    def ldw(e, sem):
        for i in range(16):
            e.dma_start(out=rwb[:, i * 7:(i + 1) * 7],
                        in_=dr["mrw"][128 * i:128 * (i + 1), :]).then_inc(sem, 16)
        for i in range(16):
            e.dma_start(out=wgb[:, i * 1024:(i + 1) * 1024],
                        in_=dr["mwg"][128 * i:128 * (i + 1), :]).then_inc(sem, 16)
        for i in range(16):
            e.dma_start(out=wub[:, i * 1024:(i + 1) * 1024],
                        in_=dr["mwu"][128 * i:128 * (i + 1), :]).then_inc(sem, 16)
        for i in range(8):
            e.dma_start(out=wdb[:, i * H:(i + 1) * H],
                        in_=dr["mwd"][128 * i:128 * (i + 1), :]).then_inc(sem, 16)
    tW = q.on("gpsimd", ldw, ndma=56)

    tXc = None
    tREv = None
    for tc in range(16):
        tXc = q.on("sync", lambda e, sem, tc=tc: xld(e, sem, tc),
                   after=[tREv, dr["tAGxn2"] if tc == 0 else None], ndma=16)

        for u in range(4):
            tt = tc * 4 + u

            def rmm(e, u=u):
                for i in range(16):
                    ins = e.matmul(psR[:, (u % 2) * 128:(u % 2) * 128 + 7],
                                   xcb[:, i * 512 + u * 128:i * 512 + (u + 1) * 128],
                                   rwb[:, i * 7:(i + 1) * 7],
                                   start=(i == 0), stop=(i == 15))
                return ins
            tRm = q.on("tensor", rmm, after=[tXc, tW, tREv])

            def rev(e, tt=tt, u=u):
                e.scalar_tensor_tensor(
                    lgAll[:].rearrange("p (s g) -> p s g", s=7)[:, :, tt:tt + 1],
                    psR[:, (u % 2) * 128:(u % 2) * 128 + 7], 1.0, rbb[:],
                    ALU.mult, ALU.add)
                return e.tensor_copy(fscr[:, 0:1], lgAll[:, 384 + tt:385 + tt])
            tREv = q.on("vector", rev, after=[tRm])

    def sprob(e):
        e.activation(prAll[:], lgAll[:], AF.Sigmoid)
        return e.copy(fscr[:, 0:32], prAll[:, 416:448])
    tS0 = q.on("scalar", sprob, after=[tREv])

    def vm1(e):
        e.reduce_max(m1[:], prAll[:].rearrange("p (s g) -> p g s", g=64), axis=AX)
        return e.tensor_copy(fscr[:, 0:64], m1[:])
    tM1 = q.on("vector", vm1, after=[tS0])

    def repm1(e, sem):
        e.dma_start(out=m1rep[:].rearrange("p (s g) -> p s g", s=7),
                    in_=m1[:].unsqueeze(1).broadcast_to([128, 7, 64])).then_inc(sem, 16)
    tRep = q.on("gpsimd", repm1, after=[tM1], ndma=1)

    def vm2(e):
        e.tensor_tensor(eqw[:], prAll[:], m1rep[:], ALU.is_equal)
        e.scalar_tensor_tensor(pmAll[:], eqw[:], -1e30, prAll[:], ALU.mult, ALU.add)
        e.reduce_max(m2[:], pmAll[:].rearrange("p (s g) -> p g s", g=64), axis=AX)
        return e.tensor_copy(fscr[:, 0:64], m2[:])
    tM2 = q.on("vector", vm2, after=[tRep])

    def sm2(e):
        e.copy(dscr[:, 0:64], m2[:])
        return e.copy(fscr[:, 0:64], dscr[:, 0:64])
    tM2h = q.on("scalar", sm2, after=[tM2])

    def vdc(e):
        e.scalar_tensor_tensor(dc[:], dscr[:, 0:64], -0.9999999, prAll[:, 0:64],
                               ALU.mult, ALU.add)
        e.tensor_tensor(s12[:], m1[:], dscr[:, 0:64], ALU.add)
        return e.tensor_copy(fscr[:, 0:64], dc[:])
    tDc = q.on("vector", vdc, after=[tM2h])

    def ssg(e):
        e.activation(sgc[:], dc[:], AF.Sign)
        return e.copy(fscr[:, 0:64], sgc[:])
    tSg = q.on("scalar", ssg, after=[tDc])

    def vwf(e):
        e.reciprocal(invc[:], s12[:])
        e.tensor_tensor(u1[:], sgc[:], prAll[:, 0:64], ALU.mult)
        e.tensor_copy(dscr[:], dscr[:])
        e.tensor_tensor(nn[:], u1[:], prAll[:, 0:64], ALU.add)
        e.tensor_copy(dscr[:], dscr[:])
        e.tensor_tensor(nn[:], nn[:], invc[:], ALU.mult)
        e.tensor_copy(dscr[:], dscr[:])
        e.tensor_scalar(wAll[:], nn[:], wselc[:, 1:2], wselc[:, 0:1], ALU.mult, ALU.add)
        return e.tensor_copy(fscr[:, 0:64], wAll[:])
    tWfin = q.on("vector", vwf, after=[tSg])

    tDevPrev = [None] * 2
    tDevLast = None
    tOst = [None] * 2
    tGU = None
    tHt = None
    for tc in range(16):
        tXc = q.on("sync", lambda e, sem, tc=tc: xld(e, sem, tc),
                   after=[tGU, tWfin if tc == 0 else None], ndma=16)

        tHtPar = [None, None]
        for it in range(8):
            pg = psG if it % 2 == 0 else psR

            def gmm(e, it=it, pg=pg):
                for i in range(16):
                    ins = e.matmul(pg[:, 0:512],
                                   wgb[:, i * 1024 + it * 128:i * 1024 + (it + 1) * 128],
                                   xcb[:, i * 512:(i + 1) * 512],
                                   start=(i == 0), stop=(i == 15))
                return ins
            tGm = q.on("tensor", gmm, after=[tXc, tW, tHtPar[it % 2]])

            def umm(e, it=it):
                for i in range(16):
                    ins = e.matmul(psU[:, 0:512],
                                   wub[:, i * 1024 + it * 128:i * 1024 + (it + 1) * 128],
                                   xcb[:, i * 512:(i + 1) * 512],
                                   start=(i == 0), stop=(i == 15))
                return ins
            tUm = q.on("tensor", umm, after=[tHt])
            tGU = tUm

            def ssig(e, pg=pg):
                e.activation(sgf[:], pg[:, 0:512], AF.Sigmoid)
                return e.copy(fscr[:, 0:1], sgf[:, 511:512])
            tSgs = q.on("scalar", ssig, after=[tGm])

            def vht(e, it=it, pg=pg):
                e.tensor_tensor(t1f[:], sgf[:], pg[:, 0:512], ALU.mult)
                e.scalar_tensor_tensor(hT[:, it * 512:(it + 1) * 512], t1f[:], 0.125,
                                       psU[:, 0:512], ALU.mult, ALU.mult)
                return e.tensor_copy(fscr[:, 0:1], hT[:, (it + 1) * 512 - 1:(it + 1) * 512])
            tHt = q.on("vector", vht, after=[tSgs, tUm])
            tHtPar[it % 2] = tHt

        for u in range(4):
            slot = u % 2
            tt = tc * 4 + u

            def dmm(e, u=u):
                for n in range(4):
                    for i in range(8):
                        ins = e.matmul(psD[:, n * 512:(n + 1) * 512],
                                       hT[:, i * 512 + u * 128:i * 512 + (u + 1) * 128],
                                       wdb[:, i * H + n * 512:i * H + (n + 1) * 512],
                                       start=(i == 0), stop=(i == 7))
                return ins
            tDm = q.on("tensor", dmm, after=[tHt, tDevLast])

            def dev(e, tt=tt, slot=slot):
                e.tensor_scalar(obuf[:, slot * H:(slot + 1) * H], psD[:], wAll[:, tt:tt + 1],
                                None, ALU.mult)
                return e.tensor_copy(fscr[:, 0:1], obuf[:, slot * H + H - 1:slot * H + H])
            tDevPrev[slot] = q.on("vector", dev, after=[tDm, tWfin, tOst[slot]])
            tDevLast = tDevPrev[slot]

            def ost(e, sem, tc=tc, u=u, slot=slot):
                e.dma_start(out=dr["part2"][tc * 512 + u * 128:tc * 512 + (u + 1) * 128, :],
                            in_=obuf[:, slot * H:(slot + 1) * H]).then_inc(sem, 16)
            tOst[slot] = q.on("sync", ost, after=[tDevPrev[slot]], ndma=1)

    tRS = q.cc(lambda e: e.collective_compute(
        "ReduceScatter", ALU.add, replica_groups=G["rg8"],
        ins=[dr["part2"].opt()], outs=[dr["rs2"].opt()]),
        after=[tOst[0], tOst[1]])
    return tRS



def emit_final(nc, q, pool, dr, G):
    pool.reset()
    sb = pool.take
    pt = sb("pt", [128, 2 * H], BF16)
    xres = sb("xres", [128, 2 * H], F32)
    xres2 = sb("xres2", [128, 2 * H], F32)
    acc = sb("acc", [128, H], F32)
    xo = sb("xo", [128, H], F32)
    xc = sb("xc", [128, H], F32)
    scrap = sb("scrap", [128, H], F32)
    yf = sb("yf_s", [128, 2 * H], F32)
    cols = sb("cols", [128, 8], F32)
    fscr = sb("fscr", [128, 8], F32)

    tSum = [None] * 8
    tStore = [None] * 8
    for jt in range(8):
        slot = jt % 2

        def ld(e, sem, jt=jt, slot=slot):
            e.dma_start(out=pt[:, slot * H:(slot + 1) * H],
                        in_=dr["rs2"][128 * jt:128 * (jt + 1), :]).then_inc(sem, 16)
            e.dma_start(out=xres[:, slot * H:(slot + 1) * H],
                        in_=dr["hf"][128 * jt:128 * (jt + 1), :]).then_inc(sem, 16)
            e.dma_start(out=xres2[:, slot * H:(slot + 1) * H],
                        in_=dr["x3f"][128 * jt:128 * (jt + 1), :]).then_inc(sem, 16)
        tL = q.on("sync", ld, after=[tSum[jt - 2] if jt >= 2 else None,
                                     dr["tRS2"] if jt == 0 else None], ndma=3)

        def vsum(e, slot=slot):
            e.scalar_tensor_tensor(acc[:], pt[:, slot * H:(slot + 1) * H], 1.0,
                                   xres2[:, slot * H:(slot + 1) * H], ALU.mult, ALU.add)
            e.scalar_tensor_tensor(xo[:], acc[:], 1.0, xres[:, slot * H:(slot + 1) * H],
                                   ALU.mult, ALU.add, accum_out=cols[:, 0:1])
            return e.tensor_copy(fscr[:, 1:2], cols[:, 0:1])
        tSum[jt] = q.on("vector", vsum, after=[tL])

        def apply(e, jt=jt, slot=slot):
            e.activation(yf[:, slot * H:(slot + 1) * H], xc[:], AF.Copy, scale=cols[:, 5:6])
            return e.copy(fscr[:, 0:1], yf[:, slot * H + H - 1:slot * H + H])
        tAPP = ln_zigzag(q, tSum[jt], xo, xc, scrap, cols, fscr, 0, H, 1e-5, apply,
                         extra_after=[tStore[jt - 2] if jt >= 2 else None])

        def st(e, sem, jt=jt, slot=slot):
            e.dma_start(out=dr["out"][128 * jt:128 * (jt + 1), :],
                        in_=yf[:, slot * H:(slot + 1) * H]).then_inc(sem, 16)
        tStore[jt] = q.on("sync", st, after=[tAPP], ndma=1)

    return [tStore[6], tStore[7]]



def build_fused(dbg=()):
    nc = bass.Bass("TRN2", target_bir_lowering=False, debug=False)
    dr = {}

    def din(name, shape, dt):
        dr[name] = nc.dram_tensor(name, shape, dt, kind="ExternalInput").ap()

    def dint(name, shape, dt):
        kind = "ExternalOutput" if name in dbg else "Internal"
        kw = {"addr_space": "Shared"} if (name.startswith("agout") and name not in dbg) else {}
        dr[name] = nc.dram_tensor(name, shape, dt, kind=kind, **kw).ap()

    din("a1xT", [H, S], F32)
    din("a1wq", [H, 1024], F32)
    din("a1wk", [H, 1024], F32)
    din("a1wv", [H, 1024], F32)
    din("l0cos", [128, 4 * S], F32)
    din("l0sin", [128, 4 * S], F32)
    din("gate", [1, 8], F32)
    din("identI", [128, 128], BF16)
    din("maskI", [128, 128], F32)
    din("idxBI", [128, 16], U32)
    din("idxPairI", [128, 32], U32)
    din("bwoT", [H, H], F32)
    din("bx", [1024, H], F32)
    din("tg", [1024, 2048], U8)
    din("tu", [1024, 2048], U8)
    din("td", [1024, 2048], U8)
    din("gsT", [64, 2048], F32)
    din("usT", [64, 2048], F32)
    din("dsT", [64, 2048], F32)
    din("e1wq", [H, 1024], F32)
    din("e1wk", [H, 256], F32)
    din("e1wv", [H, 256], F32)
    din("qg", [1, 8], F32)
    din("l1cos", [128, 16 * 64], F32)
    din("l1sin", [128, 16 * 64], F32)
    din("fwoT", [H, H], F32)
    din("ln1w", [1, H], F32)
    din("kvd", [H, 512], F32)
    din("qd", [H, 512], F32)
    din("ku", [512, 512], F32)
    din("qu", [512, 512], F32)
    din("vu", [512, 1024], F32)
    din("rk", [H, 512], F32)
    din("rq", [512, 512], F32)
    din("mcos", [128, 16 * 32], F32)
    din("msin", [128, 16 * 32], F32)
    din("hwoT", [H, H], F32)
    din("ln2w", [1, H], F32)
    din("mwg", [H, 1024], F32)
    din("mwu", [H, 1024], F32)
    din("mwd", [1024, H], F32)
    din("mrw", [H, 7], F32)
    din("mrb", [1, 7], F32)
    din("mwsel", [1, 2], F32)

    dint("qT0", [1024, S], BF16)
    dint("kT0", [1024, S], BF16)
    dint("vt0", [S, 1024], BF16)
    dint("agin_comb", [2048, 1024], BF16)
    dint("agout_comb", [16384, 1024], BF16)
    dint("x1f", [1024, H], F32)
    dint("agin_x1", [2048, 1024], BF16)
    dint("agout_x1", [16384, 1024], BF16)
    dint("inter", [2048, TT_], BF16)
    dint("part1", [TT_, H], BF16)
    dint("rs1", [1024, H], BF16)
    dint("x2f", [1024, H], F32)
    dint("agin_x2", [2048, 1024], BF16)
    dint("agout_x2", [16384, 1024], BF16)
    dint("qT1", [1024, S], BF16)
    dint("kT1", [256, S], BF16)
    dint("vt1", [S, 256], BF16)
    dint("agin_y1", [2048, 1024], BF16)
    dint("agout_y1", [16384, 1024], BF16)
    dint("x3f", [1024, H], F32)
    dint("agin_n1", [2048, 1024], BF16)
    dint("agout_xn1", [16384, 1024], BF16)
    dint("qT2", [1024, S], BF16)
    dint("kT2", [1024, S], BF16)
    dint("vt2", [S, 1024], BF16)
    dint("agin_y2", [2048, 1024], BF16)
    dint("agout_y2", [16384, 1024], BF16)
    dint("hf", [1024, H], F32)
    dint("agin_n2", [2048, 1024], BF16)
    dint("agout_xn2", [16384, 1024], BF16)
    dint("part2", [TT_, H], BF16)
    dint("rs2", [1024, H], BF16)
    dr["out"] = nc.dram_tensor("out", [1024, H], F32, kind="ExternalOutput").ap()

    with ExitStack() as es:
        idt = es.enter_context(nc.sbuf_tensor("idt_c", [128, 128], BF16))
        maskt = es.enter_context(nc.sbuf_tensor("maskt_c", [128, 128], F32))
        idxB = es.enter_context(nc.sbuf_tensor("idxB_c", [128, 16], U32))
        idxPair = es.enter_context(nc.sbuf_tensor("idxPair_c", [128, 32], U32))
        psBIG = es.enter_context(nc.psum_tensor("psBIG", [128, 2048], F32))
        ps2 = es.enter_context(nc.psum_tensor("ps2", [128, 1024], F32))
        ps3 = es.enter_context(nc.psum_tensor("ps3", [128, 512], F32))
        psTr = es.enter_context(nc.psum_tensor("psTr", [128, 1024], BF16))
        pool = Pool(nc, es, 211456)
        block = es.enter_context(nc.Block())
        q = Seq(es, nc)

        G = {
            "idt": idt[:], "maskt": maskt[:], "idxB": idxB[:], "idxPair": idxPair[:],
            "psBIG": psBIG[:], "ps2": ps2[:], "ps3": ps3[:], "psTr": psTr[:],
            "rg8": [list(range(NCORES))],
        }

        def ldconst(e, sem):
            e.dma_start(out=idt[:], in_=dr["identI"]).then_inc(sem, 16)
            e.dma_start(out=maskt[:], in_=dr["maskI"]).then_inc(sem, 16)
            e.dma_start(out=idxB[:], in_=dr["idxBI"]).then_inc(sem, 16)
            e.dma_start(out=idxPair[:], in_=dr["idxPairI"]).then_inc(sem, 16)
        q.on("sync", ldconst, ndma=4)
        q.new_stage()

        emit_l0_qkv(nc, q, pool, dr, G)
        q.new_stage()
        dr["tAGcomb"] = emit_l0_attn(nc, q, pool, dr, G)
        q.new_stage()
        dr["tAGx1"] = emit_oproj_ln(nc, q, pool, dr, G)
        q.new_stage()
        dr["tRS1"] = emit_l0_ffn(nc, q, pool, dr, G)
        q.new_stage()
        dr["tAGx2"] = emit_postffn_ln(nc, q, pool, dr, G)
        q.new_stage()
        emit_l1_qkv(nc, q, pool, dr, G)
        q.new_stage()
        tAGy1 = emit_causal_attn(nc, q, pool, dr, G, 256, 4,
                                 dr["qT1"], dr["kT1"], dr["vt1"],
                                 dr["agin_y1"], dr["agout_y1"])
        q.new_stage()
        dr["tAGxn1"] = emit_oproj_block(nc, q, pool, dr, G, True,
                                        dr["agout_y1"], tAGy1, dr["fwoT"], dr["x2f"],
                                        dr["ln1w"], dr["x3f"], dr["agin_n1"], dr["agout_xn1"])
        q.new_stage()
        emit_mla_proj(nc, q, pool, dr, G)
        q.new_stage()
        tAGy2 = emit_causal_attn(nc, q, pool, dr, G, 1024, 1,
                                 dr["qT2"], dr["kT2"], dr["vt2"],
                                 dr["agin_y2"], dr["agout_y2"])
        q.new_stage()
        tAGxn2 = emit_oproj_block(nc, q, pool, dr, G, False,
                                  dr["agout_y2"], tAGy2, dr["hwoT"], dr["x3f"],
                                  dr["ln2w"], dr["hf"], dr["agin_n2"], dr["agout_xn2"])
        dr["tAGxn2"] = tAGxn2
        q.new_stage()
        dr["tRS2"] = emit_moe(nc, q, pool, dr, G)
        q.new_stage()
        stores = emit_final(nc, q, pool, dr, G)

        q.emit(block)
        q.finish(block, stores)
    return nc



def _run(nc, in_maps):
    return run_bass_kernel_spmd(nc, in_maps, core_ids=list(range(NCORES)))


def _stage_inputs(inputs):
    x = np.asarray(inputs["L0_attn_hidden_states"], np.float32)
    wqT0 = np.ascontiguousarray(np.asarray(inputs["L0_attn_Wq"], np.float32).T)
    wkT0 = np.ascontiguousarray(np.asarray(inputs["L0_attn_Wk"], np.float32).T)
    wvT0 = np.ascontiguousarray(np.asarray(inputs["L0_attn_Wv"], np.float32).T)
    woT0 = np.ascontiguousarray(np.asarray(inputs["L0_attn_Wo"], np.float32).T)
    gateq = np.asarray(inputs["L0_attn_gate"], np.float32).reshape(1, 16)
    cos0, sin0 = _l0_costab()
    cos0T = np.ascontiguousarray(cos0.T)
    sin0T = np.ascontiguousarray(sin0.T)
    pn = _pi_n()
    gp = np.asarray(inputs["L0_ffn_gate_W_packed"])
    up = np.asarray(inputs["L0_ffn_up_W_packed"])
    dp = np.asarray(inputs["L0_ffn_down_W_packed"])
    gs = np.asarray(inputs["L0_ffn_gate_scales"], np.float32)
    us = np.asarray(inputs["L0_ffn_up_scales"], np.float32)
    ds = np.asarray(inputs["L0_ffn_down_scales"], np.float32)
    bg = np.ascontiguousarray(gp.view(np.uint8).reshape(16384, 1024, 4)[:, :, 0][pn].T)
    bu = np.ascontiguousarray(up.view(np.uint8).reshape(16384, 1024, 4)[:, :, 0][pn].T)
    bd = np.ascontiguousarray(dp.view(np.uint8).reshape(2048, 8192, 4)[:, :, 0].T)
    gsT = np.ascontiguousarray(gs[pn].T)
    usT = np.ascontiguousarray(us[pn].T)
    dsT = np.ascontiguousarray(ds.T)
    wqT1 = np.ascontiguousarray(np.asarray(inputs["L1_attn_Wq"], np.float32).T)
    wkT1 = np.ascontiguousarray(np.asarray(inputs["L1_attn_Wk"], np.float32).T)
    wvT1 = np.ascontiguousarray(np.asarray(inputs["L1_attn_Wv"], np.float32).T)
    woT1 = np.ascontiguousarray(np.asarray(inputs["L1_attn_Wo"], np.float32).T)
    qg = np.asarray(inputs["L1_attn_qg"], np.float32).reshape(16)
    cos1, sin1 = _l1_costab()
    cos1t, sin1t = _tiles128(cos1), _tiles128(sin1)
    kvd = np.asarray(inputs["L1_ffn_kv_d_w"], np.float32)
    qd = np.asarray(inputs["L1_ffn_q_d_w"], np.float32)
    ku = np.asarray(inputs["L1_ffn_k_u_w"], np.float32)
    qu = np.asarray(inputs["L1_ffn_q_u_w"], np.float32)
    vu = np.asarray(inputs["L1_ffn_v_u_w"], np.float32)
    rk = np.asarray(inputs["L1_ffn_rope_k_w"], np.float32)
    rq = np.asarray(inputs["L1_ffn_rope_q_w"], np.float32)
    cosm, sinm = _mla_costab()
    cosmt, sinmt = _tiles128(cosm), _tiles128(sinm)
    ow = np.ascontiguousarray(np.asarray(inputs["L1_ffn_o_w"], np.float32))
    ln1w = np.ascontiguousarray(np.asarray(inputs["L1_ffn_ln1_w"], np.float32).reshape(1, H))
    ln2w = np.ascontiguousarray(np.asarray(inputs["L1_ffn_ln2_w"], np.float32).reshape(1, H))
    shg = np.asarray(inputs["L1_ffn_sh_gate"], np.float32)
    shu = np.asarray(inputs["L1_ffn_sh_up"], np.float32)
    shd = np.asarray(inputs["L1_ffn_sh_down"], np.float32)
    rg_ = np.asarray(inputs["L1_ffn_r_gate"], np.float32)
    ru_ = np.asarray(inputs["L1_ffn_r_up"], np.float32)
    rd_ = np.asarray(inputs["L1_ffn_r_down"], np.float32)
    rw = np.asarray(inputs["L1_ffn_router_w"], np.float32)
    rb = np.asarray(inputs["L1_ffn_router_b"], np.float32)
    bwoT = np.ascontiguousarray(woT0[np.concatenate([_l0_colmap(0), _l0_colmap(1)])])

    in_maps = []
    ar = np.arange(128, dtype=np.uint32)
    for c in range(NCORES):
        b, p = c // 2, c % 2
        cm = _l0_colmap(p)
        gsel = gateq[:, [4 * p + i for i in range(4)] + [8 + 4 * p + i for i in range(4)]]
        idxB = np.zeros((128, 16), np.uint32)
        for pk in range(2):
            for ci in range(8):
                idxB[:, pk * 8 + ci] = 2048 * (2 * b + pk) + 1024 * p + 128 * ci + ar
        idxPair = np.zeros((128, 32), np.uint32)
        for i in range(16):
            for blk in range(2):
                idxPair[:, 2 * i + blk] = 2048 * (2 * b + blk) + 128 * i + ar
        if c == 0:
            mg, mu, md = shg, shu, shd
            perm = list(range(7))
            wsel = np.array([[1.0, 0.0]], np.float32)
        else:
            mg, mu, md = rg_[c - 1], ru_[c - 1], rd_[c - 1]
            perm = [c - 1] + [i for i in range(7) if i != c - 1]
            wsel = np.array([[0.0, 0.5]], np.float32)
        cs = slice(512 * p, 512 * (p + 1))
        in_maps.append({
            "a1xT": np.ascontiguousarray(x[b].T),
            "a1wq": np.ascontiguousarray(wqT0[:, cm]),
            "a1wk": np.ascontiguousarray(wkT0[:, cm]),
            "a1wv": np.ascontiguousarray(wvT0[:, cm]),
            "l0cos": _tiles128(cos0T[512 * p:512 * (p + 1)]),
            "l0sin": _tiles128(sin0T[512 * p:512 * (p + 1)]),
            "gate": np.ascontiguousarray(gsel),
            "identI": IDENT,
            "maskI": CAUSAL_MASK,
            "idxBI": idxB,
            "idxPairI": idxPair,
            "bwoT": bwoT,
            "bx": np.ascontiguousarray(x[b][1024 * p:1024 * (p + 1)]),
            "tg": np.ascontiguousarray(bg[:, 2048 * c:2048 * (c + 1)]),
            "tu": np.ascontiguousarray(bu[:, 2048 * c:2048 * (c + 1)]),
            "td": np.ascontiguousarray(bd[1024 * c:1024 * (c + 1), :]),
            "gsT": np.ascontiguousarray(gsT[:, 2048 * c:2048 * (c + 1)]),
            "usT": np.ascontiguousarray(usT[:, 2048 * c:2048 * (c + 1)]),
            "dsT": np.ascontiguousarray(dsT[64 * c:64 * (c + 1), :]),
            "e1wq": np.ascontiguousarray(wqT1[:, 1024 * p:1024 * (p + 1)]),
            "e1wk": np.ascontiguousarray(wkT1[:, 256 * p:256 * (p + 1)]),
            "e1wv": np.ascontiguousarray(wvT1[:, 256 * p:256 * (p + 1)]),
            "qg": np.ascontiguousarray(qg[8 * p:8 * (p + 1)].reshape(1, 8)),
            "l1cos": cos1t, "l1sin": sin1t,
            "fwoT": woT1,
            "ln1w": ln1w,
            "kvd": kvd, "qd": qd,
            "ku": np.ascontiguousarray(ku[:, cs]),
            "qu": np.ascontiguousarray(qu[:, cs]),
            "vu": np.ascontiguousarray(vu[:, 1024 * p:1024 * (p + 1)]),
            "rk": np.ascontiguousarray(rk[:, cs]),
            "rq": np.ascontiguousarray(rq[:, cs]),
            "mcos": cosmt, "msin": sinmt,
            "hwoT": ow,
            "ln2w": ln2w,
            "mwg": np.ascontiguousarray(mg),
            "mwu": np.ascontiguousarray(mu),
            "mwd": np.ascontiguousarray(md),
            "mrw": np.ascontiguousarray(rw[:, perm]),
            "mrb": np.ascontiguousarray(rb[perm].reshape(1, 7)),
            "mwsel": wsel,
        })
    return in_maps


def kernel(**inputs):
    nc = build_fused()
    in_maps = _stage_inputs(inputs)
    res = _run(nc, in_maps)
    out = np.concatenate([np.asarray(r["out"]) for r in res.results], axis=0)
    return out.reshape(B, S, H).astype(np.float32)


# version 10 (final)
# speedup vs baseline: 1.0181x; 1.0131x over previous
import numpy as np
import ml_dtypes
from contextlib import ExitStack
import concourse.bass as bass
import concourse.mybir as mybir
from concourse.bass_utils import run_bass_kernel_spmd

F32 = mybir.dt.float32
BF16 = mybir.dt.bfloat16
U8 = mybir.dt.uint8
I32 = mybir.dt.int32
U32 = mybir.dt.uint32
AF = mybir.ActivationFunctionType
ALU = mybir.AluOpType
AX = mybir.AxisListType.X
BF = ml_dtypes.bfloat16

B, S, H = 4, 2048, 2048
NCORES = 8
TT_ = 8192
RMS_EPS_L1 = float(np.finfo(np.float32).eps)


def _l0_costab():
    inv = (1.0 / 10000.0 ** (np.arange(0, H, 2, dtype=np.float32) / H)).astype(np.float32)
    fr = np.arange(S, dtype=np.float32)[:, None] * inv[None, :]
    return np.cos(fr).astype(np.float32), np.sin(fr).astype(np.float32)


def _l1_costab():
    inv = (1.0 / 10000.0 ** (np.arange(0, 128, 2, dtype=np.float32) / 128)).astype(np.float32)
    fr = np.arange(S, dtype=np.float32)[:, None] * inv[None, :]
    return np.cos(fr).astype(np.float32), np.sin(fr).astype(np.float32)


def _mla_costab():
    inv = (1.0 / 10000.0 ** (np.arange(0, 64, 2, dtype=np.float32) / 64)).astype(np.float32)
    fr = np.arange(S, dtype=np.float32)[:, None] * inv[None, :]
    return np.cos(fr).astype(np.float32), np.sin(fr).astype(np.float32)


IDENT = np.eye(128, dtype=np.float32).astype(BF)
CAUSAL_MASK = np.triu(np.full((128, 128), -30000.0, dtype=np.float32), 1)


def _l0_colmap(p):
    cols = []
    for lh in range(8):
        if lh < 4:
            base = 512 * p + lh * 128
        else:
            base = 1024 + 512 * p + (lh - 4) * 128
        cols.extend(range(base, base + 128))
    return np.array(cols, dtype=np.int64)


def _pi_n():
    p = []
    for a in range(64):
        p.extend(range(256 * a, 256 * a + 256, 2))
        p.extend(range(256 * a + 1, 256 * a + 256, 2))
    return np.array(p, dtype=np.int64)


def _tiles128(a):
    t = a.shape[0] // 128
    return np.ascontiguousarray(a.reshape(t, 128, -1).transpose(1, 0, 2).reshape(128, -1))



class Seq:
    ENGS = ("sync", "gpsimd", "vector", "scalar", "tensor")

    def __init__(self, es, nc):
        self.es = es
        self.nc = nc
        self.sems = {n: es.enter_context(nc.semaphore("q_" + n)) for n in self.ENGS}
        self.cnt = {n: 0 for n in self.ENGS}
        self.prog = {n: [] for n in self.ENGS}
        self.stage_deps = []
        self.ncc = 0

    def new_stage(self):
        self.stage_deps = [(e, self.cnt[e]) for e in self.ENGS if self.cnt[e] > 0]

    def on(self, eng, fn, after=(), ndma=0):
        deps = []
        for tok in list(after) + self.stage_deps:
            if tok is None:
                continue
            e2, c2 = tok
            if e2 != eng or ndma:
                deps.append((e2, c2))
        inc = 16 * ndma if ndma else 1
        self.cnt[eng] += inc
        target = self.cnt[eng]
        self.prog[eng].append((deps, fn, 0 if ndma else 1))
        return (eng, target)

    def cc(self, fn, after=()):
        name = f"cc{self.ncc}"
        self.ncc += 1
        sem = self.es.enter_context(self.nc.semaphore("q_" + name))
        self.sems[name] = sem
        deps = []
        for tok in list(after) + self.stage_deps:
            if tok is None:
                continue
            deps.append(tok)

        def step(e, fn=fn, sem=sem):
            ins = fn(e)
            ins.then_inc(sem)
        self.prog["gpsimd"].append((deps, step, 2))
        return (name, 1)

    def emit(self, block):
        for eng in self.ENGS:
            steps = self.prog[eng]
            if not steps:
                continue

            def body(e, steps=steps, eng=eng):
                sem = self.sems[eng]
                waited = {}
                for deps, fn, kind in steps:
                    for e2, c2 in deps:
                        if waited.get(e2, 0) < c2:
                            e.wait_ge(self.sems[e2], c2)
                            waited[e2] = c2
                    if kind == 1:
                        last = fn(e)
                        last.then_inc(sem, 1)
                    elif kind == 2:
                        fn(e)
                    else:
                        fn(e, sem)
            getattr(block, eng)(body)

    def finish(self, block, waits):
        def body(e):
            for e2, c2 in waits:
                e.wait_ge(self.sems[e2], c2)
        block.sync(body)


class Pool:

    def __init__(self, nc, es, nbytes):
        self.nb = nbytes
        self.t = es.enter_context(nc.sbuf_tensor("POOL", [128, nbytes], U8))
        self.off = 0
        self.peak = 0

    def reset(self):
        self.off = 0

    def take(self, name, shape, dt):
        assert shape[0] == 128
        cols = 1
        for d in shape[1:]:
            cols *= d
        nbytes = cols * mybir.dt.size(dt)
        off = (self.off + 63) & ~63
        self.off = off + nbytes
        self.peak = max(self.peak, self.off)
        assert self.off <= self.nb, f"pool overflow at {name}: {self.off} > {self.nb}"
        return self.t[:, off:off + nbytes].bitcast(dt)



def ln_zigzag(q, t_xo, xo, xc, scrap, cols, fscr, c0, N, eps, apply_fn, extra_after=()):
    inv_n = 1.0 / float(N)

    def s_mean(e):
        e.mul(cols[:, c0 + 1:c0 + 2], cols[:, c0:c0 + 1], inv_n)
        return e.copy(fscr[:, 0:1], cols[:, c0 + 1:c0 + 2])
    t1 = q.on("scalar", s_mean, after=[t_xo])

    def v_center(e):
        e.tensor_scalar(xc[:], xo[:], cols[:, c0 + 1:c0 + 2], None, ALU.subtract)
        return e.tensor_copy(fscr[:, 1:2], xc[:, xc.shape[1] - 1:xc.shape[1]])
    t2 = q.on("vector", v_center, after=[t1])

    def s_sq(e):
        e.activation(scrap[:], xc[:], AF.Square, accum_out=cols[:, c0 + 2:c0 + 3])
        return e.copy(fscr[:, 0:1], cols[:, c0 + 2:c0 + 3])
    t3 = q.on("scalar", s_sq, after=[t2])

    def v_w(e):
        e.tensor_scalar(cols[:, c0 + 3:c0 + 4], cols[:, c0 + 2:c0 + 3], inv_n, eps, ALU.mult, ALU.add)
        return e.tensor_copy(fscr[:, 1:2], cols[:, c0 + 3:c0 + 4])
    t4 = q.on("vector", v_w, after=[t3])

    def s_sqrt(e):
        e.activation(cols[:, c0 + 4:c0 + 5], cols[:, c0 + 3:c0 + 4], AF.Sqrt)
        return e.copy(fscr[:, 0:1], cols[:, c0 + 4:c0 + 5])
    t5 = q.on("scalar", s_sqrt, after=[t4])

    def v_rstd(e):
        e.reciprocal(cols[:, c0 + 5:c0 + 6], cols[:, c0 + 4:c0 + 5])
        return e.tensor_copy(fscr[:, 1:2], cols[:, c0 + 5:c0 + 6])
    t6 = q.on("vector", v_rstd, after=[t5])

    return q.on("scalar", apply_fn, after=[t6] + list(extra_after))



def emit_comb_out(q, comb, idt, psTr, ctr, fscr, agin_d, after):
    stores = []
    tEv = None
    tSt = None
    for ci in range(8):
        for h2 in range(2):
            def trp(e, ci=ci, h2=h2):
                for jj in range(8):
                    tj = h2 * 8 + jj
                    ins = e.transpose(psTr[:, jj * 128:(jj + 1) * 128],
                                      comb[:, tj * 1024 + ci * 128:tj * 1024 + (ci + 1) * 128],
                                      idt)
                return ins
            tt = q.on("tensor", trp, after=[after, tEv])
            eng = "scalar" if (2 * ci + h2) % 2 == 0 else "vector"

            def evp(e, h2=h2, ci=ci, eng=eng):
                cp = e.copy if eng == "scalar" else e.tensor_copy
                for jj in range(8):
                    cp(ctr[:, h2 * 1024 + jj * 128:h2 * 1024 + (jj + 1) * 128],
                       psTr[:, jj * 128:(jj + 1) * 128])
                return cp(fscr[:, 4:5], ctr[:, h2 * 1024 + 1023:h2 * 1024 + 1024])
            tEv = q.on(eng, evp, after=[tt, tSt, tEv])

        def st(e, sem, ci=ci):
            e.dma_start(out=agin_d[ci * 128:(ci + 1) * 128, :],
                        in_=ctr[:, 0:1024]).then_inc(sem, 16)
            e.dma_start(out=agin_d[1024 + ci * 128:1024 + (ci + 1) * 128, :],
                        in_=ctr[:, 1024:2048]).then_inc(sem, 16)
        tSt = q.on("sync", st, after=[tEv], ndma=2)
        stores.append(tSt)
    return stores


class XTrOut:

    def __init__(self, q, idt, psTr, xtT, fscr, agin_d):
        self.q, self.idt, self.psTr, self.xtT = q, idt, psTr, xtT
        self.fscr, self.agin_d = fscr, agin_d
        self.tEv = None
        self.done = [None] * 8

    def add_tile(self, jt, src_tile, after):
        for h2 in range(2):
            def trp(e, h2=h2, src_tile=src_tile):
                for jj in range(8):
                    hi = h2 * 8 + jj
                    ins = e.transpose(self.psTr[:, jj * 128:(jj + 1) * 128],
                                      src_tile[:, hi * 128:(hi + 1) * 128], self.idt)
                return ins
            tt = self.q.on("tensor", trp, after=[after, self.tEv])
            eng = "scalar" if h2 == 0 else "vector"

            def evp(e, h2=h2, jt=jt, eng=eng):
                cp = e.copy if eng == "scalar" else e.tensor_copy
                for jj in range(8):
                    hi = h2 * 8 + jj
                    cp(self.xtT[:, hi * 1024 + jt * 128:hi * 1024 + (jt + 1) * 128],
                       self.psTr[:, jj * 128:(jj + 1) * 128])
                return cp(self.fscr[:, 4:5],
                          self.xtT[:, (h2 * 8 + 7) * 1024 + (jt + 1) * 128 - 1:(h2 * 8 + 7) * 1024 + (jt + 1) * 128])
            self.tEv = self.q.on(eng, evp, after=[tt, self.tEv])
        self.done[jt] = self.tEv

    def store(self):
        def st(e, sem):
            for hi in range(16):
                e.dma_start(out=self.agin_d[hi * 128:(hi + 1) * 128, :],
                            in_=self.xtT[:, hi * 1024:(hi + 1) * 1024]).then_inc(sem, 16)
        return self.q.on("sync", st, after=[self.done[7]], ndma=16)



def emit_l0_qkv(nc, q, pool, dr, G):
    pool.reset()
    sb = pool.take
    xT = sb("xT", [128, 16 * S], BF16)
    wb = sb("wb", [128, 16 * 1024], BF16)
    cosT = sb("cosT", [128, 4 * S], F32)
    sinT = sb("sinT", [128, 4 * S], F32)
    scr = sb("scr", [128, S], F32)
    scr2 = sb("scr2", [128, S], F32)
    tmp1 = sb("tmp1", [128, S], F32)
    tmp2 = sb("tmp2", [128, S], F32)
    ro = sb("ro", [128, 2 * S], BF16)
    vo = sb("vo", [128, 2 * 1024], BF16)
    fscr = sb("fscr", [128, 8], F32)
    psQ = G["psBIG"]
    psV = G["ps2"]

    def ldtab(e, sem):
        e.dma_start(out=cosT[:], in_=dr["l0cos"]).then_inc(sem, 16)
        e.dma_start(out=sinT[:], in_=dr["l0sin"]).then_inc(sem, 16)
    tTab = q.on("sync", ldtab, ndma=2)

    def ldx(e, sem):
        for i in range(16):
            e.dma_start(out=xT[:, i * S:(i + 1) * S],
                        in_=dr["a1xT"][128 * i:128 * (i + 1), :]).then_inc(sem, 16)
    tX = q.on("gpsimd", ldx, ndma=16)

    tW = [None] * 3
    tLastUse = [None] * 3
    prevEvac = None
    ro_store = []
    vo_store = []
    for w in range(3):
        wd = [dr["a1wq"], dr["a1wk"], dr["a1wv"]][w]

        def ldw(e, sem, wd=wd):
            for i in range(16):
                e.dma_start(out=wb[:, i * 1024:(i + 1) * 1024],
                            in_=wd[128 * i:128 * (i + 1), :]).then_inc(sem, 16)
        tW[w] = q.on("gpsimd", ldw, after=[tLastUse[w - 1] if w else None], ndma=16)

        if w < 2:
            for a in range(4):
                def mmA(e, a=a):
                    for n in range(4):
                        for i in range(16):
                            ins = e.matmul(psQ[:, n * 512:(n + 1) * 512],
                                           wb[:, i * 1024 + a * 128:i * 1024 + (a + 1) * 128],
                                           xT[:, i * S + n * 512:i * S + (n + 1) * 512],
                                           start=(i == 0), stop=(i == 15))
                    return ins
                tA_ = q.on("tensor", mmA, after=[tX, tW[w], prevEvac])

                def evA(e):
                    e.tensor_copy(scr[:], psQ[:, 0:S])
                    return e.tensor_copy(fscr[:, 1:2], scr[:, S - 1:S])
                prevEvac = q.on("vector", evA, after=[tA_])

                def mmB(e, a=a):
                    for n in range(4):
                        for i in range(16):
                            ins = e.matmul(psQ[:, n * 512:(n + 1) * 512],
                                           wb[:, i * 1024 + (a + 4) * 128:i * 1024 + (a + 5) * 128],
                                           xT[:, i * S + n * 512:i * S + (n + 1) * 512],
                                           start=(i == 0), stop=(i == 15))
                    return ins
                tB_ = q.on("tensor", mmB, after=[prevEvac])
                tLastUse[w] = tB_

                def evB(e):
                    e.tensor_copy(scr2[:], psQ[:, 0:S])
                    return e.tensor_copy(fscr[:, 3:4], scr2[:, S - 1:S])
                prevEvac = q.on("vector", evB, after=[tB_])

                def rope(e, a=a):
                    e.tensor_tensor(tmp1[:], scr[:], cosT[:, a * S:(a + 1) * S], ALU.mult)
                    e.tensor_tensor(tmp2[:], scr2[:], sinT[:, a * S:(a + 1) * S], ALU.mult)
                    e.tensor_tensor(ro[:, 0:S], tmp1[:], tmp2[:], ALU.subtract)
                    e.tensor_tensor(tmp1[:], scr2[:], cosT[:, a * S:(a + 1) * S], ALU.mult)
                    e.tensor_tensor(tmp2[:], scr[:], sinT[:, a * S:(a + 1) * S], ALU.mult)
                    e.tensor_tensor(ro[:, S:2 * S], tmp1[:], tmp2[:], ALU.add)
                    return e.tensor_copy(fscr[:, 2:3], ro[:, 2 * S - 1:2 * S])
                tRope = q.on("vector", rope, after=[prevEvac, tTab,
                                                    ro_store[-1] if ro_store else None])

                od = [dr["qT0"], dr["kT0"]][w]

                def st2(e, sem, a=a, od=od):
                    e.dma_start(out=od[a * 128:(a + 1) * 128, :], in_=ro[:, 0:S]).then_inc(sem, 16)
                    e.dma_start(out=od[(a + 4) * 128:(a + 5) * 128, :], in_=ro[:, S:2 * S]).then_inc(sem, 16)
                ro_store.append(q.on("sync", st2, after=[tRope], ndma=2))
        else:
            prevV = None
            for j in range(16):
                def mmV(e, j=j):
                    for n in range(2):
                        for i in range(16):
                            ins = e.matmul(psV[:, n * 512:(n + 1) * 512],
                                           xT[:, i * S + j * 128:i * S + j * 128 + 128],
                                           wb[:, i * 1024 + n * 512:i * 1024 + (n + 1) * 512],
                                           start=(i == 0), stop=(i == 15))
                    return ins
                tV = q.on("tensor", mmV, after=[tW[2], prevV])
                tLastUse[2] = tV

                def evV(e, j=j):
                    e.tensor_copy(vo[:, (j % 2) * 1024:(j % 2 + 1) * 1024], psV[:])
                    return e.tensor_copy(fscr[:, 3:4], vo[:, (j % 2) * 1024 + 1023:(j % 2) * 1024 + 1024])
                prevV = q.on("vector", evV, after=[tV, vo_store[-2] if len(vo_store) >= 2 else None])

                def stV(e, sem, j=j):
                    e.dma_start(out=dr["vt0"][128 * j:128 * (j + 1), :],
                                in_=vo[:, (j % 2) * 1024:(j % 2 + 1) * 1024]).then_inc(sem, 16)
                vo_store.append(q.on("sync", stV, after=[prevV], ndma=1))



def emit_l0_attn(nc, q, pool, dr, G):
    pool.reset()
    sb = pool.take
    SCALE = float(1.0 / np.sqrt(128.0))
    qT = sb("qT_s", [128, 8 * S], BF16)
    kT = sb("kT_s", [128, 8 * S], BF16)
    vtok = sb("vt_s", [128, 16 * 1024], BF16)
    comb = sb("comb", [128, 16 * 1024], BF16)
    qs = sb("qs", [128, S], BF16)
    qf = sb("qf", [128, S], BF16)
    kf = sb("kf", [128, S], BF16)
    kft = sb("kft", [128, 16 * 128], BF16)
    tA = sb("tA", [128, S], F32)
    tB = sb("tB", [128, S], F32)
    P = sb("P", [128, S], BF16)
    PT = sb("PT", [128, 16 * 512], BF16)
    mem = sb("mem", [128, 132], BF16)
    msc = sb("msc", [128, 512], F32)
    mtmp = sb("mtmp", [128, 128], F32)
    onec = sb("onec", [128, 4], BF16)
    gbc = sb("gbc", [128, 8], F32)
    ombc = sb("ombc", [128, 8], F32)
    cols = sb("cols", [128, 8], F32)
    wcols = sb("wcols", [128, 4], F32)
    lcols = sb("lcols", [128, 4], F32)
    ctr = sb("ctr", [128, 2048], BF16)
    fscr = sb("fscr", [128, 8], F32)
    idt = G["idt"]
    psS = G["psBIG"]
    psT = G["psTr"]
    psM = G["ps3"]
    psD = G["ps2"][:, 0:512]
    psA = G["ps2"][:, 512:1024]

    def ld(e, sem):
        for lh in range(8):
            e.dma_start(out=qT[:, lh * S:(lh + 1) * S],
                        in_=dr["qT0"][128 * lh:128 * (lh + 1), :]).then_inc(sem, 16)
            e.dma_start(out=kT[:, lh * S:(lh + 1) * S],
                        in_=dr["kT0"][128 * lh:128 * (lh + 1), :]).then_inc(sem, 16)
        for j in range(16):
            e.dma_start(out=vtok[:, j * 1024:(j + 1) * 1024],
                        in_=dr["vt0"][128 * j:128 * (j + 1), :]).then_inc(sem, 16)
        e.dma_start(out=gbc[:], in_=dr["gate"].broadcast_to([128, 8])).then_inc(sem, 16)
    tLD = q.on("sync", ld, ndma=33)

    def prep_g(e):
        e.activation(gbc[:], gbc[:], AF.Sigmoid)
        return e.copy(fscr[:, 0:1], gbc[:, 7:8])
    tG = q.on("scalar", prep_g, after=[tLD])

    def prep_omg(e):
        e.tensor_scalar(ombc[:], gbc[:], -1.0, 1.0, ALU.mult, ALU.add)
        e.memset(onec[:], 1.0)
        return e.tensor_copy(fscr[:, 1:2], ombc[:, 7:8])
    tOMG = q.on("vector", prep_omg, after=[tG])

    tPTev = None
    tMemEv = None
    tAttEv = None
    tDev = None
    for lh in range(8):
        hOFF = lh * S

        def qsv(e, hOFF=hOFF):
            e.tensor_scalar_mul(qs[:], qT[:, hOFF:hOFF + S], SCALE)
            return e.tensor_copy(fscr[:, 2:3], qs[:, S - 1:S])
        tQS = q.on("vector", qsv, after=[tLD])

        def kf1(e, hOFF=hOFF):
            e.tensor_single_scalar(tA[:], kT[:, hOFF:hOFF + S], 0.0, ALU.min)
            return e.tensor_copy(fscr[:, 2:3], tA[:, S - 1:S])
        t1 = q.on("vector", kf1, after=[tLD])

        def kf2(e):
            e.activation(tB[:], tA[:], AF.Exp)
            return e.copy(fscr[:, 3:4], tB[:, S - 1:S])
        t2 = q.on("scalar", kf2, after=[t1])

        def kf3(e, hOFF=hOFF):
            e.tensor_single_scalar(tA[:], kT[:, hOFF:hOFF + S], 0.0, ALU.max)
            e.tensor_tensor(kf[:], tB[:], tA[:], ALU.add)
            return e.tensor_copy(fscr[:, 2:3], kf[:, S - 1:S])
        t3 = q.on("vector", kf3, after=[t2])

        for bt in range(2):
            def trk(e, bt=bt):
                for i in range(8):
                    ins = e.transpose(psT[:, i * 128:(i + 1) * 128],
                                      kf[:, (bt * 8 + i) * 128:(bt * 8 + i + 1) * 128], idt)
                return ins
            tt = q.on("tensor", trk, after=[t3, tPTev])
            eng = "scalar" if bt == 0 else "vector"

            def evk(e, bt=bt, eng=eng):
                cp = e.copy if eng == "scalar" else e.tensor_copy
                for i in range(8):
                    cp(kft[:, (bt * 8 + i) * 128:(bt * 8 + i + 1) * 128],
                       psT[:, i * 128:(i + 1) * 128])
                return cp(fscr[:, 4:5], kft[:, (bt * 8 + 7) * 128 + 127:(bt * 8 + 7) * 128 + 128])
            tPTev = q.on(eng, evk, after=[tt, tPTev])
        tTrKdone = tPTev

        def qf1(e, hOFF=hOFF):
            e.tensor_single_scalar(tA[:], qT[:, hOFF:hOFF + S], 0.0, ALU.min)
            return e.tensor_copy(fscr[:, 2:3], tA[:, S - 1:S])
        t4 = q.on("vector", qf1, after=[t3])

        def qf2(e):
            e.activation(tB[:], tA[:], AF.Exp)
            return e.copy(fscr[:, 3:4], tB[:, S - 1:S])
        t5 = q.on("scalar", qf2, after=[t4])

        def qf3(e, hOFF=hOFF):
            e.tensor_single_scalar(tA[:], qT[:, hOFF:hOFF + S], 0.0, ALU.max)
            e.tensor_tensor(qf[:], tB[:], tA[:], ALU.add)
            return e.tensor_copy(fscr[:, 2:3], qf[:, S - 1:S])
        t6 = q.on("vector", qf3, after=[t5])

        def memmm(e, lh=lh):
            for j in range(16):
                e.matmul(psM[:, 0:128], kft[:, j * 128:(j + 1) * 128],
                         vtok[:, j * 1024 + lh * 128:j * 1024 + (lh + 1) * 128],
                         start=(j == 0), stop=(j == 15))
            for j in range(16):
                ins = e.matmul(psM[:, 128:129], kft[:, j * 128:(j + 1) * 128],
                               onec[:, 0:1], start=(j == 0), stop=(j == 15))
            return ins
        tMM = q.on("tensor", memmm, after=[tTrKdone, tOMG, tMemEv])

        def memev(e):
            e.copy(mem[:, 0:129], psM[:, 0:129])
            return e.copy(fscr[:, 5:6], mem[:, 128:129])
        tMemEv = q.on("scalar", memev, after=[tMM])

        for qg in range(4):
            tPh = []
            tEx = [None] * 4
            tWS = [None] * 4
            tLS = [None] * 4
            for qc in range(4):
                jq = qg * 4 + qc

                def smm(e, hOFF=hOFF, jq=jq):
                    for n in range(4):
                        ins = e.matmul(psS[:, n * 512:(n + 1) * 512],
                                       qs[:, jq * 128:(jq + 1) * 128],
                                       kT[:, hOFF + n * 512:hOFF + (n + 1) * 512],
                                       start=True, stop=True)
                    return ins
                tS = q.on("tensor", smm, after=[tQS, tEx[qc - 1] if qc else tPTev, None if qc else tAttEv])

                def smax(e):
                    e.reduce_max(cols[:, 0:1], psS[:], axis=AX, negate=True)
                    return e.tensor_copy(cols[:, 1:2], cols[:, 0:1])
                tMx = q.on("vector", smax, after=[tS])

                def sexp(e):
                    e.activation(P[:], psS[:], AF.Exp, bias=cols[:, 0:1], scale=1.0,
                                 accum_out=cols[:, 3:4])
                    e.copy(fscr[:, 6:7], P[:, S - 1:S])
                    return e.copy(fscr[:, 6:7], cols[:, 3:4])
                tEx[qc] = q.on("scalar", sexp, after=[tMx])

                def vrec(e):
                    e.reciprocal(cols[:, 4:5], cols[:, 3:4])
                    return e.tensor_copy(cols[:, 5:6], cols[:, 4:5])
                tR = q.on("vector", vrec, after=[tEx[qc]])

                def swc(e, lh=lh, qc=qc):
                    e.mul(wcols[:, qc:qc + 1], cols[:, 4:5], ombc[:, lh:lh + 1])
                    return e.copy(fscr[:, 7:8], wcols[:, qc:qc + 1])
                tWS[qc] = q.on("scalar", swc, after=[tR])

                for bt in range(2):
                    def trp(e, bt=bt):
                        for i in range(8):
                            ins = e.transpose(psT[:, i * 128:(i + 1) * 128],
                                              P[:, (bt * 8 + i) * 128:(bt * 8 + i + 1) * 128], idt)
                        return ins
                    tt = q.on("tensor", trp, after=[tEx[qc], tPTev])
                    eng = "scalar" if bt == 0 else "vector"

                    def evp(e, bt=bt, qc=qc, eng=eng):
                        cp = e.copy if eng == "scalar" else e.tensor_copy
                        for i in range(8):
                            j = bt * 8 + i
                            cp(PT[:, j * 512 + qc * 128:j * 512 + (qc + 1) * 128],
                               psT[:, i * 128:(i + 1) * 128])
                        return cp(fscr[:, 4:5], PT[:, (bt * 8 + 7) * 512 + qc * 128 + 127:(bt * 8 + 7) * 512 + qc * 128 + 128])
                    tPTev = q.on(eng, evp, after=[tt, tPTev])
                    tPh.append(tPTev)

                def dmm(e, jq=jq):
                    return e.matmul(psD[:, 0:129], qf[:, jq * 128:(jq + 1) * 128],
                                    mem[:, 0:129], start=True, stop=True)
                tD = q.on("tensor", dmm, after=[tMemEv, t6, tDev, None if qc else tAttEv])

                def dev(e, qc=qc):
                    e.tensor_copy(msc[:, qc * 128:(qc + 1) * 128], psD[:, 0:128])
                    e.reciprocal(cols[:, 6:7], psD[:, 128:129])
                    return e.tensor_copy(cols[:, 7:8], cols[:, 6:7])
                tDev = q.on("vector", dev, after=[tD])

                def slc(e, lh=lh, qc=qc):
                    e.mul(lcols[:, qc:qc + 1], cols[:, 6:7], gbc[:, lh:lh + 1])
                    return e.copy(fscr[:, 7:8], lcols[:, qc:qc + 1])
                tLS[qc] = q.on("scalar", slc, after=[tDev])

            def amm(e, lh=lh):
                for qc in range(4):
                    for j in range(16):
                        ins = e.matmul(psA[:, qc * 128:(qc + 1) * 128],
                                       PT[:, j * 512 + qc * 128:j * 512 + (qc + 1) * 128],
                                       vtok[:, j * 1024 + lh * 128:j * 1024 + (lh + 1) * 128],
                                       start=(j == 0), stop=(j == 15))
                return ins
            tAm = q.on("tensor", amm, after=[tPh[-1], tAttEv])

            def combv(e, lh=lh, qg=qg):
                for qc in range(4):
                    jq = qg * 4 + qc
                    e.tensor_scalar(mtmp[:], msc[:, qc * 128:(qc + 1) * 128],
                                    lcols[:, qc:qc + 1], None, ALU.mult)
                    e.scalar_tensor_tensor(
                        comb[:, jq * 1024 + lh * 128:jq * 1024 + (lh + 1) * 128],
                        psA[:, qc * 128:(qc + 1) * 128], wcols[:, qc:qc + 1], mtmp[:],
                        ALU.mult, ALU.add)
                return e.tensor_copy(fscr[:, 2:3], comb[:, (qg * 4 + 3) * 1024 + lh * 128 + 127:(qg * 4 + 3) * 1024 + lh * 128 + 128])
            tAttEv = q.on("vector", combv, after=[tAm, tWS[3], tLS[3], tDev])

    stores = emit_comb_out(q, comb, idt, psT, ctr, fscr, dr["agin_comb"], tAttEv)
    tAG = q.cc(lambda e: e.collective_compute(
        "AllGather", ALU.bypass, replica_groups=G["rg8"],
        ins=[dr["agin_comb"].opt()], outs=[dr["agout_comb"].opt()]),
        after=stores)
    return tAG



def emit_oproj_ln(nc, q, pool, dr, G):
    pool.reset()
    sb = pool.take
    cbT = sb("cbT_s", [128, 16 * 1024], BF16)
    woT = sb("woT_s", [128, 16 * H], BF16)
    xres = sb("xres", [128, 2 * H], F32)
    xo = sb("xo", [128, H], F32)
    xc = sb("xc", [128, H], F32)
    scrap = sb("scrap", [128, H], F32)
    x1f = sb("x1f_s", [128, 2 * H], F32)
    x1b = sb("x1b_s", [128, 2 * H], BF16)
    xtT = sb("xtT", [128, 16 * 1024], BF16)
    cols = sb("cols", [128, 8], F32)
    fscr = sb("fscr", [128, 8], F32)
    psO = G["psBIG"]
    tr = XTrOut(q, G["idt"], G["psTr"], xtT, fscr, dr["agin_x1"])

    def ldc(e, sem):
        for k in range(16):
            e.indirect_dma_start(
                out=cbT[:, k * 1024:(k + 1) * 1024], out_offset=None,
                in_=dr["agout_comb"],
                in_offset=bass.IndirectOffsetOnAxis(G["idxB"][:, k:k + 1], axis=0),
            ).then_inc(sem, 16)
    tLD = q.on("gpsimd", ldc, after=[dr["tAGcomb"]], ndma=16)

    def ldw(e, sem):
        for i in range(16):
            e.dma_start(out=woT[:, i * H:(i + 1) * H],
                        in_=dr["bwoT"][128 * i:128 * (i + 1), :]).then_inc(sem, 16)
    tW = q.on("gpsimd", ldw, ndma=16)

    tXO = [None] * 8
    tAPP = [None] * 8
    tStore = [None] * 8
    for jt in range(8):
        slot = jt % 2

        def ldx(e, sem, jt=jt, slot=slot):
            e.dma_start(out=xres[:, slot * H:(slot + 1) * H],
                        in_=dr["bx"][128 * jt:128 * (jt + 1), :]).then_inc(sem, 16)
        tX = q.on("sync", ldx, after=[tXO[jt - 2] if jt >= 2 else None], ndma=1)

        def mm(e, jt=jt):
            for n in range(4):
                for i in range(16):
                    ins = e.matmul(psO[:, n * 512:(n + 1) * 512],
                                   cbT[:, i * 1024 + jt * 128:i * 1024 + (jt + 1) * 128],
                                   woT[:, i * H + n * 512:i * H + (n + 1) * 512],
                                   start=(i == 0), stop=(i == 15))
            return ins
        tM = q.on("tensor", mm, after=[tLD, tW, tXO[jt - 1] if jt >= 1 else None])

        def v_xo(e, slot=slot):
            e.scalar_tensor_tensor(xo[:], psO[:], 1.0, xres[:, slot * H:(slot + 1) * H],
                                   ALU.mult, ALU.add, accum_out=cols[:, 0:1])
            return e.tensor_copy(fscr[:, 1:2], cols[:, 0:1])
        tXO[jt] = q.on("vector", v_xo, after=[tM, tX])

        def apply(e, jt=jt, slot=slot):
            e.activation(x1f[:, slot * H:(slot + 1) * H], xc[:], AF.Copy,
                         scale=cols[:, 5:6])
            e.copy(x1b[:, slot * H:(slot + 1) * H],
                   x1f[:, slot * H:(slot + 1) * H])
            e.copy(fscr[:, 0:1], x1f[:, slot * H + H - 1:slot * H + H])
            return e.copy(fscr[:, 0:1], x1b[:, slot * H + H - 1:slot * H + H])
        tAPP[jt] = ln_zigzag(q, tXO[jt], xo, xc, scrap, cols, fscr, 0, H, 1e-5, apply,
                             extra_after=[tStore[jt - 2] if jt >= 2 else None,
                                          tr.done[jt - 2] if jt >= 2 else None])

        def st(e, sem, jt=jt, slot=slot):
            e.dma_start(out=dr["x1f"][128 * jt:128 * (jt + 1), :],
                        in_=x1f[:, slot * H:(slot + 1) * H]).then_inc(sem, 16)
        tStore[jt] = q.on("sync", st, after=[tAPP[jt]], ndma=1)

        tr.add_tile(jt, x1b[:, slot * H:(slot + 1) * H], tAPP[jt])

    tSt = tr.store()
    tAG = q.cc(lambda e: e.collective_compute(
        "AllGather", ALU.bypass, replica_groups=G["rg8"],
        ins=[dr["agin_x1"].opt()], outs=[dr["agout_x1"].opt()]),
        after=[tSt, tStore[6], tStore[7]])
    return tAG



def _emit_decode(q, tiles, src_u8, scl, dst, dst_off, after, m75):
    tA, tB, tBi, nb1, nb2, d1, d2, sgt = tiles
    NC = 2048

    def v_bytes(e):
        e.tensor_copy(tA[:], src_u8[:])
        e.tensor_scalar(tB[:], tA[:], 0.0625, -0.46875, ALU.mult, ALU.add)
        e.tensor_copy(tBi[:], tB[:])
        e.tensor_copy(tB[:], tBi[:])
        e.scalar_tensor_tensor(nb1[:], tB[:], -16.0, tA[:], ALU.mult, ALU.add)
        e.tensor_copy(nb2[:], tB[:])
        return e.tensor_copy(tA[:, 0:1], nb2[:, NC - 1:NC])
    tV = q.on("vector", v_bytes, after=after)

    outs = [None]
    for pi, nb in ((0, nb1), (1, nb2)):
        def s_sign(e, nb=nb):
            e.activation(sgt[:], nb[:], AF.Sign, bias=m75[:, 0:1])
            return e.copy(tA[:, 1:2], sgt[:, NC - 1:NC])
        tS = q.on("scalar", s_sign, after=[tV, outs[-1]])

        def v_dec(e, nb=nb, pi=pi):
            e.scalar_tensor_tensor(d1[:], sgt[:], -4.0, nb[:], ALU.mult, ALU.add)
            e.tensor_scalar(d2[:], d1[:], -8.0, 0.0, ALU.add, ALU.max)
            e.tensor_scalar(nb[:], d1[:], -10.0, 0.0, ALU.add, ALU.max)
            e.scalar_tensor_tensor(d2[:], nb[:], 2.0, d2[:], ALU.mult, ALU.add)
            e.tensor_tensor(d2[:], d2[:], d1[:], ALU.add)
            e.tensor_scalar(d2[:], d2[:], -4.0, None, ALU.add)
            e.tensor_tensor(d2[:], d2[:], sgt[:], ALU.mult)
            e.scalar_tensor_tensor(dst[:, dst_off + pi * NC:dst_off + (pi + 1) * NC],
                                   d2[:], -0.5, scl[:], ALU.mult, ALU.mult)
            return e.tensor_copy(tA[:, 2:3],
                                 dst[:, dst_off + (pi + 1) * NC - 1:dst_off + (pi + 1) * NC])
        outs.append(q.on("vector", v_dec, after=[tS]))
    return outs[-1]


def emit_l0_ffn(nc, q, pool, dr, G):
    pool.reset()
    sb = pool.take
    wg = sb("wg", [128, 16 * 2048], BF16)
    wu = sb("wu", [128, 16 * 2048], BF16)
    u8t = sb("u8t", [128, 2048], U8)
    sclt = sb("sclt", [128, 2048], F32)
    tA = sb("tA", [128, 2048], F32)
    tB = sb("tB", [128, 2048], F32)
    tBi = sb("tBi", [128, 2048], I32)
    obuf = tA.bitcast(BF16)
    nb1 = sb("nb1", [128, 2048], BF16)
    nb2 = sb("nb2", [128, 2048], BF16)
    d1 = sb("d1", [128, 2048], BF16)
    d2 = sb("d2", [128, 2048], BF16)
    sgt = sb("sgt", [128, 2048], BF16)
    sq = sb("sq", [128, 512], F32)
    cc_ = sb("cc_", [128, 512], F32)
    arg = sb("arg", [128, 512], F32)
    th = sq
    dd = cc_
    e1 = arg
    ibuf = sb("ibuf", [128, 2 * 512], BF16)
    fscr = sb("fscr", [128, 8], F32)
    m75c = sb("m75c", [128, 1], F32)
    xcb = sb("xcb", [128, 16 * 512], BF16)
    psG = G["ps2"]
    psU = G["psBIG"][:, 0:1024]
    psBIG = G["psBIG"]
    dec_tiles = (tA, tB, tBi, nb1, nb2, d1, d2, sgt)
    xg = dr["agout_x1"]

    def mkc(e):
        e.memset(m75c[:], -7.5)
        return e.tensor_copy(fscr[:, 3:4], m75c[:])
    tM75 = q.on("vector", mkc)

    tDecDone = None
    tLdPrev = [None, None]
    tDecQ = []
    for a in range(8):
        for mi, (wdst, t_d, s_d) in enumerate([(wg, dr["tg"], dr["gsT"]), (wu, dr["tu"], dr["usT"])]):
            slot = (2 * a + mi) % 2

            def ldt(e, sem, a=a, t_d=t_d, s_d=s_d):
                e.dma_start(out=u8t[:],
                            in_=t_d[128 * a:128 * (a + 1), :]).then_inc(sem, 16)
                e.dma_start(out=sclt[:],
                            in_=s_d[8 * a:8 * (a + 1), :].unsqueeze(1).broadcast_to([8, 16, 2048])).then_inc(sem, 16)
            tL = q.on("sync", ldt, after=[tLdPrev[slot], tDecDone], ndma=2)
            tDecDone = _emit_decode(q, dec_tiles, u8t[:], sclt[:],
                                    wdst, (2 * a) * 2048, after=[tL, tDecDone, tM75], m75=m75c)
            tLdPrev[slot] = tDecDone
        if a % 2 == 1:
            tDecQ.append(tDecDone)

    tGU = None
    tIst = [None] * 4
    prev_mm = [None, None]
    for tc in range(16):
        r, u = tc // 2, tc % 2

        def ldxc(e, sem, r=r, u=u):
            for i in range(16):
                base = 2048 * r + 256 * (i // 2) + (i % 2)
                e.dma_start(out=xcb[:, i * 512:(i + 1) * 512],
                            in_=xg[base:base + 255:2,
                                   512 * u:512 * (u + 1)]).then_inc(sem, 16)
        tXc = q.on("sync", ldxc, after=[tGU, dr["tAGx1"] if tc == 0 else None], ndma=16)

        for nt in range(16):
            slot = nt % 2

            if tc == 0:
                for qk in range(4):
                    def gummq(e, nt=nt, slot=slot, qk=qk):
                        for i in range(4 * qk, 4 * qk + 4):
                            e.matmul(psG[:, slot * 512:(slot + 1) * 512],
                                     wg[:, i * 2048 + nt * 128:i * 2048 + (nt + 1) * 128],
                                     xcb[:, i * 512:(i + 1) * 512],
                                     start=(i == 0), stop=(i == 15))
                        for i in range(4 * qk, 4 * qk + 4):
                            ins = e.matmul(psU[:, slot * 512:(slot + 1) * 512],
                                           wu[:, i * 2048 + nt * 128:i * 2048 + (nt + 1) * 128],
                                           xcb[:, i * 512:(i + 1) * 512],
                                           start=(i == 0), stop=(i == 15))
                        return ins
                    tGU = q.on("tensor", gummq,
                               after=[tXc, tDecQ[qk],
                                      prev_mm[slot] if qk == 0 else None])
            else:
                def gumm(e, nt=nt, slot=slot):
                    for i in range(16):
                        e.matmul(psG[:, slot * 512:(slot + 1) * 512],
                                 wg[:, i * 2048 + nt * 128:i * 2048 + (nt + 1) * 128],
                                 xcb[:, i * 512:(i + 1) * 512],
                                 start=(i == 0), stop=(i == 15))
                    for i in range(16):
                        ins = e.matmul(psU[:, slot * 512:(slot + 1) * 512],
                                       wu[:, i * 2048 + nt * 128:i * 2048 + (nt + 1) * 128],
                                       xcb[:, i * 512:(i + 1) * 512],
                                       start=(i == 0), stop=(i == 15))
                    return ins
                tGU = q.on("tensor", gumm, after=[tXc, tDecDone, prev_mm[slot]])

            def gelu1(e, slot=slot):
                e.activation(sq[:], psG[:, slot * 512:(slot + 1) * 512], AF.Square)
                return e.copy(fscr[:, 0:1], sq[:, 511:512])
            tq1 = q.on("scalar", gelu1, after=[tGU])

            def gelu2(e, slot=slot):
                e.tensor_scalar(cc_[:], sq[:], 0.044715 * 0.7978845608028654,
                                0.7978845608028654, ALU.mult, ALU.add)
                e.tensor_tensor(arg[:], cc_[:], psG[:, slot * 512:(slot + 1) * 512], ALU.mult)
                return e.tensor_copy(fscr[:, 1:2], arg[:, 511:512])
            tq2 = q.on("vector", gelu2, after=[tq1])

            def gelu3(e):
                e.activation(th[:], arg[:], AF.Tanh)
                return e.copy(fscr[:, 0:1], th[:, 511:512])
            tq3 = q.on("scalar", gelu3, after=[tq2])

            def gelu4(e, slot=slot):
                e.tensor_scalar(dd[:], th[:], 1.0, 0.5, ALU.add, ALU.mult)
                e.tensor_tensor(e1[:], dd[:], psG[:, slot * 512:(slot + 1) * 512], ALU.mult)
                e.tensor_tensor(ibuf[:, slot * 512:(slot + 1) * 512], e1[:],
                                psU[:, slot * 512:(slot + 1) * 512], ALU.mult)
                return e.tensor_copy(fscr[:, 1:2], ibuf[:, slot * 512 + 511:slot * 512 + 512])
            tq4 = q.on("vector", gelu4, after=[tq3, tIst[slot] if nt >= 2 else None])
            prev_mm[slot] = tq4

            def ist(e, sem, nt=nt, tc=tc, slot=slot):
                e.dma_start(out=dr["inter"][128 * nt:128 * (nt + 1), 512 * tc:512 * (tc + 1)],
                            in_=ibuf[:, slot * 512:(slot + 1) * 512]).then_inc(sem, 16)
            tIst[slot] = q.on("sync", ist, after=[tq4], ndma=1)
    tPh1St = [tIst[0], tIst[1]]

    wd = wg
    inT = xcb
    tDec2 = tGU
    tLdPrev2 = [tPh1St[0], tPh1St[1]]
    for a in range(8):
        slot = a % 2

        def ldt2(e, sem, a=a):
            e.dma_start(out=u8t[:],
                        in_=dr["td"][128 * a:128 * (a + 1), :]).then_inc(sem, 16)
            e.dma_start(out=sclt[:],
                        in_=dr["dsT"][8 * a:8 * (a + 1), :].unsqueeze(1).broadcast_to([8, 16, 2048])).then_inc(sem, 16)
        tL2 = q.on("sync", ldt2, after=[tLdPrev2[slot], tPh1St[slot], tDec2], ndma=2)
        tDec2 = _emit_decode(q, dec_tiles, u8t[:], sclt[:],
                             wd, (2 * a) * 2048, after=[tL2, tDec2, tGU], m75=m75c)
        tLdPrev2[slot] = tDec2
        if a == 3:
            tDec2H = tDec2

    tOev = [None, None]
    tOevLast = None
    tOst = [None, None]
    tMM2 = None
    for tc in range(16):
        def ldin(e, sem, tc=tc):
            for i in range(16):
                e.dma_start(out=inT[:, i * 512:(i + 1) * 512],
                            in_=dr["inter"][128 * i:128 * (i + 1), 512 * tc:512 * (tc + 1)]).then_inc(sem, 16)
        tI = q.on("gpsimd", ldin, after=[tMM2, tPh1St[0], tPh1St[1]], ndma=16)

        for tq_ in range(4):
            slot = tq_ % 2

            if tc == 0:
                def dmm2a(e, tq_=tq_):
                    for hq in range(4):
                        for ntx in range(8):
                            ins = e.matmul(psBIG[:, hq * 512:(hq + 1) * 512],
                                           inT[:, ntx * 512 + tq_ * 128:ntx * 512 + (tq_ + 1) * 128],
                                           wd[:, ntx * 2048 + hq * 512:ntx * 2048 + (hq + 1) * 512],
                                           start=(ntx == 0), stop=False)
                    return ins
                tMa = q.on("tensor", dmm2a, after=[tI, tDec2H, tOevLast])

                def dmm2b(e, tq_=tq_):
                    for hq in range(4):
                        for ntx in range(8, 16):
                            ins = e.matmul(psBIG[:, hq * 512:(hq + 1) * 512],
                                           inT[:, ntx * 512 + tq_ * 128:ntx * 512 + (tq_ + 1) * 128],
                                           wd[:, ntx * 2048 + hq * 512:ntx * 2048 + (hq + 1) * 512],
                                           start=False, stop=(ntx == 15))
                    return ins
                tMM2 = q.on("tensor", dmm2b, after=[tMa, tDec2])
            else:
                def dmm2(e, tq_=tq_):
                    for hq in range(4):
                        for ntx in range(16):
                            ins = e.matmul(psBIG[:, hq * 512:(hq + 1) * 512],
                                           inT[:, ntx * 512 + tq_ * 128:ntx * 512 + (tq_ + 1) * 128],
                                           wd[:, ntx * 2048 + hq * 512:ntx * 2048 + (hq + 1) * 512],
                                           start=(ntx == 0), stop=(ntx == 15))
                    return ins
                tMM2 = q.on("tensor", dmm2, after=[tI, tDec2, tOevLast])

            def oev(e, slot=slot):
                e.tensor_copy(obuf[:, slot * 2048:(slot + 1) * 2048], psBIG[:])
                return e.tensor_copy(fscr[:, 1:2], obuf[:, slot * 2048 + 2047:slot * 2048 + 2048])
            tOev[slot] = q.on("vector", oev, after=[tMM2, tOst[slot]])
            tOevLast = tOev[slot]

            def ost(e, sem, tc=tc, tq_=tq_, slot=slot):
                e.dma_start(out=dr["part1"][512 * tc + 128 * tq_:512 * tc + 128 * (tq_ + 1), :],
                            in_=obuf[:, slot * 2048:(slot + 1) * 2048]).then_inc(sem, 16)
            tOst[slot] = q.on("sync", ost, after=[tOev[slot]], ndma=1)

    tRS = q.cc(lambda e: e.collective_compute(
        "ReduceScatter", ALU.add, replica_groups=G["rg8"],
        ins=[dr["part1"].opt()], outs=[dr["rs1"].opt()]),
        after=[tOst[0], tOst[1]])
    return tRS



def emit_postffn_ln(nc, q, pool, dr, G):
    pool.reset()
    sb = pool.take
    pt = sb("pt", [128, 2 * H], BF16)
    xres = sb("xres", [128, 2 * H], F32)
    xo = sb("xo", [128, H], F32)
    xc = sb("xc", [128, H], F32)
    scrap = sb("scrap", [128, H], F32)
    yf = sb("yf_s", [128, 2 * H], F32)
    yb = sb("yb_s", [128, 2 * H], BF16)
    xtT = sb("xtT", [128, 16 * 1024], BF16)
    cols = sb("cols", [128, 8], F32)
    fscr = sb("fscr", [128, 8], F32)
    tr = XTrOut(q, G["idt"], G["psTr"], xtT, fscr, dr["agin_x2"])

    tSum = [None] * 8
    tStore = [None] * 8
    for jt in range(8):
        slot = jt % 2

        def ld(e, sem, jt=jt, slot=slot):
            e.dma_start(out=pt[:, slot * H:(slot + 1) * H],
                        in_=dr["rs1"][128 * jt:128 * (jt + 1), :]).then_inc(sem, 16)
            e.dma_start(out=xres[:, slot * H:(slot + 1) * H],
                        in_=dr["x1f"][128 * jt:128 * (jt + 1), :]).then_inc(sem, 16)
        tL = q.on("sync", ld, after=[tSum[jt - 2] if jt >= 2 else None,
                                     dr["tRS1"] if jt == 0 else None], ndma=2)

        def vsum(e, slot=slot):
            e.scalar_tensor_tensor(xo[:], pt[:, slot * H:(slot + 1) * H], 1.0,
                                   xres[:, slot * H:(slot + 1) * H],
                                   ALU.mult, ALU.add, accum_out=cols[:, 0:1])
            return e.tensor_copy(fscr[:, 1:2], cols[:, 0:1])
        tSum[jt] = q.on("vector", vsum, after=[tL])

        def apply(e, jt=jt, slot=slot):
            e.activation(yf[:, slot * H:(slot + 1) * H], xc[:], AF.Copy, scale=cols[:, 5:6])
            e.copy(yb[:, slot * H:(slot + 1) * H], yf[:, slot * H:(slot + 1) * H])
            e.copy(fscr[:, 0:1], yf[:, slot * H + H - 1:slot * H + H])
            return e.copy(fscr[:, 0:1], yb[:, slot * H + H - 1:slot * H + H])
        tAPP = ln_zigzag(q, tSum[jt], xo, xc, scrap, cols, fscr, 0, H, 1e-5, apply,
                         extra_after=[tStore[jt - 2] if jt >= 2 else None,
                                      tr.done[jt - 2] if jt >= 2 else None])

        def st(e, sem, jt=jt, slot=slot):
            e.dma_start(out=dr["x2f"][128 * jt:128 * (jt + 1), :],
                        in_=yf[:, slot * H:(slot + 1) * H]).then_inc(sem, 16)
        tStore[jt] = q.on("sync", st, after=[tAPP], ndma=1)

        tr.add_tile(jt, yb[:, slot * H:(slot + 1) * H], tAPP)

    tSt = tr.store()
    tAG = q.cc(lambda e: e.collective_compute(
        "AllGather", ALU.bypass, replica_groups=G["rg8"],
        ins=[dr["agin_x2"].opt()], outs=[dr["agout_x2"].opt()]),
        after=[tSt, tStore[6], tStore[7]])
    return tAG



def emit_l1_qkv(nc, q, pool, dr, G):
    pool.reset()
    sb = pool.take
    xT = sb("xT_s", [128, 16 * S], BF16)
    wb = sb("wb", [128, 16 * 1024], BF16)
    cosq = sb("cosq_s", [128, 16 * 64], F32)
    sinq = sb("sinq_s", [128, 16 * 64], F32)
    qgc = sb("qgc", [128, 8], F32)
    qn = sb("qn", [128, 2 * 1024], F32)
    qro = sb("qro", [128, 16 * 1024], BF16)
    kro = sb("kro", [128, 16 * 256], BF16)
    vo = sb("vo", [128, 2 * 256], BF16)
    orow = sb("orow", [128, S], BF16)
    sqc = sb("sqc", [128, 64], F32)
    scrap = sb("scrap", [128, 128], F32)
    vtmp = sb("vtmp", [128, 128], F32)
    fscr = sb("fscr", [128, 8], F32)
    idt = G["idt"]
    psQ = G["ps2"]
    psQb = G["psBIG"][:, 0:1024]
    psT = G["psTr"]

    def ld(e, sem):
        e.dma_start(out=cosq[:], in_=dr["l1cos"]).then_inc(sem, 16)
        e.dma_start(out=sinq[:], in_=dr["l1sin"]).then_inc(sem, 16)
        e.dma_start(out=qgc[:], in_=dr["qg"].broadcast_to([128, 8])).then_inc(sem, 16)
    tLD = q.on("sync", ld, ndma=3)

    def ldx(e, sem):
        for i in range(16):
            for blk in range(2):
                e.indirect_dma_start(
                    out=xT[:, i * S + blk * 1024:i * S + (blk + 1) * 1024], out_offset=None,
                    in_=dr["agout_x2"],
                    in_offset=bass.IndirectOffsetOnAxis(G["idxPair"][:, 2 * i + blk:2 * i + blk + 1], axis=0),
                ).then_inc(sem, 16)
    tX = q.on("gpsimd", ldx, after=[dr["tAGx2"]], ndma=32)

    def emit_proj_norm_rope(w_d, ncols, nheads, out_tiles, tPrevUse, eps, prev_t5):
        def ldw(e, sem):
            for i in range(16):
                e.dma_start(out=wb[:, i * ncols:(i + 1) * ncols],
                            in_=w_d[128 * i:128 * (i + 1), :]).then_inc(sem, 16)
        tW = q.on("gpsimd", ldw, after=[tPrevUse], ndma=16)

        toks = [None] * 16
        t5H = [None] * 16
        tLastMM = None
        for j in range(16):
            psq = psQ if j % 2 == 0 else psQb
            sq0 = 32 * (j % 2)
            qo = (j % 2) * 1024

            def pmm(e, j=j, ncols=ncols, psq=psq):
                nch = max(1, ncols // 512)
                for n in range(nch):
                    w512 = min(512, ncols)
                    for i in range(16):
                        ins = e.matmul(psq[:, n * 512:n * 512 + w512],
                                       xT[:, i * S + j * 128:i * S + j * 128 + 128],
                                       wb[:, i * ncols + n * w512:i * ncols + (n + 1) * w512],
                                       start=(i == 0), stop=(i == 15))
                return ins
            tMM = q.on("tensor", pmm,
                       after=[tX, tW, t5H[j - 2] if j >= 2 else prev_t5[j % 2]])
            tLastMM = tMM

            def ssq(e, nheads=nheads, psq=psq, sq0=sq0):
                for lh in range(nheads):
                    ins = e.activation(scrap[:], psq[:, lh * 128:(lh + 1) * 128], AF.Square,
                                       accum_out=sqc[:, sq0 + lh:sq0 + lh + 1])
                return e.copy(fscr[:, 0:1], sqc[:, sq0 + nheads - 1:sq0 + nheads])
            t1 = q.on("scalar", ssq, after=[tMM])

            def vw(e, nheads=nheads, eps=eps, sq0=sq0):
                e.tensor_scalar(sqc[:, sq0 + 8:sq0 + 8 + nheads], sqc[:, sq0:sq0 + nheads],
                                1.0 / 128.0, eps, ALU.mult, ALU.add)
                return e.tensor_copy(fscr[:, 1:2], sqc[:, sq0 + 8 + nheads - 1:sq0 + 8 + nheads])
            t2 = q.on("vector", vw, after=[t1])

            def s3(e, nheads=nheads, sq0=sq0):
                e.activation(sqc[:, sq0 + 16:sq0 + 16 + nheads], sqc[:, sq0 + 8:sq0 + 8 + nheads], AF.Sqrt)
                return e.copy(fscr[:, 0:1], sqc[:, sq0 + 16 + nheads - 1:sq0 + 16 + nheads])
            t3 = q.on("scalar", s3, after=[t2])

            def v4(e, nheads=nheads, sq0=sq0):
                e.reciprocal(sqc[:, sq0 + 24:sq0 + 24 + nheads], sqc[:, sq0 + 16:sq0 + 16 + nheads])
                return e.tensor_copy(fscr[:, 1:2], sqc[:, sq0 + 24 + nheads - 1:sq0 + 24 + nheads])
            t4 = q.on("vector", v4, after=[t3])

            def s5(e, nheads=nheads, psq=psq, sq0=sq0, qo=qo):
                for lh in range(nheads):
                    ins = e.activation(qn[:, qo + lh * 128:qo + (lh + 1) * 128],
                                       psq[:, lh * 128:(lh + 1) * 128], AF.Copy,
                                       scale=sqc[:, sq0 + 24 + lh:sq0 + 25 + lh])
                return e.copy(fscr[:, 0:1], qn[:, qo + nheads * 128 - 1:qo + nheads * 128])
            t5 = q.on("scalar", s5, after=[t4, toks[j - 2] if j >= 2 else None])
            t5H[j] = t5

            def vrope(e, j=j, nheads=nheads, ncols=ncols, qo=qo):
                c = cosq[:, j * 64:(j + 1) * 64]
                s = sinq[:, j * 64:(j + 1) * 64]
                for lh in range(nheads):
                    a = qn[:, qo + lh * 128:qo + lh * 128 + 64]
                    b2 = qn[:, qo + lh * 128 + 64:qo + lh * 128 + 128]
                    e.tensor_tensor(vtmp[:, 0:64], a, c, ALU.mult)
                    e.tensor_tensor(vtmp[:, 64:128], b2, s, ALU.mult)
                    e.tensor_tensor(out_tiles[:, j * ncols + lh * 128:j * ncols + lh * 128 + 64],
                                    vtmp[:, 0:64], vtmp[:, 64:128], ALU.add)
                    e.tensor_tensor(vtmp[:, 0:64], b2, c, ALU.mult)
                    e.tensor_tensor(vtmp[:, 64:128], a, s, ALU.mult)
                    e.tensor_tensor(out_tiles[:, j * ncols + lh * 128 + 64:j * ncols + (lh + 1) * 128],
                                    vtmp[:, 0:64], vtmp[:, 64:128], ALU.subtract)
                return e.tensor_copy(fscr[:, 1:2],
                                     out_tiles[:, (j + 1) * ncols - 1:(j + 1) * ncols])
            toks[j] = q.on("vector", vrope, after=[t5, tLD])
        return tLastMM, toks, (t5H[14], t5H[15])

    tQuse, tQtok, qt5 = emit_proj_norm_rope(dr["e1wq"], 1024, 8, qro, None, RMS_EPS_L1,
                                            (None, None))
    tKuse, tKtok, kt5 = emit_proj_norm_rope(dr["e1wk"], 256, 2, kro, tQuse, RMS_EPS_L1, qt5)

    tTr = None
    tQst = None
    for a in range(8):
        for half in range(2):
            def trq(e, a=a, half=half):
                for jj in range(8):
                    j = 8 * half + jj
                    ins = e.transpose(psT[:, jj * 128:(jj + 1) * 128],
                                      qro[:, j * 1024 + a * 128:j * 1024 + (a + 1) * 128], idt)
                return ins
            tt = q.on("tensor", trq, after=[tQtok[15], tTr])

            eng = "scalar" if half == 0 else "vector"

            def evq(e, half=half, eng=eng):
                cp = e.copy if eng == "scalar" else e.tensor_copy
                for jj in range(8):
                    cp(orow[:, half * 1024 + jj * 128:half * 1024 + (jj + 1) * 128],
                       psT[:, jj * 128:(jj + 1) * 128])
                return cp(fscr[:, 2:3], orow[:, half * 1024 + 1023:half * 1024 + 1024])
            tTr = q.on(eng, evq, after=[tt, tTr, tQst if half == 0 else None])

        def sgq(e, a=a):
            e.tensor_scalar(orow[:], orow[:], qgc[:, a:a + 1], None, ALU.mult)
            return e.tensor_copy(fscr[:, 3:4], orow[:, S - 1:S])
        tG = q.on("vector", sgq, after=[tTr])

        def stq(e, sem, a=a):
            e.dma_start(out=dr["qT1"][a * 128:(a + 1) * 128, :], in_=orow[:]).then_inc(sem, 16)
        tQst = q.on("sync", stq, after=[tG], ndma=1)

    for a in range(2):
        for half in range(2):
            def trk2(e, a=a, half=half):
                for jj in range(8):
                    j = 8 * half + jj
                    ins = e.transpose(psT[:, jj * 128:(jj + 1) * 128],
                                      kro[:, j * 256 + a * 128:j * 256 + (a + 1) * 128], idt)
                return ins
            tt = q.on("tensor", trk2, after=[tKtok[15], tTr])

            eng = "scalar" if half == 0 else "vector"

            def evk2(e, half=half, eng=eng):
                cp = e.copy if eng == "scalar" else e.tensor_copy
                for jj in range(8):
                    cp(orow[:, half * 1024 + jj * 128:half * 1024 + (jj + 1) * 128],
                       psT[:, jj * 128:(jj + 1) * 128])
                return cp(fscr[:, 2:3], orow[:, half * 1024 + 1023:half * 1024 + 1024])
            tTr = q.on(eng, evk2, after=[tt, tTr, tQst if half == 0 else None])

        def stk(e, sem, a=a):
            e.dma_start(out=dr["kT1"][a * 128:(a + 1) * 128, :], in_=orow[:]).then_inc(sem, 16)
        tQst = q.on("sync", stk, after=[tTr], ndma=1)

    def ldwv(e, sem):
        for i in range(16):
            e.dma_start(out=wb[:, i * 256:(i + 1) * 256],
                        in_=dr["e1wv"][128 * i:128 * (i + 1), :]).then_inc(sem, 16)
    tWv = q.on("gpsimd", ldwv, after=[tKuse], ndma=16)

    tVev = None
    tVst = [None, None]
    for j in range(16):
        def vmm(e, j=j):
            for i in range(16):
                ins = e.matmul(psQ[:, 0:256],
                               xT[:, i * S + j * 128:i * S + j * 128 + 128],
                               wb[:, i * 256:(i + 1) * 256],
                               start=(i == 0), stop=(i == 15))
            return ins
        tV = q.on("tensor", vmm, after=[tWv, tVev, kt5[0] if j == 0 else None])

        def vev(e, j=j):
            e.tensor_copy(vo[:, (j % 2) * 256:(j % 2 + 1) * 256], psQ[:, 0:256])
            return e.tensor_copy(fscr[:, 4:5], vo[:, (j % 2) * 256 + 255:(j % 2) * 256 + 256])
        tVev = q.on("vector", vev, after=[tV, tVst[j % 2]])

        def stv(e, sem, j=j):
            e.dma_start(out=dr["vt1"][128 * j:128 * (j + 1), :],
                        in_=vo[:, (j % 2) * 256:(j % 2 + 1) * 256]).then_inc(sem, 16)
        tVst[j % 2] = q.on("sync", stv, after=[tVev], ndma=1)



def emit_causal_attn(nc, q, pool, dr, G, kvc, kvdiv, qTd, kTd, vtd, agin, agout):
    pool.reset()
    sb = pool.take
    SCALE = float(1.0 / np.sqrt(128.0))
    qT = sb("qT_s", [128, 8 * S], BF16)
    kT = sb("kT_s", [128, (kvc // 128) * S], BF16)
    vtok = sb("vt_s", [128, 16 * kvc], BF16)
    comb = sb("comb", [128, 16 * 1024], BF16)
    qs = sb("qs", [128, S], BF16)
    SS = sb("SS", [128, 2 * S], BF16)
    P = sb("P", [128, 2 * S], BF16)
    PT = sb("PT", [128, 16 * 128], BF16)
    ctr = sb("ctr", [128, 2048], BF16)
    cols = sb("cols", [128, 16], F32)
    fscr = sb("fscr", [128, 8], F32)
    idt = G["idt"]
    maskt = G["maskt"]
    psS = G["psBIG"]
    psT = G["psTr"]
    psA = G["ps3"]

    def ld(e, sem):
        for lh in range(8):
            e.dma_start(out=qT[:, lh * S:(lh + 1) * S],
                        in_=qTd[128 * lh:128 * (lh + 1), :]).then_inc(sem, 16)
        for kk in range(kvc // 128):
            e.dma_start(out=kT[:, kk * S:(kk + 1) * S],
                        in_=kTd[128 * kk:128 * (kk + 1), :]).then_inc(sem, 16)
        for j in range(16):
            e.dma_start(out=vtok[:, j * kvc:(j + 1) * kvc],
                        in_=vtd[128 * j:128 * (j + 1), :]).then_inc(sem, 16)
    tLD = q.on("sync", ld, ndma=24 + kvc // 128)

    tC = None
    tPTev = None
    tV1H = [None] * 16
    tV1A = [None] * 16
    tExH = [None] * 16
    tTrpH = [None] * 16
    tRH = [None] * 16
    tWSH = [None] * 16
    tWSPrev = None
    for lh in range(8):
        hOFF = lh * S
        kOFF = (lh // kvdiv) * S if kvc != 1024 else lh * S
        voff = (lh // kvdiv) * 128 if kvc != 1024 else lh * 128

        def qsv(e, hOFF=hOFF):
            e.tensor_scalar_mul(qs[:], qT[:, hOFF:hOFF + S], SCALE)
            return e.tensor_copy(fscr[:, 2:3], qs[:, S - 1:S])
        tQS = q.on("vector", qsv, after=[tLD, tC])

        tV1Head = list(tV1H)

        def flush(j):
            nonlocal tC, tPTev, tWSPrev
            pb = 8 * (j % 2)
            tEx = tExH[j]

            def vrec(e, pb=pb):
                e.reciprocal(cols[:, pb + 4:pb + 5], cols[:, pb + 3:pb + 4])
                return e.tensor_copy(cols[:, pb + 5:pb + 6], cols[:, pb + 4:pb + 5])
            tR = q.on("vector", vrec, after=[tEx])

            def swc(e, pb=pb):
                e.copy(cols[:, pb + 6:pb + 7], cols[:, pb + 4:pb + 5])
                return e.copy(fscr[:, 7:8], cols[:, pb + 6:pb + 7])
            tWS = q.on("scalar", swc, after=[tR, tWSPrev, tC])
            tWSPrev = tWS

            nblk = j + 1
            tPh = None
            tPhs = []
            for b0 in range(0, nblk, 8):
                bn = min(8, nblk - b0)

                def trp(e, b0=b0, bn=bn, j=j):
                    for i in range(bn):
                        ins = e.transpose(psT[:, i * 128:(i + 1) * 128],
                                          P[:, (j % 2) * S + (b0 + i) * 128:(j % 2) * S + (b0 + i + 1) * 128], idt)
                    return ins
                tt = q.on("tensor", trp, after=[tEx, tPTev])
                tTrpH[j] = tt
                eng = "scalar" if (b0 // 8) % 2 == 0 else "vector"

                def evp(e, b0=b0, bn=bn, eng=eng):
                    cp = e.copy if eng == "scalar" else e.tensor_copy
                    for i in range(bn):
                        cp(PT[:, (b0 + i) * 128:(b0 + i + 1) * 128],
                           psT[:, i * 128:(i + 1) * 128])
                    return cp(fscr[:, 4:5], PT[:, (b0 + bn) * 128 - 1:(b0 + bn) * 128])
                tPTev = q.on(eng, evp, after=[tt, tPTev])
                tPh = tPTev
                tPhs.append(tPTev)

            if nblk > 8:
                def ammA(e, j=j, voff=voff):
                    for jj in range(8):
                        ins = e.matmul(psA[:, 0:128],
                                       PT[:, jj * 128:(jj + 1) * 128],
                                       vtok[:, jj * kvc + voff:jj * kvc + voff + 128],
                                       start=(jj == 0), stop=False)
                    return ins
                tAmA = q.on("tensor", ammA, after=[tPhs[0], tC])

                def ammB(e, j=j, voff=voff, nblk=nblk):
                    for jj in range(8, nblk):
                        ins = e.matmul(psA[:, 0:128],
                                       PT[:, jj * 128:(jj + 1) * 128],
                                       vtok[:, jj * kvc + voff:jj * kvc + voff + 128],
                                       start=False, stop=(jj == nblk - 1))
                    return ins
                tAm = q.on("tensor", ammB, after=[tPhs[1]])
            else:
                def amm(e, j=j, voff=voff, nblk=nblk):
                    for jj in range(nblk):
                        ins = e.matmul(psA[:, 0:128],
                                       PT[:, jj * 128:(jj + 1) * 128],
                                       vtok[:, jj * kvc + voff:jj * kvc + voff + 128],
                                       start=(jj == 0), stop=(jj == nblk - 1))
                    return ins
                tAm = q.on("tensor", amm, after=[tPh, tC])

            def combv(e, lh=lh, j=j, pb=pb):
                e.tensor_scalar(comb[:, j * 1024 + lh * 128:j * 1024 + (lh + 1) * 128],
                                psA[:, 0:128], cols[:, pb + 6:pb + 7], None, ALU.mult)
                return e.tensor_copy(fscr[:, 2:3],
                                     comb[:, j * 1024 + (lh + 1) * 128 - 1:j * 1024 + (lh + 1) * 128])
            tC = q.on("vector", combv, after=[tAm, tWS])

        for jq in range(16):
            kmax = (jq + 1) * 128
            pb = 8 * (jq % 2)
            if jq < 8:
                buf = psS[:, 1024 * (jq % 2):1024 * (jq % 2) + 1024]
                if jq >= 2:
                    sdep = [tV1H[jq - 2]]
                else:
                    sdep = [tV1Head[15], tV1Head[14]]
            else:
                buf = psS
                sdep = [tV1H[6], tV1H[7]] if jq == 8 else [tV1H[jq - 1]]

            if jq < 8:
                def smm(e, jq=jq, kOFF=kOFF, buf=buf):
                    kx = (jq + 1) * 128
                    n = 0
                    while n * 512 < kx:
                        w = min(512, kx - n * 512)
                        ins = e.matmul(buf[:, n * 512:n * 512 + w],
                                       qs[:, jq * 128:(jq + 1) * 128],
                                       kT[:, kOFF + n * 512:kOFF + n * 512 + w],
                                       start=True, stop=True)
                        n += 1
                    return ins
                tS = q.on("tensor", smm, after=[tQS] + sdep)

                def vss(e, jq=jq, kmax=kmax, buf=buf, pb=pb):
                    po = (jq % 2) * S
                    if jq:
                        e.tensor_copy(SS[:, po:po + jq * 128], buf[:, 0:jq * 128])
                    e.scalar_tensor_tensor(SS[:, po + jq * 128:po + kmax], buf[:, jq * 128:kmax],
                                           1.0, maskt, ALU.mult, ALU.add)
                    e.reduce_max(cols[:, pb:pb + 1], SS[:, po:po + kmax], axis=AX, negate=True)
                    return e.tensor_copy(cols[:, pb + 1:pb + 2], cols[:, pb:pb + 1])
                tV1 = q.on("vector", vss, after=[tS, tExH[jq - 2] if jq >= 2 else None])
                tV1H[jq] = tV1
                tV1A[jq] = tV1
            else:
                sdepA = [tV1A[6]] if jq == 8 else [tV1A[jq - 1]]
                sdepB = [tV1H[7]] if jq == 8 else [tV1H[jq - 1]]

                def smmA(e, jq=jq, kOFF=kOFF, buf=buf):
                    for n in range(2):
                        ins = e.matmul(buf[:, n * 512:n * 512 + 512],
                                       qs[:, jq * 128:(jq + 1) * 128],
                                       kT[:, kOFF + n * 512:kOFF + n * 512 + 512],
                                       start=True, stop=True)
                    return ins
                tSA = q.on("tensor", smmA, after=[tQS] + sdepA)

                def smmB(e, jq=jq, kOFF=kOFF, buf=buf):
                    kx = (jq + 1) * 128
                    n = 2
                    while n * 512 < kx:
                        w = min(512, kx - n * 512)
                        ins = e.matmul(buf[:, n * 512:n * 512 + w],
                                       qs[:, jq * 128:(jq + 1) * 128],
                                       kT[:, kOFF + n * 512:kOFF + n * 512 + w],
                                       start=True, stop=True)
                        n += 1
                    return ins
                tSB = q.on("tensor", smmB, after=sdepB)

                def vssA(e, jq=jq, buf=buf):
                    po = (jq % 2) * S
                    e.tensor_copy(SS[:, po:po + 1024], buf[:, 0:1024])
                    return e.tensor_copy(fscr[:, 5:6], SS[:, po + 1023:po + 1024])
                tV1A[jq] = q.on("vector", vssA, after=[tSA, tExH[jq - 2]])

                def vssB(e, jq=jq, kmax=kmax, buf=buf, pb=pb):
                    po = (jq % 2) * S
                    if jq * 128 > 1024:
                        e.tensor_copy(SS[:, po + 1024:po + jq * 128], buf[:, 1024:jq * 128])
                    e.scalar_tensor_tensor(SS[:, po + jq * 128:po + kmax], buf[:, jq * 128:kmax],
                                           1.0, maskt, ALU.mult, ALU.add)
                    e.reduce_max(cols[:, pb:pb + 1], SS[:, po:po + kmax], axis=AX, negate=True)
                    return e.tensor_copy(cols[:, pb + 1:pb + 2], cols[:, pb:pb + 1])
                tV1 = q.on("vector", vssB, after=[tSB])
                tV1H[jq] = tV1

            def sexp(e, kmax=kmax, jq=jq, pb=pb):
                po = (jq % 2) * S
                e.activation(P[:, po:po + kmax], SS[:, po:po + kmax], AF.Exp,
                             bias=cols[:, pb:pb + 1], scale=1.0,
                             accum_out=cols[:, pb + 3:pb + 4])
                e.copy(fscr[:, 6:7], P[:, po + kmax - 1:po + kmax])
                return e.copy(fscr[:, 6:7], cols[:, pb + 3:pb + 4])
            tEx = q.on("scalar", sexp, after=[tV1, tTrpH[jq - 2] if jq >= 2 else tTrpH[14 + (jq % 2)]])
            tExH[jq] = tEx

            if jq:
                flush(jq - 1)
        flush(15)

    stores = emit_comb_out(q, comb, idt, psT, ctr, fscr, agin, tC)
    tAG = q.cc(lambda e: e.collective_compute(
        "AllGather", ALU.bypass, replica_groups=G["rg8"],
        ins=[agin.opt()], outs=[agout.opt()]),
        after=stores)
    return tAG



def emit_oproj_block(nc, q, pool, dr, G, apply_ln, agout_src, tAGsrc, woT_d, x_d,
                     lnw_d, yf_d, agin, agout):
    pool.reset()
    sb = pool.take
    cbT = sb("cbT_s", [128, 16 * 1024], BF16)
    woT = sb("woT_s", [128, 16 * H], BF16)
    lnw = sb("lnw_s", [128, H], F32)
    xres = sb("xres", [128, 2 * H], F32)
    xo = sb("xo", [128, H], F32)
    xc = sb("xc", [128, H], F32)
    scrap = sb("scrap", [128, H], F32)
    yf = sb("yf_s", [128, 2 * H], F32)
    rn = sb("rn_s", [128, H], BF16)
    xtT = sb("xtT", [128, 16 * 1024], BF16)
    cols = sb("cols", [128, 16], F32)
    fscr = sb("fscr", [128, 8], F32)
    psO = G["psBIG"]
    tr = XTrOut(q, G["idt"], G["psTr"], xtT, fscr, agin)

    def ldc(e, sem):
        for k in range(16):
            e.indirect_dma_start(
                out=cbT[:, k * 1024:(k + 1) * 1024], out_offset=None,
                in_=agout_src,
                in_offset=bass.IndirectOffsetOnAxis(G["idxB"][:, k:k + 1], axis=0),
            ).then_inc(sem, 16)
    tLD = q.on("gpsimd", ldc, after=[tAGsrc], ndma=16)

    def ldw(e, sem):
        for i in range(16):
            e.dma_start(out=woT[:, i * H:(i + 1) * H],
                        in_=woT_d[128 * i:128 * (i + 1), :]).then_inc(sem, 16)
        e.dma_start(out=lnw[:], in_=lnw_d.broadcast_to([128, H])).then_inc(sem, 16)
    tW = q.on("gpsimd", ldw, ndma=17)

    tXO = [None] * 8
    tStore = [None] * 8
    for jt in range(8):
        slot = jt % 2

        def ldx(e, sem, jt=jt, slot=slot):
            e.dma_start(out=xres[:, slot * H:(slot + 1) * H],
                        in_=x_d[128 * jt:128 * (jt + 1), :]).then_inc(sem, 16)
        tX = q.on("sync", ldx, after=[tXO[jt - 2] if jt >= 2 else None], ndma=1)

        def mm(e, jt=jt):
            for n in range(4):
                for i in range(16):
                    ins = e.matmul(psO[:, n * 512:(n + 1) * 512],
                                   cbT[:, i * 1024 + jt * 128:i * 1024 + (jt + 1) * 128],
                                   woT[:, i * H + n * 512:i * H + (n + 1) * 512],
                                   start=(i == 0), stop=(i == 15))
            return ins
        tM = q.on("tensor", mm, after=[tLD, tW, tXO[jt - 1] if jt >= 1 else None])

        def v_xo(e, slot=slot):
            e.scalar_tensor_tensor(xo[:], psO[:], 1.0, xres[:, slot * H:(slot + 1) * H],
                                   ALU.mult, ALU.add, accum_out=cols[:, 0:1])
            return e.tensor_copy(fscr[:, 1:2], cols[:, 0:1])
        tXO[jt] = q.on("vector", v_xo, after=[tM, tX])

        if apply_ln:
            def apply(e, slot=slot):
                e.activation(yf[:, slot * H:(slot + 1) * H], xc[:], AF.Copy,
                             scale=cols[:, 5:6])
                e.activation(scrap[:], yf[:, slot * H:(slot + 1) * H], AF.Square,
                             accum_out=cols[:, 8:9])
                return e.copy(fscr[:, 0:1], cols[:, 8:9])
            tA1 = ln_zigzag(q, tXO[jt], xo, xc, scrap, cols, fscr, 0, H, 1e-5, apply,
                            extra_after=[tStore[jt - 2] if jt >= 2 else None,
                                         tr.done[jt - 2] if jt >= 2 else None])
        else:
            def apply(e, slot=slot):
                e.copy(yf[:, slot * H:(slot + 1) * H], xo[:])
                e.activation(scrap[:], xo[:], AF.Square, accum_out=cols[:, 8:9])
                return e.copy(fscr[:, 0:1], cols[:, 8:9])
            tA1 = q.on("scalar", apply, after=[tXO[jt],
                                              tStore[jt - 2] if jt >= 2 else None,
                                              tr.done[jt - 2] if jt >= 2 else None])

        def v_rw(e):
            e.tensor_scalar(cols[:, 9:10], cols[:, 8:9], 1.0 / H, 1e-5, ALU.mult, ALU.add)
            return e.tensor_copy(fscr[:, 1:2], cols[:, 9:10])
        t6 = q.on("vector", v_rw, after=[tA1])

        def s_rs(e):
            e.activation(cols[:, 10:11], cols[:, 9:10], AF.Sqrt)
            return e.copy(fscr[:, 0:1], cols[:, 10:11])
        t7 = q.on("scalar", s_rs, after=[t6])

        def v_rr(e):
            e.reciprocal(cols[:, 11:12], cols[:, 10:11])
            return e.tensor_copy(fscr[:, 1:2], cols[:, 11:12])
        t8 = q.on("vector", v_rr, after=[t7])

        def s_rn(e, slot=slot):
            e.activation(scrap[:], yf[:, slot * H:(slot + 1) * H], AF.Copy,
                         scale=cols[:, 11:12])
            return e.copy(fscr[:, 0:1], scrap[:, H - 1:H])
        t9 = q.on("scalar", s_rn, after=[t8])

        def v_rn(e):
            e.tensor_tensor(rn[:], scrap[:], lnw[:], ALU.mult)
            return e.tensor_copy(fscr[:, 1:2], rn[:, H - 1:H])
        t10 = q.on("vector", v_rn, after=[t9, tW, tr.done[jt - 1] if jt >= 1 else None])

        def st(e, sem, jt=jt, slot=slot):
            e.dma_start(out=yf_d[128 * jt:128 * (jt + 1), :],
                        in_=yf[:, slot * H:(slot + 1) * H]).then_inc(sem, 16)
        tStore[jt] = q.on("sync", st, after=[t10], ndma=1)

        tr.add_tile(jt, rn[:], t10)

    tSt = tr.store()
    tAG = q.cc(lambda e: e.collective_compute(
        "AllGather", ALU.bypass, replica_groups=G["rg8"],
        ins=[agin.opt()], outs=[agout.opt()]),
        after=[tSt, tStore[6], tStore[7]])
    return tAG



def emit_mla_proj(nc, q, pool, dr, G):
    pool.reset()
    sb = pool.take
    xcol = sb("xcol", [128, 16 * 512], BF16)
    kvdb = sb("kvdb", [128, 16 * 512], BF16)
    qdb = sb("qdb", [128, 16 * 512], BF16)
    kvdT = sb("kvdT", [128, 4 * S], BF16)
    qdT = sb("qdT", [128, 4 * S], BF16)
    kub = sb("kub", [128, 4 * 512], BF16)
    qub = sb("qub", [128, 4 * 512], BF16)
    vub = sb("vub", [128, 4 * 1024], BF16)
    rkb = sb("rkb", [128, 16 * 512], BF16)
    rqb = sb("rqb", [128, 4 * 512], BF16)
    ktot = sb("ktot", [128, 16 * 1024], BF16)
    qtot = sb("qtot", [128, 16 * 1024], BF16)
    vtot = sb("vtot", [128, 2 * 1024], BF16)
    cosm = sb("cosm_s", [128, 16 * 32], F32)
    sinm = sb("sinm_s", [128, 16 * 32], F32)
    rf = sb("rf", [128, 512], F32)
    rs = sb("rs", [128, 64], F32)
    orow = sb("orow", [128, S], BF16)
    fscr = sb("fscr", [128, 8], F32)
    idt = G["idt"]
    psA = G["ps3"]
    psB = G["ps2"][:, 0:512]
    psV = G["psBIG"][:, 0:1024]
    psT = G["psTr"]

    def ld(e, sem):
        e.dma_start(out=cosm[:], in_=dr["mcos"]).then_inc(sem, 16)
        e.dma_start(out=sinm[:], in_=dr["msin"]).then_inc(sem, 16)
    tLD = q.on("sync", ld, ndma=2)

    def ldw(e, sem):
        for i in range(16):
            e.dma_start(out=kvdb[:, i * 512:(i + 1) * 512],
                        in_=dr["kvd"][128 * i:128 * (i + 1), :]).then_inc(sem, 16)
        for i in range(16):
            e.dma_start(out=qdb[:, i * 512:(i + 1) * 512],
                        in_=dr["qd"][128 * i:128 * (i + 1), :]).then_inc(sem, 16)
        for i in range(16):
            e.dma_start(out=rkb[:, i * 512:(i + 1) * 512],
                        in_=dr["rk"][128 * i:128 * (i + 1), :]).then_inc(sem, 16)
        for lt in range(4):
            e.dma_start(out=kub[:, lt * 512:(lt + 1) * 512],
                        in_=dr["ku"][128 * lt:128 * (lt + 1), :]).then_inc(sem, 16)
            e.dma_start(out=qub[:, lt * 512:(lt + 1) * 512],
                        in_=dr["qu"][128 * lt:128 * (lt + 1), :]).then_inc(sem, 16)
            e.dma_start(out=vub[:, lt * 1024:(lt + 1) * 1024],
                        in_=dr["vu"][128 * lt:128 * (lt + 1), :]).then_inc(sem, 16)
            e.dma_start(out=rqb[:, lt * 512:(lt + 1) * 512],
                        in_=dr["rq"][128 * lt:128 * (lt + 1), :]).then_inc(sem, 16)
    tW = q.on("gpsimd", ldw, ndma=64)

    tXc = None
    tEvPrev = None
    tRopeK = None
    tKrLast = None
    for sc in range(4):
        def ldxc(e, sem, sc=sc):
            for i in range(16):
                e.indirect_dma_start(
                    out=xcol[:, i * 512:(i + 1) * 512], out_offset=None,
                    in_=dr["agout_xn1"],
                    in_offset=bass.IndirectOffsetOnAxis(
                        G["idxPair"][:, 2 * i + sc // 2:2 * i + sc // 2 + 1], axis=0),
                    element_offset=512 * (sc % 2),
                ).then_inc(sem, 16)
        tXc = q.on("gpsimd", ldxc, after=[tEvPrev, tKrLast, dr["tAGxn1"] if sc == 0 else None], ndma=16)

        for which, (wsrc, dstT) in enumerate([(kvdb, kvdT), (qdb, qdT)]):
            for lt in range(4):
                def dmm(e, sc=sc, lt=lt, wsrc=wsrc):
                    for i in range(16):
                        ins = e.matmul(psA[:, 0:512],
                                       wsrc[:, i * 512 + lt * 128:i * 512 + (lt + 1) * 128],
                                       xcol[:, i * 512:(i + 1) * 512],
                                       start=(i == 0), stop=(i == 15))
                    return ins
                tM = q.on("tensor", dmm, after=[tXc, tW, tEvPrev])

                def ev(e, sc=sc, lt=lt, dstT=dstT):
                    e.tensor_copy(dstT[:, lt * S + sc * 512:lt * S + (sc + 1) * 512], psA[:, 0:512])
                    return e.tensor_copy(fscr[:, 0:1],
                                         dstT[:, lt * S + (sc + 1) * 512 - 1:lt * S + (sc + 1) * 512])
                tEvPrev = q.on("vector", ev, after=[tM])

        for jj in range(4):
            j = sc * 4 + jj

            def krmm(e, jj=jj):
                for i in range(16):
                    ins = e.matmul(psB[:, 0:512],
                                   xcol[:, i * 512 + jj * 128:i * 512 + (jj + 1) * 128],
                                   rkb[:, i * 512:(i + 1) * 512],
                                   start=(i == 0), stop=(i == 15))
                return ins
            tKr = q.on("tensor", krmm, after=[tXc, tW, tRopeK])
            tKrLast = tKr

            def evkr(e):
                e.tensor_copy(rf[:], psB[:, 0:512])
                return e.tensor_copy(fscr[:, 1:2], rf[:, 511:512])
            tEvK = q.on("vector", evkr, after=[tKr])

            def ropek(e, j=j):
                c = cosm[:, j * 32:(j + 1) * 32]
                s = sinm[:, j * 32:(j + 1) * 32]
                for n in range(8):
                    x1 = rf[:, n * 64:n * 64 + 32]
                    x2 = rf[:, n * 64 + 32:n * 64 + 64]
                    e.tensor_tensor(rs[:, 0:32], x1, c, ALU.mult)
                    e.tensor_tensor(rs[:, 32:64], x2, s, ALU.mult)
                    e.tensor_tensor(ktot[:, j * 1024 + n * 128 + 64:j * 1024 + n * 128 + 96],
                                    rs[:, 0:32], rs[:, 32:64], ALU.subtract)
                    e.tensor_tensor(rs[:, 0:32], x2, c, ALU.mult)
                    e.tensor_tensor(rs[:, 32:64], x1, s, ALU.mult)
                    e.tensor_tensor(ktot[:, j * 1024 + n * 128 + 96:j * 1024 + (n + 1) * 128],
                                    rs[:, 0:32], rs[:, 32:64], ALU.add)
                return e.tensor_copy(fscr[:, 1:2], ktot[:, (j + 1) * 1024 - 1:(j + 1) * 1024])
            tRopeK = q.on("vector", ropek, after=[tEvK, tLD])

    tVst = [None, None]
    tPrevA = tRopeK
    tPrevB = tRopeK
    tPrevV = None
    for j in range(16):
        def kpmm(e, j=j):
            for lt in range(4):
                ins = e.matmul(psA[:, 0:512],
                               kvdT[:, lt * S + j * 128:lt * S + (j + 1) * 128],
                               kub[:, lt * 512:(lt + 1) * 512],
                               start=(lt == 0), stop=(lt == 3))
            return ins
        tKp = q.on("tensor", kpmm, after=[tEvPrev, tPrevA])

        def evkp(e, j=j):
            for n in range(8):
                e.tensor_copy(ktot[:, j * 1024 + n * 128:j * 1024 + n * 128 + 64],
                              psA[:, n * 64:(n + 1) * 64])
            return e.tensor_copy(fscr[:, 2:3], ktot[:, j * 1024 + 7 * 128 + 63:j * 1024 + 7 * 128 + 64])
        tPrevA = q.on("vector", evkp, after=[tKp])

        def qpmm(e, j=j):
            for lt in range(4):
                ins = e.matmul(psB[:, 0:512],
                               qdT[:, lt * S + j * 128:lt * S + (j + 1) * 128],
                               qub[:, lt * 512:(lt + 1) * 512],
                               start=(lt == 0), stop=(lt == 3))
            return ins
        tQp = q.on("tensor", qpmm, after=[tPrevB])

        def evqp(e, j=j):
            for n in range(8):
                e.tensor_copy(qtot[:, j * 1024 + n * 128:j * 1024 + n * 128 + 64],
                              psB[:, n * 64:(n + 1) * 64])
            return e.tensor_copy(fscr[:, 3:4], qtot[:, j * 1024 + 7 * 128 + 63:j * 1024 + 7 * 128 + 64])
        tPrevB = q.on("vector", evqp, after=[tQp])

        def qrmm(e, j=j):
            for lt in range(4):
                ins = e.matmul(psB[:, 0:512],
                               qdT[:, lt * S + j * 128:lt * S + (j + 1) * 128],
                               rqb[:, lt * 512:(lt + 1) * 512],
                               start=(lt == 0), stop=(lt == 3))
            return ins
        tQr = q.on("tensor", qrmm, after=[tPrevB])

        def evqr(e):
            e.tensor_copy(rf[:], psB[:, 0:512])
            return e.tensor_copy(fscr[:, 1:2], rf[:, 511:512])
        tEvQr = q.on("vector", evqr, after=[tQr])

        def ropeq(e, j=j):
            c = cosm[:, j * 32:(j + 1) * 32]
            s = sinm[:, j * 32:(j + 1) * 32]
            for n in range(8):
                x1 = rf[:, n * 64:n * 64 + 32]
                x2 = rf[:, n * 64 + 32:n * 64 + 64]
                e.tensor_tensor(rs[:, 0:32], x1, c, ALU.mult)
                e.tensor_tensor(rs[:, 32:64], x2, s, ALU.mult)
                e.tensor_tensor(qtot[:, j * 1024 + n * 128 + 64:j * 1024 + n * 128 + 96],
                                rs[:, 0:32], rs[:, 32:64], ALU.subtract)
                e.tensor_tensor(rs[:, 0:32], x2, c, ALU.mult)
                e.tensor_tensor(rs[:, 32:64], x1, s, ALU.mult)
                e.tensor_tensor(qtot[:, j * 1024 + n * 128 + 96:j * 1024 + (n + 1) * 128],
                                rs[:, 0:32], rs[:, 32:64], ALU.add)
            return e.tensor_copy(fscr[:, 1:2], qtot[:, (j + 1) * 1024 - 1:(j + 1) * 1024])
        tPrevB = q.on("vector", ropeq, after=[tEvQr, tLD])

        def vmm(e, j=j):
            for n in range(2):
                for lt in range(4):
                    ins = e.matmul(psV[:, n * 512:(n + 1) * 512],
                                   kvdT[:, lt * S + j * 128:lt * S + (j + 1) * 128],
                                   vub[:, lt * 1024 + n * 512:lt * 1024 + (n + 1) * 512],
                                   start=(lt == 0), stop=(lt == 3))
            return ins
        tV = q.on("tensor", vmm, after=[tEvPrev, tPrevV])

        def vev(e, j=j):
            e.tensor_copy(vtot[:, (j % 2) * 1024:(j % 2 + 1) * 1024], psV[:])
            return e.tensor_copy(fscr[:, 4:5], vtot[:, (j % 2) * 1024 + 1023:(j % 2) * 1024 + 1024])
        tPrevV = q.on("vector", vev, after=[tV, tVst[j % 2]])

        def stv(e, sem, j=j):
            e.dma_start(out=dr["vt2"][128 * j:128 * (j + 1), :],
                        in_=vtot[:, (j % 2) * 1024:(j % 2 + 1) * 1024]).then_inc(sem, 16)
        tVst[j % 2] = q.on("sync", stv, after=[tPrevV], ndma=1)

    tTr = None
    tQst = None
    for src_t, dst_d, tok0 in ((qtot, dr["qT2"], tPrevB), (ktot, dr["kT2"], tPrevA)):
        for a in range(8):
            for half in range(2):
                def trq(e, a=a, half=half, src_t=src_t):
                    for jj in range(8):
                        j = 8 * half + jj
                        ins = e.transpose(psT[:, jj * 128:(jj + 1) * 128],
                                          src_t[:, j * 1024 + a * 128:j * 1024 + (a + 1) * 128], idt)
                    return ins
                tt = q.on("tensor", trq, after=[tok0, tRopeK, tTr])

                eng = "scalar" if half == 0 else "vector"

                def evq(e, half=half, eng=eng):
                    cp = e.copy if eng == "scalar" else e.tensor_copy
                    for jj in range(8):
                        cp(orow[:, half * 1024 + jj * 128:half * 1024 + (jj + 1) * 128],
                           psT[:, jj * 128:(jj + 1) * 128])
                    return cp(fscr[:, 2:3], orow[:, half * 1024 + 1023:half * 1024 + 1024])
                tTr = q.on(eng, evq, after=[tt, tTr, tQst if half == 0 else None])

            def stq(e, sem, a=a, dst_d=dst_d):
                e.dma_start(out=dst_d[a * 128:(a + 1) * 128, :], in_=orow[:]).then_inc(sem, 16)
            tQst = q.on("sync", stq, after=[tTr], ndma=1)



def emit_moe(nc, q, pool, dr, G):
    pool.reset()
    sb = pool.take
    xcb = sb("xcb", [128, 16 * 512], BF16)
    wgb = sb("wgb", [128, 16 * 1024], BF16)
    wub = sb("wub", [128, 16 * 1024], BF16)
    wdb = sb("wdb", [128, 8 * H], BF16)
    rwb = sb("rwb", [128, 16 * 7], BF16)
    rbb = sb("rbb", [128, 7], F32)
    wselc = sb("wselc", [128, 2], F32)
    hT = sb("hT", [128, 8 * 512], BF16)
    sgf = sb("sgf", [128, 512], F32)
    t1f = sb("t1f", [128, 512], F32)
    lgAll = sb("lgAll", [128, 448], F32)
    prAll = sb("prAll", [128, 448], F32)
    m1rep = sb("m1rep", [128, 448], F32)
    eqw = sb("eqw", [128, 448], F32)
    pmAll = sb("pmAll", [128, 448], F32)
    m1 = sb("m1", [128, 64], F32)
    m2 = sb("m2", [128, 64], F32)
    dc = sb("dc", [128, 64], F32)
    sgc = sb("sgc", [128, 64], F32)
    u1 = sb("u1", [128, 64], F32)
    nn = sb("nn", [128, 64], F32)
    s12 = sb("s12", [128, 64], F32)
    invc = sb("invc", [128, 64], F32)
    wAll = sb("wAll", [128, 64], F32)
    dscr = sb("dscr", [128, 512], F32)
    obuf = sb("obuf", [128, 2 * H], BF16)
    fscr = sb("fscr", [128, 64], F32)
    psR = G["ps2"][:, 0:512]
    psG = G["ps2"][:, 512:1024]
    psU = G["ps3"]
    psD = G["psBIG"]
    xsrc = dr["agout_xn2"]

    def xld(e, sem, tc):
        r, u = tc // 2, tc % 2
        for i in range(16):
            e.dma_start(out=xcb[:, i * 512:(i + 1) * 512],
                        in_=xsrc[2048 * r + 128 * i:2048 * r + 128 * (i + 1),
                                 512 * u:512 * (u + 1)]).then_inc(sem, 16)

    def ld(e, sem):
        e.dma_start(out=rbb[:], in_=dr["mrb"].broadcast_to([128, 7])).then_inc(sem, 16)
        e.dma_start(out=wselc[:], in_=dr["mwsel"].broadcast_to([128, 2])).then_inc(sem, 16)
    tLD = q.on("sync", ld, ndma=2)

    def ldw(e, sem):
        for i in range(16):
            e.dma_start(out=rwb[:, i * 7:(i + 1) * 7],
                        in_=dr["mrw"][128 * i:128 * (i + 1), :]).then_inc(sem, 16)
        for i in range(16):
            e.dma_start(out=wgb[:, i * 1024:(i + 1) * 1024],
                        in_=dr["mwg"][128 * i:128 * (i + 1), :]).then_inc(sem, 16)
        for i in range(16):
            e.dma_start(out=wub[:, i * 1024:(i + 1) * 1024],
                        in_=dr["mwu"][128 * i:128 * (i + 1), :]).then_inc(sem, 16)
        for i in range(8):
            e.dma_start(out=wdb[:, i * H:(i + 1) * H],
                        in_=dr["mwd"][128 * i:128 * (i + 1), :]).then_inc(sem, 16)
    tW = q.on("gpsimd", ldw, ndma=56)

    tXc = None
    tREv = None
    for tc in range(16):
        tXc = q.on("sync", lambda e, sem, tc=tc: xld(e, sem, tc),
                   after=[tREv, dr["tAGxn2"] if tc == 0 else None], ndma=16)

        for u in range(4):
            tt = tc * 4 + u

            def rmm(e, u=u):
                for i in range(16):
                    ins = e.matmul(psR[:, (u % 2) * 128:(u % 2) * 128 + 7],
                                   xcb[:, i * 512 + u * 128:i * 512 + (u + 1) * 128],
                                   rwb[:, i * 7:(i + 1) * 7],
                                   start=(i == 0), stop=(i == 15))
                return ins
            tRm = q.on("tensor", rmm, after=[tXc, tW, tREv])

            def rev(e, tt=tt, u=u):
                e.scalar_tensor_tensor(
                    lgAll[:].rearrange("p (s g) -> p s g", s=7)[:, :, tt:tt + 1],
                    psR[:, (u % 2) * 128:(u % 2) * 128 + 7], 1.0, rbb[:],
                    ALU.mult, ALU.add)
                return e.tensor_copy(fscr[:, 0:1], lgAll[:, 384 + tt:385 + tt])
            tREv = q.on("vector", rev, after=[tRm])

    def sprob(e):
        e.activation(prAll[:], lgAll[:], AF.Sigmoid)
        return e.copy(fscr[:, 0:32], prAll[:, 416:448])
    tS0 = q.on("scalar", sprob, after=[tREv])

    def vm1(e):
        e.reduce_max(m1[:], prAll[:].rearrange("p (s g) -> p g s", g=64), axis=AX)
        return e.tensor_copy(fscr[:, 0:64], m1[:])
    tM1 = q.on("vector", vm1, after=[tS0])

    def repm1(e, sem):
        e.dma_start(out=m1rep[:].rearrange("p (s g) -> p s g", s=7),
                    in_=m1[:].unsqueeze(1).broadcast_to([128, 7, 64])).then_inc(sem, 16)
    tRep = q.on("gpsimd", repm1, after=[tM1], ndma=1)

    def vm2(e):
        e.tensor_tensor(eqw[:], prAll[:], m1rep[:], ALU.is_equal)
        e.scalar_tensor_tensor(pmAll[:], eqw[:], -1e30, prAll[:], ALU.mult, ALU.add)
        e.reduce_max(m2[:], pmAll[:].rearrange("p (s g) -> p g s", g=64), axis=AX)
        return e.tensor_copy(fscr[:, 0:64], m2[:])
    tM2 = q.on("vector", vm2, after=[tRep])

    def sm2(e):
        e.copy(dscr[:, 0:64], m2[:])
        return e.copy(fscr[:, 0:64], dscr[:, 0:64])
    tM2h = q.on("scalar", sm2, after=[tM2])

    def vdc(e):
        e.scalar_tensor_tensor(dc[:], dscr[:, 0:64], -0.9999999, prAll[:, 0:64],
                               ALU.mult, ALU.add)
        e.tensor_tensor(s12[:], m1[:], dscr[:, 0:64], ALU.add)
        return e.tensor_copy(fscr[:, 0:64], dc[:])
    tDc = q.on("vector", vdc, after=[tM2h])

    def ssg(e):
        e.activation(sgc[:], dc[:], AF.Sign)
        return e.copy(fscr[:, 0:64], sgc[:])
    tSg = q.on("scalar", ssg, after=[tDc])

    def vwf(e):
        e.reciprocal(invc[:], s12[:])
        e.tensor_tensor(u1[:], sgc[:], prAll[:, 0:64], ALU.mult)
        e.tensor_copy(dscr[:], dscr[:])
        e.tensor_tensor(nn[:], u1[:], prAll[:, 0:64], ALU.add)
        e.tensor_copy(dscr[:], dscr[:])
        e.tensor_tensor(nn[:], nn[:], invc[:], ALU.mult)
        e.tensor_copy(dscr[:], dscr[:])
        e.tensor_scalar(wAll[:], nn[:], wselc[:, 1:2], wselc[:, 0:1], ALU.mult, ALU.add)
        return e.tensor_copy(fscr[:, 0:64], wAll[:])
    tWfin = q.on("vector", vwf, after=[tSg])

    tDevPrev = [None] * 2
    tDevLast = None
    tOst = [None] * 2
    tGU = None
    tHt = None
    for tc in range(16):
        tXc = q.on("sync", lambda e, sem, tc=tc: xld(e, sem, tc),
                   after=[tGU, tWfin if tc == 0 else None], ndma=16)

        tHtPar = [None, None]
        for it in range(8):
            pg = psG if it % 2 == 0 else psR

            def gmm(e, it=it, pg=pg):
                for i in range(16):
                    ins = e.matmul(pg[:, 0:512],
                                   wgb[:, i * 1024 + it * 128:i * 1024 + (it + 1) * 128],
                                   xcb[:, i * 512:(i + 1) * 512],
                                   start=(i == 0), stop=(i == 15))
                return ins
            tGm = q.on("tensor", gmm, after=[tXc, tW, tHtPar[it % 2]])

            def umm(e, it=it):
                for i in range(16):
                    ins = e.matmul(psU[:, 0:512],
                                   wub[:, i * 1024 + it * 128:i * 1024 + (it + 1) * 128],
                                   xcb[:, i * 512:(i + 1) * 512],
                                   start=(i == 0), stop=(i == 15))
                return ins
            tUm = q.on("tensor", umm, after=[tHt])
            tGU = tUm

            def ssig(e, pg=pg):
                e.activation(sgf[:], pg[:, 0:512], AF.Sigmoid)
                return e.copy(fscr[:, 0:1], sgf[:, 511:512])
            tSgs = q.on("scalar", ssig, after=[tGm])

            def vht(e, it=it, pg=pg):
                e.tensor_tensor(t1f[:], sgf[:], pg[:, 0:512], ALU.mult)
                e.scalar_tensor_tensor(hT[:, it * 512:(it + 1) * 512], t1f[:], 0.125,
                                       psU[:, 0:512], ALU.mult, ALU.mult)
                return e.tensor_copy(fscr[:, 0:1], hT[:, (it + 1) * 512 - 1:(it + 1) * 512])
            tHt = q.on("vector", vht, after=[tSgs, tUm])
            tHtPar[it % 2] = tHt

        for u in range(4):
            slot = u % 2
            tt = tc * 4 + u

            def dmm(e, u=u):
                for n in range(4):
                    for i in range(8):
                        ins = e.matmul(psD[:, n * 512:(n + 1) * 512],
                                       hT[:, i * 512 + u * 128:i * 512 + (u + 1) * 128],
                                       wdb[:, i * H + n * 512:i * H + (n + 1) * 512],
                                       start=(i == 0), stop=(i == 7))
                return ins
            tDm = q.on("tensor", dmm, after=[tHt, tDevLast])

            def dev(e, tt=tt, slot=slot):
                e.tensor_scalar(obuf[:, slot * H:(slot + 1) * H], psD[:], wAll[:, tt:tt + 1],
                                None, ALU.mult)
                return e.tensor_copy(fscr[:, 0:1], obuf[:, slot * H + H - 1:slot * H + H])
            tDevPrev[slot] = q.on("vector", dev, after=[tDm, tWfin, tOst[slot]])
            tDevLast = tDevPrev[slot]

            def ost(e, sem, tc=tc, u=u, slot=slot):
                e.dma_start(out=dr["part2"][tc * 512 + u * 128:tc * 512 + (u + 1) * 128, :],
                            in_=obuf[:, slot * H:(slot + 1) * H]).then_inc(sem, 16)
            tOst[slot] = q.on("sync", ost, after=[tDevPrev[slot]], ndma=1)

    tRS = q.cc(lambda e: e.collective_compute(
        "ReduceScatter", ALU.add, replica_groups=G["rg8"],
        ins=[dr["part2"].opt()], outs=[dr["rs2"].opt()]),
        after=[tOst[0], tOst[1]])
    return tRS



def emit_final(nc, q, pool, dr, G):
    pool.reset()
    sb = pool.take
    pt = sb("pt", [128, 2 * H], BF16)
    xres = sb("xres", [128, 2 * H], F32)
    xres2 = sb("xres2", [128, 2 * H], F32)
    acc = sb("acc", [128, H], F32)
    xo = sb("xo", [128, H], F32)
    xc = sb("xc", [128, H], F32)
    scrap = sb("scrap", [128, H], F32)
    yf = sb("yf_s", [128, 2 * H], F32)
    cols = sb("cols", [128, 8], F32)
    fscr = sb("fscr", [128, 8], F32)

    tSum = [None] * 8
    tStore = [None] * 8
    for jt in range(8):
        slot = jt % 2

        def ld(e, sem, jt=jt, slot=slot):
            e.dma_start(out=pt[:, slot * H:(slot + 1) * H],
                        in_=dr["rs2"][128 * jt:128 * (jt + 1), :]).then_inc(sem, 16)
            e.dma_start(out=xres[:, slot * H:(slot + 1) * H],
                        in_=dr["hf"][128 * jt:128 * (jt + 1), :]).then_inc(sem, 16)
            e.dma_start(out=xres2[:, slot * H:(slot + 1) * H],
                        in_=dr["x3f"][128 * jt:128 * (jt + 1), :]).then_inc(sem, 16)
        tL = q.on("sync", ld, after=[tSum[jt - 2] if jt >= 2 else None,
                                     dr["tRS2"] if jt == 0 else None], ndma=3)

        def vsum(e, slot=slot):
            e.scalar_tensor_tensor(acc[:], pt[:, slot * H:(slot + 1) * H], 1.0,
                                   xres2[:, slot * H:(slot + 1) * H], ALU.mult, ALU.add)
            e.scalar_tensor_tensor(xo[:], acc[:], 1.0, xres[:, slot * H:(slot + 1) * H],
                                   ALU.mult, ALU.add, accum_out=cols[:, 0:1])
            return e.tensor_copy(fscr[:, 1:2], cols[:, 0:1])
        tSum[jt] = q.on("vector", vsum, after=[tL])

        def apply(e, jt=jt, slot=slot):
            e.activation(yf[:, slot * H:(slot + 1) * H], xc[:], AF.Copy, scale=cols[:, 5:6])
            return e.copy(fscr[:, 0:1], yf[:, slot * H + H - 1:slot * H + H])
        tAPP = ln_zigzag(q, tSum[jt], xo, xc, scrap, cols, fscr, 0, H, 1e-5, apply,
                         extra_after=[tStore[jt - 2] if jt >= 2 else None])

        def st(e, sem, jt=jt, slot=slot):
            e.dma_start(out=dr["out"][128 * jt:128 * (jt + 1), :],
                        in_=yf[:, slot * H:(slot + 1) * H]).then_inc(sem, 16)
        tStore[jt] = q.on("sync", st, after=[tAPP], ndma=1)

    return [tStore[6], tStore[7]]



def build_fused(dbg=()):
    nc = bass.Bass("TRN2", target_bir_lowering=False, debug=False)
    dr = {}

    def din(name, shape, dt):
        dr[name] = nc.dram_tensor(name, shape, dt, kind="ExternalInput").ap()

    def dint(name, shape, dt):
        kind = "ExternalOutput" if name in dbg else "Internal"
        kw = {"addr_space": "Shared"} if (name.startswith("agout") and name not in dbg) else {}
        dr[name] = nc.dram_tensor(name, shape, dt, kind=kind, **kw).ap()

    din("a1xT", [H, S], F32)
    din("a1wq", [H, 1024], F32)
    din("a1wk", [H, 1024], F32)
    din("a1wv", [H, 1024], F32)
    din("l0cos", [128, 4 * S], F32)
    din("l0sin", [128, 4 * S], F32)
    din("gate", [1, 8], F32)
    din("identI", [128, 128], BF16)
    din("maskI", [128, 128], F32)
    din("idxBI", [128, 16], U32)
    din("idxPairI", [128, 32], U32)
    din("bwoT", [H, H], F32)
    din("bx", [1024, H], F32)
    din("tg", [1024, 2048], U8)
    din("tu", [1024, 2048], U8)
    din("td", [1024, 2048], U8)
    din("gsT", [64, 2048], F32)
    din("usT", [64, 2048], F32)
    din("dsT", [64, 2048], F32)
    din("e1wq", [H, 1024], F32)
    din("e1wk", [H, 256], F32)
    din("e1wv", [H, 256], F32)
    din("qg", [1, 8], F32)
    din("l1cos", [128, 16 * 64], F32)
    din("l1sin", [128, 16 * 64], F32)
    din("fwoT", [H, H], F32)
    din("ln1w", [1, H], F32)
    din("kvd", [H, 512], F32)
    din("qd", [H, 512], F32)
    din("ku", [512, 512], F32)
    din("qu", [512, 512], F32)
    din("vu", [512, 1024], F32)
    din("rk", [H, 512], F32)
    din("rq", [512, 512], F32)
    din("mcos", [128, 16 * 32], F32)
    din("msin", [128, 16 * 32], F32)
    din("hwoT", [H, H], F32)
    din("ln2w", [1, H], F32)
    din("mwg", [H, 1024], F32)
    din("mwu", [H, 1024], F32)
    din("mwd", [1024, H], F32)
    din("mrw", [H, 7], F32)
    din("mrb", [1, 7], F32)
    din("mwsel", [1, 2], F32)

    dint("qT0", [1024, S], BF16)
    dint("kT0", [1024, S], BF16)
    dint("vt0", [S, 1024], BF16)
    dint("agin_comb", [2048, 1024], BF16)
    dint("agout_comb", [16384, 1024], BF16)
    dint("x1f", [1024, H], F32)
    dint("agin_x1", [2048, 1024], BF16)
    dint("agout_x1", [16384, 1024], BF16)
    dint("inter", [2048, TT_], BF16)
    dint("part1", [TT_, H], BF16)
    dint("rs1", [1024, H], BF16)
    dint("x2f", [1024, H], F32)
    dint("agin_x2", [2048, 1024], BF16)
    dint("agout_x2", [16384, 1024], BF16)
    dint("qT1", [1024, S], BF16)
    dint("kT1", [256, S], BF16)
    dint("vt1", [S, 256], BF16)
    dint("agin_y1", [2048, 1024], BF16)
    dint("agout_y1", [16384, 1024], BF16)
    dint("x3f", [1024, H], F32)
    dint("agin_n1", [2048, 1024], BF16)
    dint("agout_xn1", [16384, 1024], BF16)
    dint("qT2", [1024, S], BF16)
    dint("kT2", [1024, S], BF16)
    dint("vt2", [S, 1024], BF16)
    dint("agin_y2", [2048, 1024], BF16)
    dint("agout_y2", [16384, 1024], BF16)
    dint("hf", [1024, H], F32)
    dint("agin_n2", [2048, 1024], BF16)
    dint("agout_xn2", [16384, 1024], BF16)
    dint("part2", [TT_, H], BF16)
    dint("rs2", [1024, H], BF16)
    dr["out"] = nc.dram_tensor("out", [1024, H], F32, kind="ExternalOutput").ap()

    with ExitStack() as es:
        idt = es.enter_context(nc.sbuf_tensor("idt_c", [128, 128], BF16))
        maskt = es.enter_context(nc.sbuf_tensor("maskt_c", [128, 128], F32))
        idxB = es.enter_context(nc.sbuf_tensor("idxB_c", [128, 16], U32))
        idxPair = es.enter_context(nc.sbuf_tensor("idxPair_c", [128, 32], U32))
        psBIG = es.enter_context(nc.psum_tensor("psBIG", [128, 2048], F32))
        ps2 = es.enter_context(nc.psum_tensor("ps2", [128, 1024], F32))
        ps3 = es.enter_context(nc.psum_tensor("ps3", [128, 512], F32))
        psTr = es.enter_context(nc.psum_tensor("psTr", [128, 1024], BF16))
        pool = Pool(nc, es, 211456)
        block = es.enter_context(nc.Block())
        q = Seq(es, nc)

        G = {
            "idt": idt[:], "maskt": maskt[:], "idxB": idxB[:], "idxPair": idxPair[:],
            "psBIG": psBIG[:], "ps2": ps2[:], "ps3": ps3[:], "psTr": psTr[:],
            "rg8": [list(range(NCORES))],
        }

        def ldconst(e, sem):
            e.dma_start(out=idt[:], in_=dr["identI"]).then_inc(sem, 16)
            e.dma_start(out=maskt[:], in_=dr["maskI"]).then_inc(sem, 16)
            e.dma_start(out=idxB[:], in_=dr["idxBI"]).then_inc(sem, 16)
            e.dma_start(out=idxPair[:], in_=dr["idxPairI"]).then_inc(sem, 16)
        q.on("sync", ldconst, ndma=4)
        q.new_stage()

        emit_l0_qkv(nc, q, pool, dr, G)
        q.new_stage()
        dr["tAGcomb"] = emit_l0_attn(nc, q, pool, dr, G)
        q.new_stage()
        dr["tAGx1"] = emit_oproj_ln(nc, q, pool, dr, G)
        q.new_stage()
        dr["tRS1"] = emit_l0_ffn(nc, q, pool, dr, G)
        q.new_stage()
        dr["tAGx2"] = emit_postffn_ln(nc, q, pool, dr, G)
        q.new_stage()
        emit_l1_qkv(nc, q, pool, dr, G)
        q.new_stage()
        tAGy1 = emit_causal_attn(nc, q, pool, dr, G, 256, 4,
                                 dr["qT1"], dr["kT1"], dr["vt1"],
                                 dr["agin_y1"], dr["agout_y1"])
        q.new_stage()
        dr["tAGxn1"] = emit_oproj_block(nc, q, pool, dr, G, True,
                                        dr["agout_y1"], tAGy1, dr["fwoT"], dr["x2f"],
                                        dr["ln1w"], dr["x3f"], dr["agin_n1"], dr["agout_xn1"])
        q.new_stage()
        emit_mla_proj(nc, q, pool, dr, G)
        q.new_stage()
        tAGy2 = emit_causal_attn(nc, q, pool, dr, G, 1024, 1,
                                 dr["qT2"], dr["kT2"], dr["vt2"],
                                 dr["agin_y2"], dr["agout_y2"])
        q.new_stage()
        tAGxn2 = emit_oproj_block(nc, q, pool, dr, G, False,
                                  dr["agout_y2"], tAGy2, dr["hwoT"], dr["x3f"],
                                  dr["ln2w"], dr["hf"], dr["agin_n2"], dr["agout_xn2"])
        dr["tAGxn2"] = tAGxn2
        q.new_stage()
        dr["tRS2"] = emit_moe(nc, q, pool, dr, G)
        q.new_stage()
        stores = emit_final(nc, q, pool, dr, G)

        q.emit(block)
        q.finish(block, stores)
    return nc



def _run(nc, in_maps):
    return run_bass_kernel_spmd(nc, in_maps, core_ids=list(range(NCORES)))


def _stage_inputs(inputs):
    x = np.asarray(inputs["L0_attn_hidden_states"], np.float32)
    wqT0 = np.ascontiguousarray(np.asarray(inputs["L0_attn_Wq"], np.float32).T)
    wkT0 = np.ascontiguousarray(np.asarray(inputs["L0_attn_Wk"], np.float32).T)
    wvT0 = np.ascontiguousarray(np.asarray(inputs["L0_attn_Wv"], np.float32).T)
    woT0 = np.ascontiguousarray(np.asarray(inputs["L0_attn_Wo"], np.float32).T)
    gateq = np.asarray(inputs["L0_attn_gate"], np.float32).reshape(1, 16)
    cos0, sin0 = _l0_costab()
    cos0T = np.ascontiguousarray(cos0.T)
    sin0T = np.ascontiguousarray(sin0.T)
    pn = _pi_n()
    gp = np.asarray(inputs["L0_ffn_gate_W_packed"])
    up = np.asarray(inputs["L0_ffn_up_W_packed"])
    dp = np.asarray(inputs["L0_ffn_down_W_packed"])
    gs = np.asarray(inputs["L0_ffn_gate_scales"], np.float32)
    us = np.asarray(inputs["L0_ffn_up_scales"], np.float32)
    ds = np.asarray(inputs["L0_ffn_down_scales"], np.float32)
    bg = np.ascontiguousarray(gp.view(np.uint8).reshape(16384, 1024, 4)[:, :, 0][pn].T)
    bu = np.ascontiguousarray(up.view(np.uint8).reshape(16384, 1024, 4)[:, :, 0][pn].T)
    bd = np.ascontiguousarray(dp.view(np.uint8).reshape(2048, 8192, 4)[:, :, 0].T)
    gsT = np.ascontiguousarray(gs[pn].T)
    usT = np.ascontiguousarray(us[pn].T)
    dsT = np.ascontiguousarray(ds.T)
    wqT1 = np.ascontiguousarray(np.asarray(inputs["L1_attn_Wq"], np.float32).T)
    wkT1 = np.ascontiguousarray(np.asarray(inputs["L1_attn_Wk"], np.float32).T)
    wvT1 = np.ascontiguousarray(np.asarray(inputs["L1_attn_Wv"], np.float32).T)
    woT1 = np.ascontiguousarray(np.asarray(inputs["L1_attn_Wo"], np.float32).T)
    qg = np.asarray(inputs["L1_attn_qg"], np.float32).reshape(16)
    cos1, sin1 = _l1_costab()
    cos1t, sin1t = _tiles128(cos1), _tiles128(sin1)
    kvd = np.asarray(inputs["L1_ffn_kv_d_w"], np.float32)
    qd = np.asarray(inputs["L1_ffn_q_d_w"], np.float32)
    ku = np.asarray(inputs["L1_ffn_k_u_w"], np.float32)
    qu = np.asarray(inputs["L1_ffn_q_u_w"], np.float32)
    vu = np.asarray(inputs["L1_ffn_v_u_w"], np.float32)
    rk = np.asarray(inputs["L1_ffn_rope_k_w"], np.float32)
    rq = np.asarray(inputs["L1_ffn_rope_q_w"], np.float32)
    cosm, sinm = _mla_costab()
    cosmt, sinmt = _tiles128(cosm), _tiles128(sinm)
    ow = np.ascontiguousarray(np.asarray(inputs["L1_ffn_o_w"], np.float32))
    ln1w = np.ascontiguousarray(np.asarray(inputs["L1_ffn_ln1_w"], np.float32).reshape(1, H))
    ln2w = np.ascontiguousarray(np.asarray(inputs["L1_ffn_ln2_w"], np.float32).reshape(1, H))
    shg = np.asarray(inputs["L1_ffn_sh_gate"], np.float32)
    shu = np.asarray(inputs["L1_ffn_sh_up"], np.float32)
    shd = np.asarray(inputs["L1_ffn_sh_down"], np.float32)
    rg_ = np.asarray(inputs["L1_ffn_r_gate"], np.float32)
    ru_ = np.asarray(inputs["L1_ffn_r_up"], np.float32)
    rd_ = np.asarray(inputs["L1_ffn_r_down"], np.float32)
    rw = np.asarray(inputs["L1_ffn_router_w"], np.float32)
    rb = np.asarray(inputs["L1_ffn_router_b"], np.float32)
    bwoT = np.ascontiguousarray(woT0[np.concatenate([_l0_colmap(0), _l0_colmap(1)])])

    in_maps = []
    ar = np.arange(128, dtype=np.uint32)
    for c in range(NCORES):
        b, p = c // 2, c % 2
        cm = _l0_colmap(p)
        gsel = gateq[:, [4 * p + i for i in range(4)] + [8 + 4 * p + i for i in range(4)]]
        idxB = np.zeros((128, 16), np.uint32)
        for pk in range(2):
            for ci in range(8):
                idxB[:, pk * 8 + ci] = 2048 * (2 * b + pk) + 1024 * p + 128 * ci + ar
        idxPair = np.zeros((128, 32), np.uint32)
        for i in range(16):
            for blk in range(2):
                idxPair[:, 2 * i + blk] = 2048 * (2 * b + blk) + 128 * i + ar
        if c == 0:
            mg, mu, md = shg, shu, shd
            perm = list(range(7))
            wsel = np.array([[1.0, 0.0]], np.float32)
        else:
            mg, mu, md = rg_[c - 1], ru_[c - 1], rd_[c - 1]
            perm = [c - 1] + [i for i in range(7) if i != c - 1]
            wsel = np.array([[0.0, 0.5]], np.float32)
        cs = slice(512 * p, 512 * (p + 1))
        in_maps.append({
            "a1xT": np.ascontiguousarray(x[b].T),
            "a1wq": np.ascontiguousarray(wqT0[:, cm]),
            "a1wk": np.ascontiguousarray(wkT0[:, cm]),
            "a1wv": np.ascontiguousarray(wvT0[:, cm]),
            "l0cos": _tiles128(cos0T[512 * p:512 * (p + 1)]),
            "l0sin": _tiles128(sin0T[512 * p:512 * (p + 1)]),
            "gate": np.ascontiguousarray(gsel),
            "identI": IDENT,
            "maskI": CAUSAL_MASK,
            "idxBI": idxB,
            "idxPairI": idxPair,
            "bwoT": bwoT,
            "bx": np.ascontiguousarray(x[b][1024 * p:1024 * (p + 1)]),
            "tg": np.ascontiguousarray(bg[:, 2048 * c:2048 * (c + 1)]),
            "tu": np.ascontiguousarray(bu[:, 2048 * c:2048 * (c + 1)]),
            "td": np.ascontiguousarray(bd[1024 * c:1024 * (c + 1), :]),
            "gsT": np.ascontiguousarray(gsT[:, 2048 * c:2048 * (c + 1)]),
            "usT": np.ascontiguousarray(usT[:, 2048 * c:2048 * (c + 1)]),
            "dsT": np.ascontiguousarray(dsT[64 * c:64 * (c + 1), :]),
            "e1wq": np.ascontiguousarray(wqT1[:, 1024 * p:1024 * (p + 1)]),
            "e1wk": np.ascontiguousarray(wkT1[:, 256 * p:256 * (p + 1)]),
            "e1wv": np.ascontiguousarray(wvT1[:, 256 * p:256 * (p + 1)]),
            "qg": np.ascontiguousarray(qg[8 * p:8 * (p + 1)].reshape(1, 8)),
            "l1cos": cos1t, "l1sin": sin1t,
            "fwoT": woT1,
            "ln1w": ln1w,
            "kvd": kvd, "qd": qd,
            "ku": np.ascontiguousarray(ku[:, cs]),
            "qu": np.ascontiguousarray(qu[:, cs]),
            "vu": np.ascontiguousarray(vu[:, 1024 * p:1024 * (p + 1)]),
            "rk": np.ascontiguousarray(rk[:, cs]),
            "rq": np.ascontiguousarray(rq[:, cs]),
            "mcos": cosmt, "msin": sinmt,
            "hwoT": ow,
            "ln2w": ln2w,
            "mwg": np.ascontiguousarray(mg),
            "mwu": np.ascontiguousarray(mu),
            "mwd": np.ascontiguousarray(md),
            "mrw": np.ascontiguousarray(rw[:, perm]),
            "mrb": np.ascontiguousarray(rb[perm].reshape(1, 7)),
            "mwsel": wsel,
        })
    return in_maps


def kernel(**inputs):
    nc = build_fused()
    in_maps = _stage_inputs(inputs)
    res = _run(nc, in_maps)
    out = np.concatenate([np.asarray(r["out"]) for r in res.results], axis=0)
    return out.reshape(B, S, H).astype(np.float32)
